# version 02
# baseline (speedup 1.0000x reference)
import contextlib

import numpy as np
import ml_dtypes

import concourse.bass as bass
import concourse.mybir as mybir
from concourse.bass_utils import run_bass_kernel_spmd

F32 = mybir.dt.float32
BF16 = mybir.dt.bfloat16
AF = mybir.ActivationFunctionType
ALU = mybir.AluOpType
AX = mybir.AxisListType

NCORES = 8
D = 2048
NCH = 16
T = 2048
NTG = 4
TG = 512
S = 8192
NH = 16
DK = 128
RD2 = DK ** 0.5


class Buf:
    __slots__ = ("last_w", "readers", "name")
    def __init__(self, name=""):
        self.last_w = None
        self.readers = []
        self.name = name


class Prog:
    ENGS = ("t", "v", "c", "g", "s")
    DQS = ("s", "g", "c", "v")

    NSLOT = 6

    def __init__(self):
        self._q = {e: [] for e in self.ENGS}
        self.cnt = {e: 0 for e in self.ENGS}
        self.dcnt = {q: 0 for q in self.DQS}
        self.slotcnt = {q: [0] * self.NSLOT for q in self.DQS}
        self.watermark = {}
        self._pending = {e: [] for e in self.ENGS}

    PARANOID = False

    def _paranoid_need(self, need):
        for e in self.ENGS:
            if self.cnt[e]:
                need["P" + e] = max(need.get("P" + e, 0), self.cnt[e])
        for q in self.DQS:
            for slot in range(self.NSLOT):
                if self.slotcnt[q][slot]:
                    sname = f"D{q}{slot}"
                    need[sname] = max(need.get(sname, 0), 16 * self.slotcnt[q][slot])

    def _need(self, need, tok):
        if tok is None:
            return
        sem, val = tok
        assert val is not None, "dependency on an un-inc'd instruction"
        if need.get(sem, 0) < val:
            need[sem] = val

    def _emit_waits(self, eng, need):
        for sem, val in sorted(need.items()):
            if self.watermark.get((eng, sem), 0) >= val:
                continue
            self._q[eng].append(("wait", sem, val))
            self.watermark[(eng, sem)] = val

    def op(self, eng, fn, reads=(), writes=(), inc=True):
        need = {}
        for b in reads:
            self._need(need, b.last_w)
        for b in writes:
            self._need(need, b.last_w)
            for r in b.readers:
                self._need(need, r)
        if self.PARANOID:
            self._paranoid_need(need)
        self._emit_waits(eng, need)
        self._q[eng].append(("op", fn, inc))
        if inc:
            self.cnt[eng] += 1
            tok = ("P" + eng, self.cnt[eng])
            for b, iswrite in self._pending[eng]:
                if iswrite:
                    b.last_w = tok
                    b.readers = []
                else:
                    b.readers.append(tok)
            self._pending[eng].clear()
            for b in writes:
                b.last_w = tok
                b.readers = []
            for b in reads:
                b.readers.append(tok)
            return tok
        for b in writes:
            self._pending[eng].append((b, True))
        for b in reads:
            self._pending[eng].append((b, False))
        return None

    def dma(self, q, out_ap, in_ap, reads=(), writes=(), **kw):
        need = {}
        for b in reads:
            self._need(need, b.last_w)
        for b in writes:
            self._need(need, b.last_w)
            for r in b.readers:
                self._need(need, r)
        if self.PARANOID:
            self._paranoid_need(need)
        slot = self.dcnt[q] % self.NSLOT
        sname = f"D{q}{slot}"
        if self.slotcnt[q][slot]:
            need[sname] = max(need.get(sname, 0), 16 * self.slotcnt[q][slot])
        self._emit_waits(q, need)
        self._q[q].append(("dma", out_ap, in_ap, sname, kw))
        self.dcnt[q] += 1
        self.slotcnt[q][slot] += 1
        tok = (sname, 16 * self.slotcnt[q][slot])
        for b in writes:
            b.last_w = tok
            b.readers = []
        for b in reads:
            b.readers.append(tok)
        return tok

    def finish(self):
        allneed = {}
        for q in self.DQS:
            qneed = {}
            for slot in range(self.NSLOT):
                if self.slotcnt[q][slot]:
                    qneed[f"D{q}{slot}"] = 16 * self.slotcnt[q][slot]
            self._emit_waits(q, qneed)
            allneed.update(qneed)
        for e in self.ENGS:
            if self.cnt[e]:
                allneed["P" + e] = self.cnt[e]
        self._emit_waits("s", allneed)

    def materialize(self, nc, block, stack):
        sems = {}
        for e in self.ENGS:
            sems["P" + e] = stack.enter_context(nc.semaphore("semP" + e))
        for q in self.DQS:
            for slot in range(self.NSLOT):
                if self.slotcnt[q][slot]:
                    sems[f"D{q}{slot}"] = stack.enter_context(
                        nc.semaphore(f"semD{q}{slot}"))
        attr = {"t": "tensor", "v": "vector", "c": "scalar", "g": "gpsimd", "s": "sync"}
        for e in self.ENGS:
            items = self._q[e]
            psem = sems["P" + e]

            def body(eng, items=items, psem=psem):
                for it in items:
                    if it[0] == "wait":
                        eng.wait_ge(sems[it[1]], it[2])
                    elif it[0] == "op":
                        inst = it[1](eng)
                        if it[2]:
                            inst.then_inc(psem, 1)
                    else:
                        eng.dma_start(out=it[1], in_=it[2], **it[4]).then_inc(sems[it[3]], 16)
            getattr(block, attr[e])(body)


class Ctx:
    def __init__(self, nc, stack, p):
        self.nc = nc
        self.stack = stack
        self.p = p
        self.n_sb = 0
        self.pt = [stack.enter_context(nc.psum_tensor(f"pp{i}", [128, 512], F32))
                   for i in range(7)]
        self.pb = [Buf(f"psum{i}") for i in range(7)]
        self.ptb = stack.enter_context(nc.psum_tensor("ppb", [128, 1024], BF16))
        self.pbb = Buf("psumBF")

    def sb(self, shape, dtype, name=None):
        self.n_sb += 1
        t = self.stack.enter_context(
            self.nc.sbuf_tensor(name or f"sb{self.n_sb}", shape, dtype))
        return t

    def sbuf(self, shape, dtype, name=None):
        t = self.sb(shape, dtype, name)
        return t, Buf(name or f"b{self.n_sb}")


def make_consts(ctx, p, dram):
    c = {}
    t, b = ctx.sbuf([128, 1], F32, "c_ones_col")
    p.dma("s", t[:, :], dram["ones_col"], writes=[b])
    c["ones_col"] = (t, b)
    t, b = ctx.sbuf([128, 1], BF16, "c_ones_col_bf")
    p.dma("g", t[:, :], dram["ones_col"], writes=[b])
    c["ones_col_bf"] = (t, b)
    t, b = ctx.sbuf([1, 128], F32, "c_ones_k1")
    p.dma("s", t[:, :], dram["ones_k1"], writes=[b])
    c["ones_k1"] = (t, b)
    t, b = ctx.sbuf([1, 128], BF16, "c_ones_k1_bf")
    p.dma("g", t[:, :], dram["ones_k1"], writes=[b])
    c["ones_k1_bf"] = (t, b)
    t, b = ctx.sbuf([128, 128], BF16, "c_ident_bf")
    p.dma("g", t[:, :], dram["ident"], writes=[b])
    c["ident_bf"] = (t, b)
    t, b = ctx.sbuf([128, 128], F32, "c_ident_f32")
    p.dma("s", t[:, :], dram["ident"], writes=[b])
    c["ident_f32"] = (t, b)
    return c


class LNBlock:
    def __init__(self, ctx):
        self.ctx = ctx
        self.xt = [ctx.sbuf([128, TG], F32, f"ln_x{i}") for i in range(4)]
        self.sq = [ctx.sbuf([128, TG], BF16, f"ln_sq{i}") for i in range(4)]
        self.rows = ctx.sb([1, 8 * TG], F32, "ln_rows")
        self.rowb = [Buf(f"ln_row{i}") for i in range(8)]
        self.t1 = [ctx.sbuf([128, TG], F32, f"ln_t1{i}") for i in range(2)]

    def emit(self, x_dram, g, xn_sb, xnbufs, consts, ps_sum, ps_sq, ps_a, ps_b,
             eps=1e-5, also_keep_x=None, xn_off=0, xn_stride=TG):
        ctx, p = self.ctx, self.ctx.p
        sum_t, sum_b = ps_sum
        sq_t, sq_b = ps_sq
        a_t, a_b = ps_a
        b_t, b_b = ps_b
        ones_col, ones_col_b = consts["ones_col"]
        ones_col_bf, ones_col_bf_b = consts["ones_col_bf"]
        ones_k1, ones_k1_b = consts["ones_k1"]
        M, E2, SD, R, BR = 0, 1, 2, 3, 4

        def xsrc(c):
            return x_dram[c * 128:(c + 1) * 128, g * TG:(g + 1) * TG]

        keep_sb, keep_bufs = (None, None) if also_keep_x is None else also_keep_x
        for c in range(NCH):
            if keep_sb is not None:
                xt, xb = keep_sb, keep_bufs[c]
                xap = keep_sb[:, c * TG:(c + 1) * TG]
                p.dma("s", xap, xsrc(c), writes=[xb])
            else:
                xt, xb = self.xt[c % 4]
                xap = xt[:, :]
                p.dma("s", xap, xsrc(c), writes=[xb])
            sqt, sqb = self.sq[c % 4]
            p.op("c", lambda e, xap=xap, sqt=sqt: e.activation(sqt[:, :], xap, AF.Square),
                 reads=[xb], writes=[sqb])
            p.op("t", lambda e, xap=xap, c=c: e.matmul(
                sum_t[0:1, :], ones_col[:, :], xap, start=(c == 0), stop=(c == NCH - 1)),
                reads=[xb, ones_col_b], writes=[sum_b])
            p.op("t", lambda e, sqt=sqt, c=c: e.matmul(
                sq_t[0:1, :], ones_col_bf[:, :], sqt[:, :], start=(c == 0), stop=(c == NCH - 1),
                skip_group_check=True),
                reads=[sqb, ones_col_bf_b], writes=[sq_b])
        rows, rowb = self.rows, self.rowb
        p.op("v", lambda e: e.tensor_scalar(rows[0:1, M*TG:(M+1)*TG], sum_t[0:1, :], 1.0 / D, None, ALU.mult),
             reads=[sum_b], writes=[rowb[M]])
        p.op("v", lambda e: e.tensor_scalar(rows[0:1, E2*TG:(E2+1)*TG], sq_t[0:1, :], 1.0 / D, None, ALU.mult),
             reads=[sq_b], writes=[rowb[E2]])
        p.op("v", lambda e: e.scalar_tensor_tensor(
            rows[0:1, SD*TG:(SD+1)*TG], rows[0:1, M*TG:(M+1)*TG], -1.0, rows[0:1, M*TG:(M+1)*TG], ALU.mult, ALU.mult),
            reads=[rowb[M]], writes=[rowb[SD]])
        p.op("v", lambda e: e.scalar_tensor_tensor(
            rows[0:1, SD*TG:(SD+1)*TG], rows[0:1, SD*TG:(SD+1)*TG], float(eps),
            rows[0:1, E2*TG:(E2+1)*TG], ALU.add, ALU.add),
            reads=[rowb[SD], rowb[E2]], writes=[rowb[SD]])
        p.op("c", lambda e: e.activation(rows[0:1, SD*TG:(SD+1)*TG], rows[0:1, SD*TG:(SD+1)*TG], AF.Sqrt),
             reads=[rowb[SD]], writes=[rowb[SD]])
        p.op("v", lambda e: e.reciprocal(rows[0:1, R*TG:(R+1)*TG], rows[0:1, SD*TG:(SD+1)*TG]),
             reads=[rowb[SD]], writes=[rowb[R]])
        p.op("v", lambda e: e.scalar_tensor_tensor(
            rows[0:1, BR*TG:(BR+1)*TG], rows[0:1, M*TG:(M+1)*TG], -1.0, rows[0:1, R*TG:(R+1)*TG], ALU.mult, ALU.mult),
            reads=[rowb[M], rowb[R]], writes=[rowb[BR]])
        p.op("t", lambda e: e.matmul(a_t[:, :], ones_k1[:, :], rows[0:1, R*TG:(R+1)*TG],
                                     start=True, stop=True),
             reads=[rowb[R], ones_k1_b], writes=[a_b])
        p.op("t", lambda e: e.matmul(b_t[:, :], ones_k1[:, :], rows[0:1, BR*TG:(BR+1)*TG],
                                     start=True, stop=True),
             reads=[rowb[BR], ones_k1_b], writes=[b_b])
        for c in range(NCH):
            if keep_sb is not None:
                xap = keep_sb[:, c * TG:(c + 1) * TG]
                xb = keep_bufs[c]
            else:
                xt, xb = self.xt[c % 4]
                xap = xt[:, :]
                p.dma("s", xap, xsrc(c), writes=[xb])
            t1t, t1b = self.t1[c % 2]
            p.op("v", lambda e, xap=xap, t1t=t1t: e.tensor_mul(t1t[:, :], xap, a_t[:, :]),
                 reads=[xb, a_b], writes=[t1b])
            p.op("v", lambda e, t1t=t1t, c=c: e.tensor_add(
                xn_sb[:, c * xn_stride + xn_off: c * xn_stride + xn_off + TG],
                t1t[:, :], b_t[:, :]),
                reads=[t1b, b_b], writes=[xnbufs[c]])



def emit_l0a_group(ctx, consts, ln, g, x_dram, y_dram, wq_dram, wg_dram, kT_dram,
                   v_dram, woutT_dram):
    p = ctx.p
    xn_sb, xnb = ctx.l0a_xn
    xk_sb, xkb = ctx.l0a_xk
    qh, qhb = ctx.l0a_qh
    gh, ghb = ctx.l0a_gh
    wqt, wqtb = ctx.l0a_wqt
    wgt, wgtb = ctx.l0a_wgt
    kt = ctx.l0a_kt
    ktb = ctx.l0a_ktb
    vt = ctx.l0a_vt
    vtb = ctx.l0a_vtb
    es = ctx.l0a_es
    og_sb, ogb = ctx.l0a_og
    wot, wotb = ctx.l0a_wot

    ln.emit(x_dram, g, xn_sb, xnb, consts,
            (ctx.pt[0], ctx.pb[0]), (ctx.pt[1], ctx.pb[1]),
            (ctx.pt[2], ctx.pb[2]), (ctx.pt[3], ctx.pb[3]),
            also_keep_x=(xk_sb, xkb))

    e_ps = [(ctx.pt[0], ctx.pb[0]), (ctx.pt[1], ctx.pb[1])]
    q_ps = (ctx.pt[4], ctx.pb[4])
    g_ps = (ctx.pt[5], ctx.pb[5])
    o_ps = (ctx.pt[6], ctx.pb[6])

    for h in range(NH):
        p.dma("g", wqt[:, :], wq_dram[h * 128:(h + 1) * 128, :], writes=[wqtb])
        p.dma("g", wgt[:, :], wg_dram[h * 128:(h + 1) * 128, :], writes=[wgtb])
        p.dma("g", kt[:, :], kT_dram[h * 128:(h + 1) * 128, :], writes=[ktb])
        p.dma("g", vt[:, :], v_dram[h * 128:(h + 1) * 128, :], writes=[vtb])
        qt_, qb_ = q_ps
        for kc in range(NCH):
            p.op("t", lambda e, kc=kc: e.matmul(
                qt_[:, :], wqt[:, kc * 128:(kc + 1) * 128],
                xn_sb[:, kc * TG:(kc + 1) * TG],
                start=(kc == 0), stop=(kc == NCH - 1)),
                reads=[wqtb, xnb[kc]], writes=[qb_])
        p.op("c", lambda e: e.activation(qh[:, :], qt_[:, :], AF.Copy),
             reads=[qb_], writes=[qhb])
        gt_, gb_ = g_ps
        for kc in range(NCH):
            p.op("t", lambda e, kc=kc: e.matmul(
                gt_[:, :], wgt[:, kc * 128:(kc + 1) * 128],
                xn_sb[:, kc * TG:(kc + 1) * TG],
                start=(kc == 0), stop=(kc == NCH - 1)),
                reads=[wgtb, xnb[kc]], writes=[gb_])
        p.op("c", lambda e: e.activation(gh[:, :], gt_[:, :], AF.Silu),
             reads=[gb_], writes=[ghb])
        ot_, ob_ = o_ps
        for mc in range(32):
            et_, eb_ = e_ps[mc % 2]
            p.op("t", lambda e, mc=mc, et_=et_: e.matmul(
                et_[:, :], kt[:, mc * 128:(mc + 1) * 128], qh[:, :],
                start=True, stop=True),
                reads=[ktb, qhb], writes=[eb_])
            est, esb = es[mc % 3]
            p.op("c", lambda e, et_=et_, est=est: e.activation(
                est[:, :], et_[:, :], AF.Silu, scale=float(RD2)),
                reads=[eb_], writes=[esb])
            p.op("t", lambda e, mc=mc, est=est: e.matmul(
                ot_[:, :], vt[:, mc * 128:(mc + 1) * 128], est[:, :],
                start=(mc == 0), stop=(mc == 31)),
                reads=[vtb, esb], writes=[ob_])
        p.op("v", lambda e, h=h: e.scalar_tensor_tensor(
            og_sb[:, h * TG:(h + 1) * TG], ot_[:, :], float(RD2), gh[:, :],
            ALU.mult, ALU.mult),
            reads=[ob_, ghb], writes=[ogb[h]])

    for fcg in range(4):
        p.dma("g", wot[:, :], woutT_dram[fcg * 128:(fcg + 1) * 128, :], writes=[wotb])
        for fc in range(4):
            yt_, yb_ = (ctx.pt[fc], ctx.pb[fc])
            for kc in range(NCH):
                p.op("t", lambda e, kc=kc, fc=fc, yt_=yt_: e.matmul(
                    yt_[:, :], wot[:, kc * 512 + fc * 128: kc * 512 + (fc + 1) * 128],
                    og_sb[:, kc * TG:(kc + 1) * TG],
                    start=(kc == 0), stop=(kc == NCH - 1)),
                    reads=[wotb, ogb[kc]], writes=[yb_])
            c = fcg * 4 + fc
            p.op("v", lambda e, c=c, yt_=yt_: e.tensor_add(
                xk_sb[:, c * TG:(c + 1) * TG], xk_sb[:, c * TG:(c + 1) * TG], yt_[:, :]),
                reads=[xkb[c], yb_], writes=[xkb[c]])
            p.dma("s", y_dram[c * 128:(c + 1) * 128, g * TG:(g + 1) * TG],
                  xk_sb[:, c * TG:(c + 1) * TG], reads=[xkb[c]])


def build_l0a():
    nc = bass.Bass("TRN2", target_bir_lowering=False)
    xT = nc.dram_tensor("xT", [D, T], F32, kind="ExternalInput").ap()
    yT = nc.dram_tensor("yT", [D, T], F32, kind="ExternalOutput").ap()
    wqT = nc.dram_tensor("wqT", [NH * 128, D], F32, kind="ExternalInput").ap()
    wgT = nc.dram_tensor("wgT", [NH * 128, D], F32, kind="ExternalInput").ap()
    kTw = nc.dram_tensor("kTw", [D, 4096], F32, kind="ExternalInput").ap()
    vW = nc.dram_tensor("vW", [NH * 128, 32 * 128], F32, kind="ExternalInput").ap()
    woutT = nc.dram_tensor("woutT", [4 * 128, NCH * 512], F32, kind="ExternalInput").ap()
    ones_col = nc.dram_tensor("ones_col", [128, 1], F32, kind="ExternalInput").ap()
    ones_k1 = nc.dram_tensor("ones_k1", [1, 128], F32, kind="ExternalInput").ap()
    ident = nc.dram_tensor("ident", [128, 128], F32, kind="ExternalInput").ap()
    dram = {"ones_col": ones_col, "ones_k1": ones_k1, "ident": ident}

    p = Prog()
    with contextlib.ExitStack() as stack:
        ctx = Ctx(nc, stack, p)
        consts = make_consts(ctx, p, dram)
        ln = LNBlock(ctx)
        ctx.l0a_xn = (ctx.sb([128, NCH * TG], BF16, "xn"), [Buf(f"xn{c}") for c in range(NCH)])
        ctx.l0a_xk = (ctx.sb([128, NCH * TG], F32, "xk"), [Buf(f"xk{c}") for c in range(NCH)])
        ctx.l0a_qh = ctx.sbuf([128, TG], BF16, "qh")
        ctx.l0a_gh = ctx.sbuf([128, TG], BF16, "gh")
        ctx.l0a_wqt = ctx.sbuf([128, NCH * 128], BF16, "wqt")
        ctx.l0a_wgt = ctx.sbuf([128, NCH * 128], BF16, "wgt")
        kt, ktb = ctx.sbuf([128, 4096], BF16, "kt")
        ctx.l0a_kt, ctx.l0a_ktb = kt, ktb
        vt, vtb = ctx.sbuf([128, 4096], BF16, "vt")
        ctx.l0a_vt, ctx.l0a_vtb = vt, vtb
        ctx.l0a_es = [ctx.sbuf([128, TG], BF16, f"es{i}") for i in range(3)]
        ctx.l0a_og = (ctx.sb([128, NCH * TG], BF16, "og"), [Buf(f"og{c}") for c in range(NCH)])
        ctx.l0a_wot = ctx.sbuf([128, NCH * 512], BF16, "wot")

        with nc.Block() as block:
            for g in range(NTG):
                emit_l0a_group(ctx, consts, ln, g, xT, yT, wqT, wgT, kTw, vW, woutT)
            p.finish()
            p.materialize(nc, block, stack)
    return nc




def emit_gating_l0(ctx, consts, g, W, xn_sb, xnb, combT, combTb, srow, srowb,
                   gwt, gwtb, segt, segtb):
    p = ctx.p
    sc_ps, sc_psb = ctx.pt[4], ctx.pb[4]
    tr_ps, tr_psb = ctx.pt[5], ctx.pb[5]
    sg_ps, sg_psb = ctx.pt[3], ctx.pb[3]
    ident_f32, ident_f32b = consts["ident_f32"]
    sc_sb, sc_sbb = ctx.ffn_sc
    st_sb, st_sbb = ctx.ffn_st
    eq1, eq1b = ctx.ffn_eq1
    eq2, eq2b = ctx.ffn_eq2
    smk, smkb = ctx.ffn_smk
    ct, ctb = ctx.ffn_ct
    stats, statsb = ctx.ffn_stats
    for kc in range(NCH):
        p.op("t", lambda e, kc=kc: e.matmul(
            sc_ps[0:8, :], gwt[:, kc * 8:(kc + 1) * 8],
            xn_sb[:, kc * T + g * TG: kc * T + g * TG + TG],
            start=(kc == 0), stop=(kc == NCH - 1)),
            reads=[gwtb, xnb[kc][g]], writes=[sc_psb])
        p.op("t", lambda e, kc=kc: e.matmul(
            sg_ps[0:1, :], segt[:, kc:kc + 1],
            xn_sb[:, kc * T + g * TG: kc * T + g * TG + TG],
            start=(kc == 0), stop=(kc == NCH - 1), skip_group_check=True),
            reads=[segtb, xnb[kc][g]], writes=[sg_psb])
    p.op("c", lambda e: e.activation(sc_sb[:, :], sc_ps[0:8, :], AF.Copy),
         reads=[sc_psb], writes=[sc_sbb])
    p.op("c", lambda e, g=g: e.activation(srow[0:1, g * TG:(g + 1) * TG],
                                          sg_ps[0:1, :], AF.Sigmoid),
         reads=[sg_psb], writes=[srowb])
    for j in range(4):
        p.op("t", lambda e, j=j: e.transpose(
            tr_ps[:, 0:8], sc_sb[0:8, j * 128:(j + 1) * 128], ident_f32[0:8, 0:8]),
            reads=[sc_sbb, ident_f32b], writes=[tr_psb])
        p.op("v", lambda e: e.tensor_copy(st_sb[:, :], tr_ps[:, 0:8]),
             reads=[tr_psb], writes=[st_sbb])
        p.op("v", lambda e: e.reduce_max(stats[:, 0:1], st_sb[:, :], AX.X),
             reads=[st_sbb], writes=[statsb])
        p.op("v", lambda e: e.tensor_scalar(eq1[:, :], st_sb[:, :], stats[:, 0:1],
                                            None, ALU.is_equal),
             reads=[st_sbb, statsb], writes=[eq1b])
        p.op("v", lambda e: e.scalar_tensor_tensor(
            smk[:, :], eq1[:, :], -1e30, st_sb[:, :], ALU.mult, ALU.add),
            reads=[eq1b, st_sbb], writes=[smkb])
        p.op("v", lambda e: e.reduce_max(stats[:, 1:2], smk[:, :], AX.X),
             reads=[smkb, statsb], writes=[statsb])
        p.op("v", lambda e: e.tensor_sub(stats[:, 2:3], stats[:, 0:1], stats[:, 1:2]),
             reads=[statsb], writes=[statsb])
        p.op("c", lambda e: e.activation(stats[:, 3:4], stats[:, 2:3], AF.Sigmoid),
             reads=[statsb], writes=[statsb])
        p.op("c", lambda e: e.activation(stats[:, 4:5], stats[:, 2:3], AF.Sigmoid,
                                         scale=-1.0),
             reads=[statsb], writes=[statsb])
        p.op("v", lambda e: e.tensor_scalar(eq2[:, :], smk[:, :], stats[:, 1:2],
                                            None, ALU.is_equal),
             reads=[smkb, statsb], writes=[eq2b])
        p.op("v", lambda e: e.tensor_scalar(eq1[:, :], eq1[:, :], stats[:, 3:4],
                                            None, ALU.mult),
             reads=[eq1b, statsb], writes=[eq1b])
        p.op("v", lambda e: e.tensor_scalar(eq2[:, :], eq2[:, :], stats[:, 4:5],
                                            None, ALU.mult),
             reads=[eq2b, statsb], writes=[eq2b])
        p.op("v", lambda e: e.tensor_add(ct[:, :], eq1[:, :], eq2[:, :]),
             reads=[eq1b, eq2b], writes=[ctb])
        p.op("t", lambda e, j=j: e.transpose(
            tr_ps[0:8, 128:256], ct[:, :], ident_f32[:, :]),
            reads=[ctb, ident_f32b], writes=[tr_psb])
        p.op("c", lambda e, j=j, g=g: e.activation(
            combT[0:8, g * TG + j * 128: g * TG + (j + 1) * 128],
            tr_ps[0:8, 128:256], AF.Copy),
            reads=[tr_psb], writes=[combTb])


def emit_gating_l1(ctx, consts, g, W, xn_sb, xnb, combT, combTb, gwt, gwtb):
    p = ctx.p
    sc_ps, sc_psb = ctx.pt[4], ctx.pb[4]
    tr_ps, tr_psb = ctx.pt[5], ctx.pb[5]
    ident_f32, ident_f32b = consts["ident_f32"]
    sc_sb, sc_sbb = ctx.ffn_sc
    st_sb, st_sbb = ctx.ffn_st
    eq1, eq1b = ctx.ffn_eq1
    eq2, eq2b = ctx.ffn_eq2
    smk, smkb = ctx.ffn_smk
    ct, ctb = ctx.ffn_ct
    stats, statsb = ctx.ffn_stats
    for kc in range(NCH):
        p.op("t", lambda e, kc=kc: e.matmul(
            sc_ps[0:8, :], gwt[:, kc * 8:(kc + 1) * 8],
            xn_sb[:, kc * T + g * TG: kc * T + g * TG + TG],
            start=(kc == 0), stop=(kc == NCH - 1)),
            reads=[gwtb, xnb[kc][g]], writes=[sc_psb])
    p.op("c", lambda e: e.activation(sc_sb[:, :], sc_ps[0:8, :], AF.Copy),
         reads=[sc_psb], writes=[sc_sbb])
    for j in range(4):
        p.op("t", lambda e, j=j: e.transpose(
            tr_ps[:, 0:8], sc_sb[0:8, j * 128:(j + 1) * 128], ident_f32[0:8, 0:8]),
            reads=[sc_sbb, ident_f32b], writes=[tr_psb])
        p.op("v", lambda e: e.tensor_copy(st_sb[:, :], tr_ps[:, 0:8]),
             reads=[tr_psb], writes=[st_sbb])
        p.op("v", lambda e: e.reduce_max(stats[:, 0:1], st_sb[:, :], AX.X),
             reads=[st_sbb], writes=[statsb])
        p.op("v", lambda e: e.tensor_scalar(stats[:, 1:2], stats[:, 0:1], -1.0,
                                            None, ALU.mult),
             reads=[statsb], writes=[statsb])
        p.op("c", lambda e: e.activation(st_sb[:, :], st_sb[:, :], AF.Exp,
                                         bias=stats[:, 1:2]),
             reads=[st_sbb, statsb], writes=[st_sbb])
        p.op("v", lambda e: e.reduce_sum(stats[:, 2:3], st_sb[:, :], AX.X),
             reads=[st_sbb, statsb], writes=[statsb])
        p.op("v", lambda e: e.reciprocal(stats[:, 3:4], stats[:, 2:3]),
             reads=[statsb], writes=[statsb])
        p.op("v", lambda e: e.tensor_scalar(st_sb[:, :], st_sb[:, :], stats[:, 3:4],
                                            None, ALU.mult),
             reads=[st_sbb, statsb], writes=[st_sbb])
        p.op("v", lambda e: e.reduce_max(stats[:, 4:5], st_sb[:, :], AX.X),
             reads=[st_sbb, statsb], writes=[statsb])
        p.op("v", lambda e: e.tensor_scalar(eq1[:, :], st_sb[:, :], stats[:, 4:5],
                                            None, ALU.is_equal),
             reads=[st_sbb, statsb], writes=[eq1b])
        p.op("v", lambda e: e.scalar_tensor_tensor(
            smk[:, :], eq1[:, :], -1e30, st_sb[:, :], ALU.mult, ALU.add),
            reads=[eq1b, st_sbb], writes=[smkb])
        p.op("v", lambda e: e.reduce_max(stats[:, 5:6], smk[:, :], AX.X),
             reads=[smkb, statsb], writes=[statsb])
        p.op("v", lambda e: e.tensor_scalar(eq2[:, :], smk[:, :], stats[:, 5:6],
                                            None, ALU.is_equal),
             reads=[smkb, statsb], writes=[eq2b])
        p.op("v", lambda e: e.tensor_add(eq1[:, :], eq1[:, :], eq2[:, :]),
             reads=[eq1b, eq2b], writes=[eq1b])
        p.op("v", lambda e: e.tensor_mul(ct[:, :], st_sb[:, :], eq1[:, :]),
             reads=[st_sbb, eq1b], writes=[ctb])
        p.op("t", lambda e, j=j: e.transpose(
            tr_ps[0:8, 128:256], ct[:, :], ident_f32[:, :]),
            reads=[ctb, ident_f32b], writes=[tr_psb])
        p.op("c", lambda e, j=j, g=g: e.activation(
            combT[0:8, g * TG + j * 128: g * TG + (j + 1) * 128],
            tr_ps[0:8, 128:256], AF.Copy),
            reads=[tr_psb], writes=[combTb])


def emit_ffn(ctx, consts, ln, x_dram, y_dram, W, nh_ch, nhs_ch, kind):
    p = ctx.p
    nc = ctx.nc
    xn_sb, xnb = ctx.ffn_xn
    combT, combTb = ctx.ffn_combT
    sel, selb = ctx.ffn_sel
    gwt, gwtb = ctx.ffn_gwt
    wgt = ctx.ffn_wgt
    wut = ctx.ffn_wut
    wdt = ctx.ffn_wdt
    x1t = ctx.ffn_x1t
    m_sb, m_b = ctx.ffn_m
    cbc, cbcb = ctx.ffn_cbc
    dt_sb = ctx.ffn_dt
    ones_k1_bf, ones_k1_bfb = consts["ones_k1_bf"]

    has_sgate = (kind == "l0")
    if has_sgate:
        srow, srowb = ctx.ffn_srow
        segt, segtb = ctx.ffn_segt

    p.dma("g", gwt[:, :].rearrange("p (kc f) -> p kc f", kc=NCH),
          W["gwT"].rearrange("(kc p) f -> p kc f", p=128), writes=[gwtb])
    if has_sgate:
        p.dma("g", segt[:, :], W["segT"], writes=[segtb])
    p.dma("g", sel[:, :], W["sel"], writes=[selb])
    for g in range(NTG):
        ln.emit(x_dram, g, xn_sb, [xnb[c][g] for c in range(NCH)], consts,
                (ctx.pt[0], ctx.pb[0]), (ctx.pt[1], ctx.pb[1]),
                (ctx.pt[2], ctx.pb[2]), (ctx.pt[3], ctx.pb[3]),
                xn_off=g * TG, xn_stride=T)
        if kind == "l0":
            emit_gating_l0(ctx, consts, g, W, xn_sb, xnb, combT, combTb,
                           srow, srowb, gwt, gwtb, segt, segtb)
        else:
            emit_gating_l1(ctx, consts, g, W, xn_sb, xnb, combT, combTb, gwt, gwtb)

    ybufs = [[Buf(f"y{fc}_{ts}") for ts in range(4)] for fc in range(NCH)]
    p.dma("s", y_dram, ctx.ffn_yin, writes=[b for row in ybufs for b in row])

    x1_ps = [(ctx.pt[0], ctx.pb[0]), (ctx.pt[1], ctx.pb[1])]
    x3_ps = [(ctx.pt[2], ctx.pb[2]), (ctx.pt[3], ctx.pb[3])]
    dn_ps = [(ctx.pt[4], ctx.pb[4]), (ctx.pt[5], ctx.pb[5])]
    cb_ps = (ctx.pt[6], ctx.pb[6])

    def expert_pass(nch_h, load_g, load_u, load_d, scale_mode, e_tag):
        if scale_mode is not None:
            cbt, cbb = cb_ps
            for ts in range(4):
                if scale_mode[0] == "sel":
                    eidx = scale_mode[1]
                    p.op("t", lambda e, ts=ts, eidx=eidx: e.matmul(
                        cbt[:, :], sel[:, eidx * 128:(eidx + 1) * 128],
                        combT[0:8, ts * TG:(ts + 1) * TG], start=True, stop=True),
                        reads=[selb, combTb], writes=[cbb])
                else:
                    p.op("t", lambda e, ts=ts: e.matmul(
                        cbt[:, :], ones_k1_bf[:, :],
                        srow[0:1, ts * TG:(ts + 1) * TG], start=True, stop=True),
                        reads=[ones_k1_bfb, srowb], writes=[cbb])
                p.op("c", lambda e, ts=ts: e.activation(
                    cbc[:, ts * TG:(ts + 1) * TG], cbt[:, :], AF.Copy),
                    reads=[cbb], writes=[cbcb])
        for hc in range(nch_h):
            wgt_t, wgt_b = wgt[hc % 2]
            wut_t, wut_b = wut[hc % 2]
            p.dma("g", wgt_t[:, :], load_g(hc), writes=[wgt_b])
            p.dma("g", wut_t[:, :], load_u(hc), writes=[wut_b])
            for ts in range(4):
                x1t_t, x1t_b = x1t[ts % 2]
                xps, xpsb = x1_ps[ts % 2]
                ups, upsb = x3_ps[ts % 2]
                for kc in range(NCH):
                    p.op("t", lambda e, kc=kc, ts=ts, xps=xps, wgt_t=wgt_t: e.matmul(
                        xps[:, :], wgt_t[:, kc * 128:(kc + 1) * 128],
                        xn_sb[:, kc * T + ts * TG: kc * T + ts * TG + TG],
                        start=(kc == 0), stop=(kc == NCH - 1)),
                        reads=[wgt_b, xnb[kc][ts]], writes=[xpsb])
                p.op("c", lambda e, xps=xps, x1t_t=x1t_t: e.activation(
                    x1t_t[:, :], xps[:, :], AF.Silu),
                    reads=[xpsb], writes=[x1t_b])
                for kc in range(NCH):
                    p.op("t", lambda e, kc=kc, ts=ts, ups=ups, wut_t=wut_t: e.matmul(
                        ups[:, :], wut_t[:, kc * 128:(kc + 1) * 128],
                        xn_sb[:, kc * T + ts * TG: kc * T + ts * TG + TG],
                        start=(kc == 0), stop=(kc == NCH - 1), skip_group_check=True),
                        reads=[wut_b, xnb[kc][ts]], writes=[upsb])
                mslice = m_sb[:, hc * T + ts * TG: hc * T + ts * TG + TG]
                p.op("v", lambda e, mslice=mslice, x1t_t=x1t_t, ups=ups: e.tensor_mul(
                    mslice, x1t_t[:, :], ups[:, :]),
                    reads=[x1t_b, upsb], writes=[m_b[hc][ts]])
                if scale_mode is not None:
                    p.op("v", lambda e, mslice=mslice, ts=ts: e.tensor_mul(
                        mslice, mslice, cbc[:, ts * TG:(ts + 1) * TG]),
                        reads=[m_b[hc][ts], cbcb], writes=[m_b[hc][ts]])
        for fcg in range(4):
            wdt_t, wdt_b = wdt[fcg % 2]
            p.dma("g", wdt_t[:, 0:nch_h * 512], load_d(fcg), writes=[wdt_b])
            for fc4 in range(4):
                fc = fcg * 4 + fc4
                for ts in range(4):
                    dps, dpsb = dn_ps[(fc4 * 4 + ts) % 2]
                    for hc in range(nch_h):
                        p.op("t", lambda e, hc=hc, fc4=fc4, ts=ts, dps=dps, wdt_t=wdt_t: e.matmul(
                            dps[:, :], wdt_t[:, hc * 512 + fc4 * 128: hc * 512 + (fc4 + 1) * 128],
                            m_sb[:, hc * T + ts * TG: hc * T + ts * TG + TG],
                            start=(hc == 0), stop=(hc == nch_h - 1)),
                            reads=[wdt_b, m_b[hc][ts]], writes=[dpsb])
                    dt_t, dt_b = dt_sb[(fc4 * 4 + ts) % 2]
                    p.op("v", lambda e, dps=dps, dt_t=dt_t: e.tensor_copy(dt_t[:, :], dps[:, :]),
                         reads=[dpsb], writes=[dt_b])
                    p.dma("g", y_dram[fc * 128:(fc + 1) * 128, ts * TG:(ts + 1) * TG],
                          dt_t[:, :], reads=[dt_b, ybufs[fc][ts]],
                          writes=[ybufs[fc][ts]], accum_op=ALU.add)

    for e_i in range(8):
        expert_pass(
            nh_ch,
            lambda hc, e_i=e_i: W["wg"][(e_i * nh_ch + hc) * 128:
                                        (e_i * nh_ch + hc + 1) * 128, :],
            lambda hc, e_i=e_i: W["wu"][(e_i * nh_ch + hc) * 128:
                                        (e_i * nh_ch + hc + 1) * 128, :],
            lambda fcg, e_i=e_i: W["wd"][(e_i * 4 + fcg) * 128:
                                         (e_i * 4 + fcg + 1) * 128, :],
            ("sel", e_i), f"e{e_i}")
    half = nhs_ch // 2
    for sh_i in range(2):
        expert_pass(
            half,
            lambda hc, sh_i=sh_i: W["shg"][(sh_i * half + hc) * 128:
                                           (sh_i * half + hc + 1) * 128, :],
            lambda hc, sh_i=sh_i: W["shu"][(sh_i * half + hc) * 128:
                                           (sh_i * half + hc + 1) * 128, :],
            lambda fcg, sh_i=sh_i: W["shd"][fcg * 128:(fcg + 1) * 128,
                                            sh_i * half * 512:(sh_i + 1) * half * 512],
            ("sgate", None) if has_sgate else None, f"sh{sh_i}")


def build_ffn(kind):
    nh_ch = 8 if kind == "l0" else 11
    nhs_ch = 16 if kind == "l0" else 22
    hdim = nh_ch * 128
    hs = nhs_ch * 128
    nc = bass.Bass("TRN2", target_bir_lowering=False)
    xT = nc.dram_tensor("xT", [D, T], F32, kind="ExternalInput").ap()
    yT = nc.dram_tensor("yT", [D, T], F32, kind="ExternalOutput").ap()
    yin = nc.dram_tensor("yin", [D, T], F32, kind="ExternalInput").ap()
    W = {
        "gwT": nc.dram_tensor("gwT", [D, 8], F32, kind="ExternalInput").ap(),
        "wg": nc.dram_tensor("wg", [8 * nh_ch * 128, D], F32, kind="ExternalInput").ap(),
        "wu": nc.dram_tensor("wu", [8 * nh_ch * 128, D], F32, kind="ExternalInput").ap(),
        "wd": nc.dram_tensor("wd", [8 * 4 * 128, nh_ch * 512], F32, kind="ExternalInput").ap(),
        "shg": nc.dram_tensor("shg", [nhs_ch * 128, D], F32, kind="ExternalInput").ap(),
        "shu": nc.dram_tensor("shu", [nhs_ch * 128, D], F32, kind="ExternalInput").ap(),
        "shd": nc.dram_tensor("shd", [4 * 128, nhs_ch * 512], F32, kind="ExternalInput").ap(),
        "sel": nc.dram_tensor("sel", [8, 1024], F32, kind="ExternalInput").ap(),
    }
    if kind == "l0":
        W["segT"] = nc.dram_tensor("segT", [128, NCH], F32, kind="ExternalInput").ap()
    ones_col = nc.dram_tensor("ones_col", [128, 1], F32, kind="ExternalInput").ap()
    ones_k1 = nc.dram_tensor("ones_k1", [1, 128], F32, kind="ExternalInput").ap()
    ident = nc.dram_tensor("ident", [128, 128], F32, kind="ExternalInput").ap()
    dram = {"ones_col": ones_col, "ones_k1": ones_k1, "ident": ident}

    p = Prog()
    with contextlib.ExitStack() as stack:
        ctx = Ctx(nc, stack, p)
        consts = make_consts(ctx, p, dram)
        ln = LNBlock(ctx)
        ctx.ffn_xn = (ctx.sb([128, NCH * T], BF16, "xn"),
                      [[Buf(f"xn{c}_{g}") for g in range(NTG)] for c in range(NCH)])
        ctx.ffn_combT = ctx.sbuf([8, T], BF16, "combT")
        ctx.ffn_sel = ctx.sbuf([8, 1024], BF16, "self_sel")
        ctx.ffn_gwt = ctx.sbuf([128, NCH * 8], BF16, "gwt")
        ctx.ffn_wgt = [ctx.sbuf([128, NCH * 128], BF16, f"wgt{i}") for i in range(2)]
        ctx.ffn_wut = [ctx.sbuf([128, NCH * 128], BF16, f"wut{i}") for i in range(2)]
        ctx.ffn_wdt = [ctx.sbuf([128, 11 * 512], BF16, f"wdt{i}") for i in range(2)]
        ctx.ffn_x1t = [ctx.sbuf([128, TG], BF16, f"x1t{i}") for i in range(2)]
        nh_max = max(nh_ch, nhs_ch // 2)
        ctx.ffn_m = (ctx.sb([128, nh_max * T], BF16, "m_sb"),
                     [[Buf(f"m{h}_{t}") for t in range(4)] for h in range(nh_max)])
        ctx.ffn_cbc = ctx.sbuf([128, NTG * TG], BF16, "cbc")
        ctx.ffn_dt = [ctx.sbuf([128, TG], F32, f"dtb{i}") for i in range(2)]
        ctx.ffn_sc = ctx.sbuf([8, TG], F32, "ffn_sc")
        ctx.ffn_st = ctx.sbuf([128, 8], F32, "ffn_st")
        ctx.ffn_eq1 = ctx.sbuf([128, 8], F32, "ffn_eq1")
        ctx.ffn_eq2 = ctx.sbuf([128, 8], F32, "ffn_eq2")
        ctx.ffn_smk = ctx.sbuf([128, 8], F32, "ffn_smk")
        ctx.ffn_ct = ctx.sbuf([128, 8], F32, "ffn_ct")
        ctx.ffn_stats = ctx.sbuf([128, 8], F32, "ffn_stats")
        if kind == "l0":
            ctx.ffn_srow = ctx.sbuf([1, T], BF16, "ffn_srow")
            ctx.ffn_segt = ctx.sbuf([128, NCH], BF16, "ffn_segt")

        with nc.Block() as block:
            ctx.ffn_yin = yin
            emit_ffn(ctx, consts, ln, xT, yT, W, nh_ch, nhs_ch, kind)
            p.finish()
            p.materialize(nc, block, stack)
    return nc


def run_ffn(kind, xTs, w):
    nc = _get_nc("ffn_" + kind, lambda: build_ffn(kind))
    cn = _consts_np()
    shared = dict(w)
    shared.update(cn)
    in_maps = [{"xT": xTs[c], "yin": xTs[c], **shared} for c in range(NCORES)]
    res = run_bass_kernel_spmd(nc, in_maps, core_ids=list(range(NCORES)))
    return [res.results[c]["yT"] for c in range(NCORES)]


def _sel_np():
    s = np.zeros((8, 1024), np.float32)
    for e in range(8):
        s[e, e * 128:(e + 1) * 128] = 1.0
    return s


def _rg(w, kc_n, out_n, ow):
    a = np.asarray(w, np.float32).reshape(kc_n, 128, out_n, ow)
    return np.ascontiguousarray(a.transpose(2, 1, 0, 3).reshape(out_n * 128, kc_n * ow))


def _restage_up(w_de, nh):
    a = np.asarray(w_de, np.float32).reshape(NCH, 128, nh, 128)
    return np.ascontiguousarray(a.transpose(2, 1, 0, 3).reshape(nh * 128, D))


def _restage_down(w_hd, nh):
    a = np.asarray(w_hd, np.float32).reshape(nh, 128, 4, 512)
    return np.ascontiguousarray(a.transpose(2, 1, 0, 3).reshape(4 * 128, nh * 512))


def prep_ffn_l0(ins):
    wg = np.asarray(ins["L0_ffn_w_gate"], np.float32)
    wu = np.asarray(ins["L0_ffn_w_up"], np.float32)
    wd = np.asarray(ins["L0_ffn_w_down"], np.float32)
    return {
        "gwT": _np(ins["L0_ffn_gate_w"].T),
        "wg": np.concatenate([_restage_up(wg[e].T, 8) for e in range(8)], axis=0),
        "wu": np.concatenate([_restage_up(wu[e].T, 8) for e in range(8)], axis=0),
        "wd": np.concatenate([_restage_down(wd[e].T, 8) for e in range(8)], axis=0),
        "shg": _restage_up(np.asarray(ins["L0_ffn_sh_gate"], np.float32).T, 16),
        "shu": _restage_up(np.asarray(ins["L0_ffn_sh_up"], np.float32).T, 16),
        "shd": _restage_down(np.asarray(ins["L0_ffn_sh_down"], np.float32).T, 16),
        "segT": _np(np.asarray(ins["L0_ffn_se_gate_w"]).reshape(NCH, 128).T),
        "sel": _sel_np(),
    }


def prep_ffn_l1(ins):
    wg = np.asarray(ins["L1_ffn_w_gate"], np.float32)
    wu = np.asarray(ins["L1_ffn_w_up"], np.float32)
    wd = np.asarray(ins["L1_ffn_w_down"], np.float32)
    return {
        "gwT": _np(ins["L1_ffn_gate_w"]),
        "wg": np.concatenate([_restage_up(wg[e], 11) for e in range(8)], axis=0),
        "wu": np.concatenate([_restage_up(wu[e], 11) for e in range(8)], axis=0),
        "wd": np.concatenate([_restage_down(wd[e], 11) for e in range(8)], axis=0),
        "shg": _restage_up(np.asarray(ins["L1_ffn_sw_gate"], np.float32), 22),
        "shu": _restage_up(np.asarray(ins["L1_ffn_sw_up"], np.float32), 22),
        "shd": _restage_down(np.asarray(ins["L1_ffn_sw_down"], np.float32), 22),
        "sel": _sel_np(),
    }




def emit_rope_evac(ctx, src_ps, src_psb, dst_ap, dst_buf, cosx, cosxb, sinx, sinxb,
                   gcol, rot_t, rot_tb, rq_ps, rq_psb):
    p = ctx.p
    ra, rab = ctx.rope_ra
    qsb, qsbb = ctx.rope_qs
    p.op("c", lambda e: e.activation(qsb[:, :], src_ps[:, :], AF.Copy),
         reads=[src_psb], writes=[qsbb])
    p.op("t", lambda e: e.matmul(rq_ps[:, :], rot_t[:, :], qsb[:, :],
                                 start=True, stop=True),
         reads=[qsbb, rot_tb], writes=[rq_psb])
    p.op("v", lambda e: e.tensor_mul(ra[:, :], qsb[:, :],
                                     cosx[:, gcol:gcol + TG]),
         reads=[qsbb, cosxb], writes=[rab])
    p.op("v", lambda e: e.tensor_mul(dst_ap, rq_ps[:, :],
                                     sinx[:, gcol:gcol + TG]),
         reads=[rq_psb, sinxb], writes=[dst_buf])
    p.op("v", lambda e: e.tensor_add(dst_ap, dst_ap, ra[:, :]),
         reads=[rab, dst_buf], writes=[dst_buf])


def emit_norm_stat(ctx, consts, src_ap, src_buf, nrow, nrowb, h, first):
    p = ctx.p
    ones_col_bf, ones_col_bfb = consts["ones_col_bf"]
    sq, sqb = ctx.ns_sq
    ns_ps, ns_psb = ctx.pt[6], ctx.pb[6]
    tmp, tmpb = ctx.ns_tmp
    p.op("c", lambda e: e.activation(sq[:, :], src_ap, AF.Square),
         reads=[src_buf], writes=[sqb])
    p.op("t", lambda e: e.matmul(ns_ps[0:1, :], ones_col_bf[:, :], sq[:, :],
                                 start=True, stop=True),
         reads=[sqb, ones_col_bfb], writes=[ns_psb])
    p.op("v", lambda e: e.reduce_max(tmp[0:1, 0:1], ns_ps[0:1, :], AX.X),
         reads=[ns_psb], writes=[tmpb])
    if first:
        p.op("v", lambda e, h=h: e.tensor_copy(nrow[0:1, h:h + 1], tmp[0:1, 0:1]),
             reads=[tmpb], writes=[nrowb])
    else:
        p.op("v", lambda e, h=h: e.tensor_max(nrow[0:1, h:h + 1], nrow[0:1, h:h + 1],
                                              tmp[0:1, 0:1]),
             reads=[tmpb, nrowb], writes=[nrowb])


def build_mla_prep():
    nc = bass.Bass("TRN2", target_bir_lowering=False)
    xT = nc.dram_tensor("xT", [D, T], F32, kind="ExternalInput").ap()
    qT_o = nc.dram_tensor("qT_o", [D, T], BF16, kind="ExternalOutput").ap()
    kT_o = nc.dram_tensor("kT_o", [D, T], BF16, kind="ExternalOutput").ap()
    v_o = nc.dram_tensor("v_o", [NH * 128, NCH * 128], BF16, kind="ExternalOutput").ap()
    qn_o = nc.dram_tensor("qn_o", [1, NH], F32, kind="ExternalOutput").ap()
    kn_o = nc.dram_tensor("kn_o", [1, NH], F32, kind="ExternalOutput").ap()
    qwT = nc.dram_tensor("qwT", [NH * 128, D], F32, kind="ExternalInput").ap()
    kvlT = nc.dram_tensor("kvlT", [4 * 128, D], F32, kind="ExternalInput").ap()
    kvbkT = nc.dram_tensor("kvbkT", [NH * 128, 512], F32, kind="ExternalInput").ap()
    kvbvT = nc.dram_tensor("kvbvT", [4 * 128, 4 * 512], F32, kind="ExternalInput").ap()
    kvnw = nc.dram_tensor("kvnw", [128, 4], F32, kind="ExternalInput").ap()
    cos_d = nc.dram_tensor("cos_d", [128, T], F32, kind="ExternalInput").ap()
    sin_d = nc.dram_tensor("sin_d", [128, T], F32, kind="ExternalInput").ap()
    rot_d = nc.dram_tensor("rot_d", [128, 128], F32, kind="ExternalInput").ap()
    ones_col = nc.dram_tensor("ones_col", [128, 1], F32, kind="ExternalInput").ap()
    ones_k1 = nc.dram_tensor("ones_k1", [1, 128], F32, kind="ExternalInput").ap()
    ident = nc.dram_tensor("ident", [128, 128], F32, kind="ExternalInput").ap()
    dram = {"ones_col": ones_col, "ones_k1": ones_k1, "ident": ident}

    p = Prog()
    with contextlib.ExitStack() as stack:
        ctx = Ctx(nc, stack, p)
        consts = make_consts(ctx, p, dram)
        ln = LNBlock(ctx)
        xn_sb = ctx.sb([128, NCH * T], BF16, "xn")
        xnb = [[Buf(f"xn{c}_{g}") for g in range(NTG)] for c in range(NCH)]
        cos_sb, cosb = ctx.sbuf([128, T], F32, "cos_sb")
        sin_sb, sinb = ctx.sbuf([128, T], F32, "sin_sb")
        ctx.rope_ra = ctx.sbuf([128, TG], F32, "rope_ra")
        ctx.rope_qs = ctx.sbuf([128, TG], BF16, "rope_qs")
        rot_sb, rot_sbb = ctx.sbuf([128, 128], BF16, "rot_sb")
        ctx.ns_sq = ctx.sbuf([128, TG], BF16, "ns_sq")
        ctx.ns_tmp = ctx.sbuf([1, 2], F32, "ns_tmp")
        qn_sb, qnb = ctx.sbuf([1, NH], F32, "qn_sb")
        kn_sb, knb = ctx.sbuf([1, NH], F32, "kn_sb")
        wqt = [ctx.sbuf([128, NCH * 128], BF16, f"wqt{i}") for i in range(2)]
        kvlt = [ctx.sbuf([128, NCH * 128], BF16, f"kvlt{i}") for i in range(2)]
        kbt = [ctx.sbuf([128, 4 * 128], BF16, f"kbt{i}") for i in range(2)]
        vbt = [ctx.sbuf([128, 4 * 512], BF16, f"vbt{i}") for i in range(2)]
        lraw = ctx.sbuf([128, 4 * TG], F32, "lraw")
        latn_sb = ctx.sb([128, 4 * T], BF16, "latn")
        latnb = [[Buf(f"lat{lc}_{g}") for g in range(NTG)] for lc in range(4)]
        kvnw_sb, kvnwb = ctx.sbuf([128, 4], F32, "kvnw_sb")
        otile = [ctx.sbuf([128, TG], BF16, f"otile{i}") for i in range(2)]
        rows2 = ctx.sb([1, 4 * TG], F32, "rms_rows")
        rows2b = [Buf(f"rms_row{i}") for i in range(4)]

        p.dma("s", cos_sb[:, :], cos_d, writes=[cosb])
        p.dma("s", sin_sb[:, :], sin_d, writes=[sinb])
        p.dma("s", kvnw_sb[:, :], kvnw, writes=[kvnwb])
        p.dma("g", rot_sb[:, :], rot_d, writes=[rot_sbb])

        with nc.Block() as block:
            for g in range(NTG):
                ln.emit(xT, g, xn_sb, [xnb[c][g] for c in range(NCH)], consts,
                        (ctx.pt[0], ctx.pb[0]), (ctx.pt[1], ctx.pb[1]),
                        (ctx.pt[2], ctx.pb[2]), (ctx.pt[3], ctx.pb[3]),
                        xn_off=g * TG, xn_stride=T)
                sq_ps, sq_psb = ctx.pt[4], ctx.pb[4]
                for lc in range(4):
                    lt, ltb = kvlt[lc % 2]
                    p.dma("g", lt[:, :], kvlT[lc * 128:(lc + 1) * 128, :], writes=[ltb])
                    lps, lpsb = ctx.pt[lc % 2], ctx.pb[lc % 2]
                    for kc in range(NCH):
                        p.op("t", lambda e, kc=kc, lt=lt, lps=lps, g=g: e.matmul(
                            lps[:, :], lt[:, kc * 128:(kc + 1) * 128],
                            xn_sb[:, kc * T + g * TG: kc * T + g * TG + TG],
                            start=(kc == 0), stop=(kc == NCH - 1)),
                            reads=[ltb, xnb[kc][g]], writes=[lpsb])
                    p.op("v", lambda e, lc=lc, lps=lps: e.tensor_copy(
                        lraw[0][:, lc * TG:(lc + 1) * TG], lps[:, :]),
                        reads=[lpsb], writes=[lraw[1]])
                    sqt, sqtb = ctx.ns_sq
                    p.op("c", lambda e, lc=lc: e.activation(
                        sqt[:, :], lraw[0][:, lc * TG:(lc + 1) * TG], AF.Square),
                        reads=[lraw[1]], writes=[sqtb])
                    p.op("t", lambda e, lc=lc, sqt=sqt: e.matmul(
                        sq_ps[0:1, :], consts["ones_col_bf"][0][:, :], sqt[:, :],
                        start=(lc == 0), stop=(lc == 3), skip_group_check=True),
                        reads=[sqtb, consts["ones_col_bf"][1]], writes=[sq_psb])
                MS, RS = 0, 1
                p.op("v", lambda e: e.tensor_scalar(
                    rows2[0:1, MS * TG:(MS + 1) * TG], sq_ps[0:1, :], 1.0 / 512.0,
                    1e-6, ALU.mult, ALU.add),
                    reads=[sq_psb], writes=[rows2b[MS]])
                p.op("c", lambda e: e.activation(
                    rows2[0:1, MS * TG:(MS + 1) * TG], rows2[0:1, MS * TG:(MS + 1) * TG],
                    AF.Sqrt), reads=[rows2b[MS]], writes=[rows2b[MS]])
                p.op("v", lambda e: e.reciprocal(
                    rows2[0:1, RS * TG:(RS + 1) * TG], rows2[0:1, MS * TG:(MS + 1) * TG]),
                    reads=[rows2b[MS]], writes=[rows2b[RS]])
                rbc_ps, rbc_psb = ctx.pt[5], ctx.pb[5]
                p.op("t", lambda e: e.matmul(
                    rbc_ps[:, :], consts["ones_k1"][0][:, :],
                    rows2[0:1, RS * TG:(RS + 1) * TG], start=True, stop=True),
                    reads=[rows2b[RS], consts["ones_k1"][1]], writes=[rbc_psb])
                for lc in range(4):
                    p.op("v", lambda e, lc=lc: e.tensor_mul(
                        lraw[0][:, lc * TG:(lc + 1) * TG],
                        lraw[0][:, lc * TG:(lc + 1) * TG], rbc_ps[:, :]),
                        reads=[lraw[1], rbc_psb], writes=[lraw[1]])
                    p.op("v", lambda e, lc=lc, g=g: e.tensor_scalar(
                        latn_sb[:, lc * T + g * TG: lc * T + g * TG + TG],
                        lraw[0][:, lc * TG:(lc + 1) * TG], kvnw_sb[:, lc:lc + 1],
                        None, ALU.mult),
                        reads=[lraw[1], kvnwb], writes=[latnb[lc][g]])
            q_ps = [(ctx.pt[0], ctx.pb[0]), (ctx.pt[1], ctx.pb[1])]
            for h in range(NH):
                wt, wtb = wqt[h % 2]
                p.dma("g", wt[:, :], qwT[h * 128:(h + 1) * 128, :], writes=[wtb])
                for g in range(NTG):
                    qp, qpb = q_ps[g % 2]
                    for kc in range(NCH):
                        p.op("t", lambda e, kc=kc, wt=wt, qp=qp, g=g: e.matmul(
                            qp[:, :], wt[:, kc * 128:(kc + 1) * 128],
                            xn_sb[:, kc * T + g * TG: kc * T + g * TG + TG],
                            start=(kc == 0), stop=(kc == NCH - 1)),
                            reads=[wtb, xnb[kc][g]], writes=[qpb])
                    ot, otb = otile[g % 2]
                    emit_rope_evac(ctx, qp, qpb, ot[:, :], otb, cos_sb, cosb,
                                   sin_sb, sinb, g * TG, rot_sb, rot_sbb,
                                   ctx.pt[5], ctx.pb[5])
                    emit_norm_stat(ctx, consts, ot[:, :], otb, qn_sb, qnb, h, g == 0)
                    p.dma("g", qT_o[h * 128:(h + 1) * 128, g * TG:(g + 1) * TG],
                          ot[:, :], reads=[otb])
            for h in range(NH):
                kt, ktb2 = kbt[h % 2]
                p.dma("g", kt[:, :], kvbkT[h * 128:(h + 1) * 128, :], writes=[ktb2])
                for g in range(NTG):
                    qp, qpb = q_ps[g % 2]
                    for lc in range(4):
                        p.op("t", lambda e, lc=lc, kt=kt, qp=qp, g=g: e.matmul(
                            qp[:, :], kt[:, lc * 128:(lc + 1) * 128],
                            latn_sb[:, lc * T + g * TG: lc * T + g * TG + TG],
                            start=(lc == 0), stop=(lc == 3)),
                            reads=[ktb2, latnb[lc][g]], writes=[qpb])
                    ot, otb = otile[g % 2]
                    emit_rope_evac(ctx, qp, qpb, ot[:, :], otb, cos_sb, cosb,
                                   sin_sb, sinb, g * TG, rot_sb, rot_sbb,
                                   ctx.pt[5], ctx.pb[5])
                    emit_norm_stat(ctx, consts, ot[:, :], otb, kn_sb, knb, h, g == 0)
                    p.dma("g", kT_o[h * 128:(h + 1) * 128, g * TG:(g + 1) * TG],
                          ot[:, :], reads=[otb])
            for vfg in range(4):
                vt, vtb = vbt[vfg % 2]
                p.dma("g", vt[:, :], kvbvT[vfg * 128:(vfg + 1) * 128, :], writes=[vtb])
                for g in range(NTG):
                    for tsub in range(4):
                        vp, vpb = q_ps[tsub % 2]
                        for lc in range(4):
                            p.op("t", lambda e, lc=lc, vt=vt, vp=vp, g=g, tsub=tsub: e.matmul(
                                vp[:, :],
                                latn_sb[:, lc * T + g * TG + tsub * 128:
                                        lc * T + g * TG + (tsub + 1) * 128],
                                vt[:, lc * 512:(lc + 1) * 512],
                                start=(lc == 0), stop=(lc == 3)),
                                reads=[vtb, latnb[lc][g]], writes=[vpb])
                        ot, otb = otile[tsub % 2]
                        p.op("c", lambda e, vp=vp, ot=ot: e.activation(
                            ot[:, :], vp[:, :], AF.Copy), reads=[vpb], writes=[otb])
                        p.dma("g",
                              v_o.rearrange("(h p) (kc j) -> p h kc j", p=128, j=128)
                              [:, vfg * 4:(vfg + 1) * 4, g * 4 + tsub, :],
                              ot[:, :].rearrange("p (hs j) -> p hs j", hs=4),
                              reads=[otb])
            p.dma("g", qn_o, qn_sb[:, :], reads=[qnb])
            p.dma("g", kn_o, kn_sb[:, :], reads=[knb])
            p.finish()
            p.materialize(nc, block, stack)
    return nc


def build_flash(with_bias):
    nc = bass.Bass("TRN2", target_bir_lowering=False)
    qT = nc.dram_tensor("qT", [D, T], BF16, kind="ExternalInput").ap()
    kTf = nc.dram_tensor("kTf", [D, S], BF16, kind="ExternalInput").ap()
    vf = nc.dram_tensor("vf", [NH * 128, 64 * 128], BF16, kind="ExternalInput").ap()
    qn = nc.dram_tensor("qn", [1, NH], F32, kind="ExternalInput").ap()
    kn_all = nc.dram_tensor("kn_all", [1, 4 * NH], F32, kind="ExternalInput").ap()
    owT = nc.dram_tensor("owT", [4 * 128, NCH * 512], F32, kind="ExternalInput").ap()
    yin = nc.dram_tensor("yin", [D, T], F32, kind="ExternalInput").ap()
    yT = nc.dram_tensor("yT", [D, T], F32, kind="ExternalOutput").ap()
    if with_bias:
        pbias = nc.dram_tensor("pbias", [128, NCH], F32, kind="ExternalInput").ap()
    ones_col = nc.dram_tensor("ones_col", [128, 1], F32, kind="ExternalInput").ap()
    ones_k1 = nc.dram_tensor("ones_k1", [1, 128], F32, kind="ExternalInput").ap()
    ident = nc.dram_tensor("ident", [128, 128], F32, kind="ExternalInput").ap()
    dram = {"ones_col": ones_col, "ones_k1": ones_k1, "ident": ident}
    SCALE = float(DK) ** -0.5

    p = Prog()
    with contextlib.ExitStack() as stack:
        ctx = Ctx(nc, stack, p)
        consts = make_consts(ctx, p, dram)
        kt_sb = [ctx.sbuf([128, S], BF16, f"kt_sb{i}") for i in range(2)]
        v_sb = [ctx.sbuf([128, 64 * 129], BF16, f"v_sb{i}") for i in range(2)]
        q_sb = [ctx.sbuf([128, TG], BF16, f"q_sb{i}") for i in range(2)]
        at_sb = [ctx.sbuf([128, TG], BF16, f"at_sb{i}") for i in range(3)]
        outT = ctx.sb([128, NCH * T], BF16, "outT")
        outTb = [[Buf(f"oT{c}_{g}") for g in range(NTG)] for c in range(NCH)]
        nrow, nrowb = ctx.sbuf([1, 8 * NH], F32, "nrow")
        bias_sb, bias_sbb = ctx.sbuf([128, NH], F32, "bias_sb")
        rl_sb, rlb = ctx.sbuf([128, 1], F32, "rl_sb")
        osb, osbb = ctx.sbuf([128, 128], BF16, "osb")
        wot = [ctx.sbuf([128, NCH * 512], BF16, f"wotf{i}") for i in range(2)]
        dt_sb = [ctx.sbuf([128, TG], F32, f"dtf{i}") for i in range(2)]
        if with_bias:
            pb_sb, pb_sbb = ctx.sbuf([128, NCH], F32, "pb_sb")
            p.dma("s", pb_sb[:, :], pbias, writes=[pb_sbb])

        vob = [Buf(f"vones{i}") for i in range(2)]
        for i in range(2):
            p.op("v", lambda e, i=i: e.memset(
                v_sb[i][0][:, :].rearrange("p (c f) -> p c f", c=64)[:, :, 128:129], 1.0),
                writes=[v_sb[i][1], vob[i]])

        KN, QN, BR = 0, 1, 2
        p.dma("s", nrow[0:1, 4 * NH:5 * NH], qn, writes=[nrowb])
        p.dma("s", nrow[0:1, 0:4 * NH], kn_all, writes=[nrowb])
        p.op("v", lambda e: e.tensor_max(nrow[0:1, 0:NH], nrow[0:1, 0:NH],
                                         nrow[0:1, NH:2 * NH]),
             reads=[nrowb], writes=[nrowb])
        p.op("v", lambda e: e.tensor_max(nrow[0:1, 2 * NH:3 * NH], nrow[0:1, 2 * NH:3 * NH],
                                         nrow[0:1, 3 * NH:4 * NH]),
             reads=[nrowb], writes=[nrowb])
        p.op("v", lambda e: e.tensor_max(nrow[0:1, 0:NH], nrow[0:1, 0:NH],
                                         nrow[0:1, 2 * NH:3 * NH]),
             reads=[nrowb], writes=[nrowb])
        p.op("c", lambda e: e.activation(nrow[0:1, 0:NH], nrow[0:1, 0:NH], AF.Sqrt),
             reads=[nrowb], writes=[nrowb])
        p.op("c", lambda e: e.activation(nrow[0:1, 4 * NH:5 * NH],
                                         nrow[0:1, 4 * NH:5 * NH], AF.Sqrt),
             reads=[nrowb], writes=[nrowb])
        p.op("v", lambda e: e.tensor_mul(nrow[0:1, 5 * NH:6 * NH], nrow[0:1, 0:NH],
                                         nrow[0:1, 4 * NH:5 * NH]),
             reads=[nrowb], writes=[nrowb])
        p.op("v", lambda e: e.tensor_scalar(nrow[0:1, 5 * NH:6 * NH],
                                            nrow[0:1, 5 * NH:6 * NH], -SCALE,
                                            None, ALU.mult),
             reads=[nrowb], writes=[nrowb])
        bb_ps, bb_psb = ctx.pt[6], ctx.pb[6]
        p.op("t", lambda e: e.matmul(bb_ps[:, 0:NH], consts["ones_k1"][0][:, :],
                                     nrow[0:1, 5 * NH:6 * NH], start=True, stop=True),
             reads=[nrowb, consts["ones_k1"][1]], writes=[bb_psb])
        p.op("v", lambda e: e.tensor_copy(bias_sb[:, :], bb_ps[:, 0:NH]),
             reads=[bb_psb], writes=[bias_sbb])

        ov_ps = [(ctx.pt[i], ctx.pb[i]) for i in range(4)]
        st_ps = [(ctx.pt[4], ctx.pb[4]), (ctx.pt[5], ctx.pb[5])]

        with nc.Block() as block:
            for h in range(NH):
                ktt, kttb = kt_sb[h % 2]
                p.dma("g", ktt[:, :], kTf[h * 128:(h + 1) * 128, :], writes=[kttb])
                vtt, vttb = v_sb[h % 2]
                p.dma("g", vtt[:, :].rearrange("p (c f) -> p c f", c=64)[:, :, 0:128],
                      vf[h * 128:(h + 1) * 128, :].rearrange("p (c j) -> p c j", j=128),
                      reads=[vob[h % 2]], writes=[vttb])
                for qg in range(NTG):
                    qt, qtb = q_sb[qg % 2]
                    p.dma("g", qt[:, :], qT[h * 128:(h + 1) * 128, qg * TG:(qg + 1) * TG],
                          writes=[qtb])
                    first = True
                    for kc in range(64):
                        sp, spb = st_ps[kc % 2]
                        p.op("t", lambda e, kc=kc, ktt=ktt, qt=qt, sp=sp: e.matmul(
                            sp[:, :], ktt[:, kc * 128:(kc + 1) * 128], qt[:, :],
                            start=True, stop=True),
                            reads=[kttb, qtb], writes=[spb])
                        att, attb = at_sb[kc % 3]
                        p.op("c", lambda e, sp=sp, att=att, h=h: e.activation(
                            att[:, :], sp[:, :], AF.Exp, bias=bias_sb[:, h:h + 1],
                            scale=SCALE),
                            reads=[spb, bias_sbb], writes=[attb])
                        for qs in range(4):
                            op_, opb_ = ov_ps[qs]
                            p.op("t", lambda e, kc=kc, qs=qs, att=att, vtt=vtt,
                                 op_=op_, first=first: e.matmul(
                                op_[:, 0:129], att[:, qs * 128:(qs + 1) * 128],
                                vtt[:, kc * 129:(kc + 1) * 129],
                                start=first, stop=(kc == 63), skip_group_check=True),
                                reads=[attb, vttb], writes=[opb_])
                        first = False
                    for qs in range(4):
                        op_, opb_ = ov_ps[qs]
                        p.op("v", lambda e, op_=op_: e.reciprocal(rl_sb[:, :],
                                                                  op_[:, 128:129]),
                             reads=[opb_], writes=[rlb])
                        p.op("v", lambda e, op_=op_: e.tensor_scalar(
                            osb[:, :], op_[:, 0:128], rl_sb[:, 0:1], None, ALU.mult),
                            reads=[opb_, rlb], writes=[osbb])
                        p.op("t", lambda e: e.transpose(ctx.ptb[:, 0:128], osb[:, :],
                                                        consts["ident_bf"][0][:, :]),
                             reads=[osbb, consts["ident_bf"][1]], writes=[ctx.pbb])
                        p.op("c", lambda e, h=h, qg=qg, qs=qs: e.activation(
                            outT[:, h * T + qg * TG + qs * 128:
                                 h * T + qg * TG + (qs + 1) * 128],
                            ctx.ptb[:, 0:128], AF.Copy),
                            reads=[ctx.pbb], writes=[outTb[h][qg]])
            ybufs = [[Buf(f"y{fc}_{ts}") for ts in range(4)] for fc in range(NCH)]
            p.dma("s", yT, yin, writes=[b for row in ybufs for b in row])
            dn_ps = [(ctx.pt[4], ctx.pb[4]), (ctx.pt[5], ctx.pb[5])]
            for fcg in range(4):
                wt, wtb = wot[fcg % 2]
                p.dma("g", wt[:, :], owT[fcg * 128:(fcg + 1) * 128, :], writes=[wtb])
                for fc4 in range(4):
                    fc = fcg * 4 + fc4
                    for ts in range(4):
                        dps, dpsb = dn_ps[(fc4 * 4 + ts) % 2]
                        for kc in range(NCH):
                            p.op("t", lambda e, kc=kc, fc4=fc4, ts=ts, dps=dps,
                                 wt=wt: e.matmul(
                                dps[:, :], wt[:, kc * 512 + fc4 * 128:
                                              kc * 512 + (fc4 + 1) * 128],
                                outT[:, kc * T + ts * TG:(kc) * T + ts * TG + TG],
                                start=(kc == 0), stop=(kc == NCH - 1)),
                                reads=[wtb, outTb[kc][ts]], writes=[dpsb])
                        dt, dtb = dt_sb[(fc4 * 4 + ts) % 2]
                        if with_bias:
                            p.op("v", lambda e, dps=dps, dt=dt, fc=fc: e.tensor_scalar(
                                dt[:, :], dps[:, :], pb_sb[:, fc:fc + 1], None, ALU.add),
                                reads=[dpsb, pb_sbb], writes=[dtb])
                        else:
                            p.op("v", lambda e, dps=dps, dt=dt: e.tensor_copy(
                                dt[:, :], dps[:, :]),
                                reads=[dpsb], writes=[dtb])
                        p.dma("g", yT[fc * 128:(fc + 1) * 128, ts * TG:(ts + 1) * TG],
                              dt[:, :], reads=[dtb, ybufs[fc][ts]],
                              writes=[ybufs[fc][ts]], accum_op=ALU.add)
            p.finish()
            p.materialize(nc, block, stack)
    return nc


def _yarn_cos_sin():
    import math
    ROPE_DIM, MAX_POS, ORIG_POS, BASE = 64, 131072, 8192, 500000.0
    BETA_FAST, BETA_SLOW = 32.0, 1.0
    pos_freqs = BASE ** (np.arange(0, ROPE_DIM, 2, dtype=np.float64) / ROPE_DIM)
    extrap = 1.0 / pos_freqs
    interp = 1.0 / ((MAX_POS / ORIG_POS) * pos_freqs)
    low = max(math.floor(ROPE_DIM * math.log(ORIG_POS / (BETA_FAST * 2 * math.pi))
                         / (2 * math.log(BASE))), 0)
    high = min(math.ceil(ROPE_DIM * math.log(ORIG_POS / (BETA_SLOW * 2 * math.pi))
                         / (2 * math.log(BASE))), ROPE_DIM - 1)
    i = np.arange(ROPE_DIM // 2, dtype=np.float64)
    smooth = np.clip((i - low) / max(high - low, 1), 0.0, 1.0)
    inv_freq = ((1.0 - smooth) * interp + smooth * extrap).astype(np.float32)
    posv = np.arange(S, dtype=np.float32)
    freqs = posv[:, None] * inv_freq[None, :]
    emb = np.concatenate([freqs, freqs], axis=-1)
    mscale = 0.1 * math.log(MAX_POS / ORIG_POS) + 1.0
    cos = (np.cos(emb) * mscale).astype(np.float32)
    sin = (np.sin(emb) * mscale).astype(np.float32)
    cosx = np.concatenate([cos.T, np.ones((64, S), np.float32)], axis=0)
    sinx = np.concatenate([sin.T, np.zeros((64, S), np.float32)], axis=0)
    return np.ascontiguousarray(cosx), np.ascontiguousarray(sinx)


def _rot_np():
    P = np.zeros((128, 128), np.float32)
    for r in range(32):
        P[r, r + 32] = -1.0
    for r in range(32, 64):
        P[r, r - 32] = 1.0
    return np.ascontiguousarray(P.T)


def run_l0b(xTs, ins):
    cosT, sinT = _yarn_cos_sin()
    w = {
        "qwT": _rg(_np(ins["L0_attn_b_q_w"].T), NCH, NH, 128),
        "kvlT": _rg(_np(np.asarray(ins["L0_attn_b_kv_a_w"])[64:, :].T), NCH, 4, 128),
        "kvbkT": _rg(_np(np.asarray(ins["L0_attn_b_kv_b_w"]).reshape(NH, 2, 128, 512)[:, 0]
                     .transpose(2, 0, 1).reshape(512, D)), 4, NH, 128),
        "kvbvT": _rg(_np(np.asarray(ins["L0_attn_b_kv_b_w"]).reshape(NH, 2, 128, 512)[:, 1]
                     .transpose(2, 0, 1).reshape(512, D)), 4, 4, 512),
        "kvnw": _np(np.asarray(ins["L0_attn_b_kv_norm_w"]).reshape(4, 128).T),
        "rot_d": _rot_np(),
    }
    cn = _consts_np()
    nc = _get_nc("mla_prep", build_mla_prep)
    in_maps = []
    for c in range(NCORES):
        b, off = divmod(c, 4)
        in_maps.append({
            "xT": xTs[c],
            "cos_d": np.ascontiguousarray(cosT[:, off * T:(off + 1) * T]),
            "sin_d": np.ascontiguousarray(sinT[:, off * T:(off + 1) * T]),
            **w, **cn,
        })
    res = run_bass_kernel_spmd(nc, in_maps, core_ids=list(range(NCORES)))
    pre = [res.results[c] for c in range(NCORES)]
    kTfull = [np.concatenate([pre[b * 4 + j]["kT_o"] for j in range(4)], axis=1)
              for b in range(2)]
    vfull = [np.concatenate([pre[b * 4 + j]["v_o"] for j in range(4)], axis=1)
             for b in range(2)]
    knall = [np.concatenate([pre[b * 4 + j]["kn_o"] for j in range(4)], axis=1)
             for b in range(2)]
    owT = _rg(_np(ins["L0_attn_b_o_w"].T), NCH, 4, 512)
    nc2 = _get_nc("flash_nobias", lambda: build_flash(False))
    in_maps2 = []
    for c in range(NCORES):
        b = c // 4
        in_maps2.append({
            "qT": pre[c]["qT_o"], "kTf": kTfull[b], "vf": vfull[b],
            "qn": pre[c]["qn_o"], "kn_all": knall[b], "owT": owT,
            "yin": xTs[c], **cn,
        })
    res2 = run_bass_kernel_spmd(nc2, in_maps2, core_ids=list(range(NCORES)))
    return [res2.results[c]["yT"] for c in range(NCORES)]




def _emit_ln0_full(ctx, consts, ln, x_dram, y_bf, ybufs):
    for g in range(NTG):
        ln.emit(x_dram, g, y_bf, [ybufs[c][g] for c in range(NCH)], ctx.consts_,
                (ctx.pt[0], ctx.pb[0]), (ctx.pt[1], ctx.pb[1]),
                (ctx.pt[2], ctx.pb[2]), (ctx.pt[3], ctx.pb[3]),
                xn_off=g * TG, xn_stride=T)


def _emit_ln_rows_from_sbuf(ctx, consts, src_sb, src_bufs, g, rows, rowsb, roff,
                            eps=1e-5):
    p = ctx.p
    ones_col_bf, ones_col_bfb = consts["ones_col_bf"]
    ones_k1, ones_k1b = consts["ones_k1"]
    sum_t, sum_b = ctx.pt[0], ctx.pb[0]
    sq_t, sq_b = ctx.pt[1], ctx.pb[1]
    sq, sqb = ctx.ssm_sq
    T4 = NTG * TG

    def col(r):
        return r * T4 + g * TG

    for c in range(NCH):
        sl = src_sb[:, c * T + g * TG: c * T + g * TG + TG]
        p.op("c", lambda e, sl=sl: e.activation(sq[:, :], sl, AF.Square),
             reads=[src_bufs[c][g]], writes=[sqb])
        p.op("t", lambda e, sl=sl, c=c: e.matmul(
            sum_t[0:1, :], ones_col_bf[:, :], sl, start=(c == 0), stop=(c == NCH - 1)),
            reads=[src_bufs[c][g], ones_col_bfb], writes=[sum_b])
        p.op("t", lambda e, c=c: e.matmul(
            sq_t[0:1, :], ones_col_bf[:, :], sq[:, :], start=(c == 0),
            stop=(c == NCH - 1), skip_group_check=True),
            reads=[sqb, ones_col_bfb], writes=[sq_b])
    A, B = roff, roff + 1
    p.op("v", lambda e: e.tensor_scalar(rows[0:1, col(A):col(A) + TG], sum_t[0:1, :],
                                        1.0 / D, None, ALU.mult),
         reads=[sum_b], writes=[rowsb])
    p.op("v", lambda e: e.tensor_scalar(rows[0:1, col(B):col(B) + TG], sq_t[0:1, :],
                                        1.0 / D, None, ALU.mult),
         reads=[sq_b], writes=[rowsb])
    t4 = ctx.ssm_trow
    p.op("v", lambda e: e.scalar_tensor_tensor(
        t4[0][0:1, 0:TG], rows[0:1, col(A):col(A) + TG], -1.0,
        rows[0:1, col(A):col(A) + TG], ALU.mult, ALU.mult),
        reads=[rowsb], writes=[t4[1]])
    p.op("v", lambda e: e.scalar_tensor_tensor(
        t4[0][0:1, 0:TG], rows[0:1, col(B):col(B) + TG], float(eps),
        t4[0][0:1, 0:TG], ALU.add, ALU.add),
        reads=[rowsb, t4[1]], writes=[t4[1]])
    p.op("c", lambda e: e.activation(t4[0][0:1, 0:TG], t4[0][0:1, 0:TG], AF.Sqrt),
         reads=[t4[1]], writes=[t4[1]])
    p.op("v", lambda e: e.reciprocal(t4[0][0:1, TG:2 * TG], t4[0][0:1, 0:TG]),
         reads=[t4[1]], writes=[t4[1]])
    p.op("v", lambda e: e.scalar_tensor_tensor(
        rows[0:1, col(B):col(B) + TG], rows[0:1, col(A):col(A) + TG], -1.0,
        t4[0][0:1, TG:2 * TG], ALU.mult, ALU.mult),
        reads=[rowsb, t4[1]], writes=[rowsb])
    p.op("v", lambda e: e.tensor_copy(rows[0:1, col(A):col(A) + TG],
                                      t4[0][0:1, TG:2 * TG]),
         reads=[t4[1]], writes=[rowsb])


def _bcast_rows(ctx, consts, rows, rowsb, roff, g, ps_a, ps_b):
    p = ctx.p
    ones_k1, ones_k1b = consts["ones_k1"]
    T4 = NTG * TG
    a_t, a_b = ps_a
    b_t, b_b = ps_b
    p.op("t", lambda e: e.matmul(a_t[:, :], ones_k1[:, :],
                                 rows[0:1, roff * T4 + g * TG: roff * T4 + g * TG + TG],
                                 start=True, stop=True),
         reads=[rowsb, ones_k1b], writes=[a_b])
    p.op("t", lambda e: e.matmul(b_t[:, :], ones_k1[:, :],
                                 rows[0:1, (roff + 1) * T4 + g * TG:
                                      (roff + 1) * T4 + g * TG + TG],
                                 start=True, stop=True),
         reads=[rowsb, ones_k1b], writes=[b_b])


def _ssm_common_tensors(ctx, nc):
    ctx.ssm_sq = ctx.sbuf([128, TG], BF16, "ssm_sq")
    ctx.ssm_trow = ctx.sbuf([1, 2 * TG], F32, "ssm_trow")
    ctx.ssm_y = (ctx.sb([128, NCH * T], BF16, "ssm_y"),
                 [[Buf(f"y{c}_{g}") for g in range(NTG)] for c in range(NCH)])
    ctx.ssm_rows = ctx.sbuf([1, 4 * NTG * TG], F32, "ssm_rows")
    ctx.ssm_xt1 = [ctx.sbuf([128, TG], BF16, f"ssm_xt1_{i}") for i in range(2)]
    ctx.ssm_t32 = [ctx.sbuf([128, TG], F32, f"ssm_t32_{i}") for i in range(3)]
    ctx.ssm_g = ctx.sbuf([128, NCH], F32, "ssm_g")
    ctx.ssm_bt = ctx.sbuf([128, NCH], F32, "ssm_bt")
    ctx.ssm_u = ctx.sbuf([8, T], F32, "ssm_u")
    ctx.ssm_db = ctx.sbuf([8, T], F32, "ssm_dbb")
    ctx.ssm_h = ctx.sbuf([8, T], F32, "ssm_h")
    ctx.ssm_small = ctx.sbuf([8, 16], F32, "ssm_small")
    ctx.ssm_winT = ctx.sbuf([128, NCH * 8], BF16, "ssm_winT")


def _emit_xn1_tile(ctx, consts, c, g, a_ps, b_ps, dst, dstb, src_sb, src_bufs):
    p = ctx.p
    t32, t32b = ctx.ssm_t32[0]
    gcol, gcolb = ctx.ssm_gb
    bcol, bcolb = ctx.ssm_btb
    sl = src_sb[:, c * T + g * TG: c * T + g * TG + TG]
    p.op("v", lambda e, sl=sl: e.tensor_mul(t32[:, :], sl, a_ps[0][:, :]),
         reads=[src_bufs[c][g], a_ps[1]], writes=[t32b])
    p.op("v", lambda e: e.tensor_add(t32[:, :], t32[:, :], b_ps[0][:, :]),
         reads=[t32b, b_ps[1]], writes=[t32b])
    p.op("v", lambda e, c=c: e.tensor_scalar(
        dst, t32[:, :], ctx.ssm_g[0][:, c:c + 1], ctx.ssm_bt[0][:, c:c + 1],
        ALU.mult, ALU.add),
        reads=[t32b, ctx.ssm_gb2, ctx.ssm_btb2],
        writes=[dstb])


def build_ssm_a():
    nc = bass.Bass("TRN2", target_bir_lowering=False)
    xT = nc.dram_tensor("xT", [D, T], F32, kind="ExternalInput").ap()
    u_o = nc.dram_tensor("u_o", [8, T], F32, kind="ExternalOutput").ap()
    hend_o = nc.dram_tensor("hend_o", [8, 1], F32, kind="ExternalOutput").ap()
    winT = nc.dram_tensor("winT", [D, 8], F32, kind="ExternalInput").ap()
    b_in_d = nc.dram_tensor("b_in_d", [8, 1], F32, kind="ExternalInput").ap()
    dl_d = nc.dram_tensor("dl_d", [8, 1], F32, kind="ExternalInput").ap()
    gam_d = nc.dram_tensor("gam_d", [128, NCH], F32, kind="ExternalInput").ap()
    bet_d = nc.dram_tensor("bet_d", [128, NCH], F32, kind="ExternalInput").ap()
    ones_col = nc.dram_tensor("ones_col", [128, 1], F32, kind="ExternalInput").ap()
    ones_k1 = nc.dram_tensor("ones_k1", [1, 128], F32, kind="ExternalInput").ap()
    ident = nc.dram_tensor("ident", [128, 128], F32, kind="ExternalInput").ap()
    dram = {"ones_col": ones_col, "ones_k1": ones_k1, "ident": ident}

    p = Prog()
    with contextlib.ExitStack() as stack:
        ctx = Ctx(nc, stack, p)
        consts = make_consts(ctx, p, dram)
        ctx.consts_ = consts
        ln = LNBlock(ctx)
        _ssm_common_tensors(ctx, nc)
        ctx.ssm_gb2 = Buf("gam")
        ctx.ssm_btb2 = Buf("bet")
        ctx.ssm_gb = (ctx.ssm_g[0], ctx.ssm_gb2)
        ctx.ssm_btb = (ctx.ssm_bt[0], ctx.ssm_btb2)
        y_bf, ybufs = ctx.ssm_y
        rows, rowsb_ = ctx.ssm_rows
        rowsb = rowsb_
        u_sb, u_b = ctx.ssm_u
        sm, smb = ctx.ssm_small
        winT_sb, winTb = ctx.ssm_winT

        p.dma("s", ctx.ssm_g[0][:, :], gam_d, writes=[ctx.ssm_gb2])
        p.dma("s", ctx.ssm_bt[0][:, :], bet_d, writes=[ctx.ssm_btb2])
        p.dma("s", sm[0:8, 0:1], b_in_d, writes=[smb])
        p.dma("s", sm[0:8, 1:2], dl_d, writes=[smb])
        p.dma("g", winT_sb[:, :].rearrange("p (kc f) -> p kc f", kc=NCH),
              winT.rearrange("(kc p) f -> p kc f", p=128), writes=[winTb])

        with nc.Block() as block:
            _emit_ln0_full(ctx, consts, ln, xT, y_bf, ybufs)
            for g in range(NTG):
                _emit_ln_rows_from_sbuf(ctx, consts, y_bf, ybufs, g, rows, rowsb, 0)
            u_ps, u_psb = ctx.pt[6], ctx.pb[6]
            for g in range(NTG):
                _bcast_rows(ctx, consts, rows, rowsb, 0, g,
                            (ctx.pt[2], ctx.pb[2]), (ctx.pt[3], ctx.pb[3]))
                for kc in range(NCH):
                    xt1, xt1b = ctx.ssm_xt1[kc % 2]
                    _emit_xn1_tile(ctx, consts, kc, g, (ctx.pt[2], ctx.pb[2]),
                                   (ctx.pt[3], ctx.pb[3]), xt1[:, :], xt1b,
                                   y_bf, ybufs)
                    p.op("t", lambda e, kc=kc, xt1=xt1, g=g: e.matmul(
                        u_ps[0:8, :], winT_sb[:, kc * 8:(kc + 1) * 8], xt1[:, :],
                        start=(kc == 0), stop=(kc == NCH - 1)),
                        reads=[winTb, xt1b], writes=[u_psb])
                p.op("v", lambda e, g=g: e.tensor_scalar(
                    u_sb[0:8, g * TG:(g + 1) * TG], u_ps[0:8, :], sm[0:8, 0:1],
                    None, ALU.add),
                    reads=[u_psb, smb], writes=[u_b])
            p.op("c", lambda e: e.activation(sm[0:8, 2:3], sm[0:8, 1:2], AF.Sigmoid),
                 reads=[smb], writes=[smb])
            db_sb, db_b = ctx.ssm_db
            p.op("v", lambda e: e.memset(db_sb[0:8, :], 1.0), writes=[db_b])
            p.op("v", lambda e: e.tensor_scalar(db_sb[0:8, :], db_sb[0:8, :],
                                                sm[0:8, 2:3], None, ALU.mult),
                 reads=[smb, db_b], writes=[db_b])
            h_sb, h_b = ctx.ssm_h
            p.op("v", lambda e: e.tensor_tensor_scan(
                h_sb[0:8, :], db_sb[0:8, :], u_sb[0:8, :], 0.0, ALU.mult, ALU.add),
                reads=[db_b, u_b], writes=[h_b])
            p.dma("g", u_o, u_sb[0:8, :], reads=[u_b])
            p.dma("g", hend_o, h_sb[0:8, T - 1:T], reads=[h_b])
            p.finish()
            p.materialize(nc, block, stack)
    return nc


def build_ssm_b():
    nc = bass.Bass("TRN2", target_bir_lowering=False)
    xT = nc.dram_tensor("xT", [D, T], F32, kind="ExternalInput").ap()
    yT = nc.dram_tensor("yT", [D, T], F32, kind="ExternalOutput").ap()
    u_d = nc.dram_tensor("u_d", [8, T], F32, kind="ExternalInput").ap()
    hends_d = nc.dram_tensor("hends_d", [8, 4], F32, kind="ExternalInput").ap()
    expv_d = nc.dram_tensor("expv_d", [8, 4], F32, kind="ExternalInput").ap()
    iota_d = nc.dram_tensor("iota_d", [8, T], F32, kind="ExternalInput").ap()
    winT = nc.dram_tensor("winT", [D, 8], F32, kind="ExternalInput").ap()
    b_in_d = nc.dram_tensor("b_in_d", [8, 1], F32, kind="ExternalInput").ap()
    dl_d = nc.dram_tensor("dl_d", [8, 1], F32, kind="ExternalInput").ap()
    wout_d = nc.dram_tensor("wout_d", [8, D], F32, kind="ExternalInput").ap()
    bout_d = nc.dram_tensor("bout_d", [128, NCH], F32, kind="ExternalInput").ap()
    dskip_d = nc.dram_tensor("dskip_d", [128, NCH], F32, kind="ExternalInput").ap()
    wdn_d = nc.dram_tensor("wdn_d", [D, 32], F32, kind="ExternalInput").ap()
    bdn_d = nc.dram_tensor("bdn_d", [32, 1], F32, kind="ExternalInput").ap()
    wup_d = nc.dram_tensor("wup_d", [32, D], F32, kind="ExternalInput").ap()
    bup_d = nc.dram_tensor("bup_d", [128, NCH], F32, kind="ExternalInput").ap()
    gam_d = nc.dram_tensor("gam_d", [128, NCH], F32, kind="ExternalInput").ap()
    bet_d = nc.dram_tensor("bet_d", [128, NCH], F32, kind="ExternalInput").ap()
    ones_col = nc.dram_tensor("ones_col", [128, 1], F32, kind="ExternalInput").ap()
    ones_k1 = nc.dram_tensor("ones_k1", [1, 128], F32, kind="ExternalInput").ap()
    ident = nc.dram_tensor("ident", [128, 128], F32, kind="ExternalInput").ap()
    dram = {"ones_col": ones_col, "ones_k1": ones_k1, "ident": ident}

    p = Prog()
    with contextlib.ExitStack() as stack:
        ctx = Ctx(nc, stack, p)
        consts = make_consts(ctx, p, dram)
        ctx.consts_ = consts
        ln = LNBlock(ctx)
        _ssm_common_tensors(ctx, nc)
        ctx.ssm_gb2 = Buf("gam")
        ctx.ssm_btb2 = Buf("bet")
        ctx.ssm_gb = (ctx.ssm_g[0], ctx.ssm_gb2)
        ctx.ssm_btb = (ctx.ssm_bt[0], ctx.ssm_btb2)
        y_bf, ybufs = ctx.ssm_y
        rows, rowsb = ctx.ssm_rows
        u_sb, u_b = ctx.ssm_u
        sm, smb = ctx.ssm_small
        winT_sb, winTb = ctx.ssm_winT
        bcols, bcolsb = ctx.sbuf([128, 4 * NCH], F32, "ssm_bcols")
        wo_sb, wo_b = ctx.sbuf([8, D], BF16, "ssm_wo")
        wd_sb, wd_b = ctx.sbuf([128, NCH * 32], BF16, "ssm_wd")
        wu_sb, wu_b = ctx.sbuf([32, D], BF16, "ssm_wu")
        bdn_sb, bdn_b = ctx.sbuf([32, 1], F32, "ssm_bdn")
        hf_sb, hf_b = ctx.sbuf([32, TG], BF16, "ssm_hf")
        he_sb, he_b = ctx.sbuf([8, 8], F32, "ssm_he")
        h_bf, h_bfb = ctx.sbuf([8, T], BF16, "ssm_hbf")
        xt_f = [ctx.sbuf([128, TG], F32, f"ssm_xf{i}") for i in range(2)]

        p.dma("s", ctx.ssm_g[0][:, :], gam_d, writes=[ctx.ssm_gb2])
        p.dma("s", ctx.ssm_bt[0][:, :], bet_d, writes=[ctx.ssm_btb2])
        p.dma("s", sm[0:8, 0:1], b_in_d, writes=[smb])
        p.dma("s", sm[0:8, 1:2], dl_d, writes=[smb])
        p.dma("s", u_sb[0:8, :], u_d, writes=[u_b])
        p.dma("s", he_sb[0:8, 0:4], hends_d, writes=[he_b])
        p.dma("s", he_sb[0:8, 4:8], expv_d, writes=[he_b])
        iota_sb, iota_b = ctx.sbuf([8, T], F32, "ssm_iota")
        p.dma("s", iota_sb[0:8, :], iota_d, writes=[iota_b])
        p.dma("g", wo_sb[0:8, :], wout_d, writes=[wo_b])
        p.dma("g", wd_sb[:, :].rearrange("p (kc f) -> p kc f", kc=NCH),
              wdn_d.rearrange("(kc p) f -> p kc f", p=128), writes=[wd_b])
        p.dma("g", wu_sb[0:32, :], wup_d, writes=[wu_b])
        p.dma("s", bdn_sb[0:32, :], bdn_d, writes=[bdn_b])
        p.dma("s", bcols[:, 0:NCH], bout_d, writes=[bcolsb])
        p.dma("s", bcols[:, NCH:2 * NCH], dskip_d, writes=[bcolsb])
        p.dma("s", bcols[:, 2 * NCH:3 * NCH], bup_d, writes=[bcolsb])

        with nc.Block() as block:
            p.op("c", lambda e: e.activation(sm[0:8, 2:3], sm[0:8, 1:2], AF.Sigmoid),
                 reads=[smb], writes=[smb])
            p.op("c", lambda e: e.activation(sm[0:8, 3:4], sm[0:8, 2:3], AF.Ln),
                 reads=[smb], writes=[smb])
            db_sb, db_b = ctx.ssm_db
            p.op("v", lambda e: e.memset(db_sb[0:8, :], 1.0), writes=[db_b])
            p.op("v", lambda e: e.tensor_scalar(db_sb[0:8, :], db_sb[0:8, :],
                                                sm[0:8, 2:3], None, ALU.mult),
                 reads=[smb, db_b], writes=[db_b])
            h_sb, h_b = ctx.ssm_h
            p.op("v", lambda e: e.tensor_tensor_scan(
                h_sb[0:8, :], db_sb[0:8, :], u_sb[0:8, :], 0.0, ALU.mult, ALU.add),
                reads=[db_b, u_b], writes=[h_b])
            p.op("v", lambda e: e.tensor_scalar(he_sb[0:8, 4:8], he_sb[0:8, 4:8],
                                                sm[0:8, 3:4], None, ALU.mult),
                 reads=[smb, he_b], writes=[he_b])
            p.op("c", lambda e: e.activation(he_sb[0:8, 4:8], he_sb[0:8, 4:8], AF.Exp),
                 reads=[he_b], writes=[he_b])
            p.op("v", lambda e: e.tensor_mul(he_sb[0:8, 4:8], he_sb[0:8, 4:8],
                                             he_sb[0:8, 0:4]),
                 reads=[he_b], writes=[he_b])
            p.op("v", lambda e: e.reduce_sum(sm[0:8, 4:5], he_sb[0:8, 4:8], AX.X),
                 reads=[he_b], writes=[smb])
            p.op("v", lambda e: e.tensor_scalar(iota_sb[0:8, :], iota_sb[0:8, :],
                                                sm[0:8, 3:4], None, ALU.mult),
                 reads=[smb, iota_b], writes=[iota_b])
            p.op("c", lambda e: e.activation(iota_sb[0:8, :], iota_sb[0:8, :], AF.Exp),
                 reads=[iota_b], writes=[iota_b])
            p.op("v", lambda e: e.tensor_scalar(iota_sb[0:8, :], iota_sb[0:8, :],
                                                sm[0:8, 4:5], None, ALU.mult),
                 reads=[smb, iota_b], writes=[iota_b])
            p.op("v", lambda e: e.tensor_add(h_sb[0:8, :], h_sb[0:8, :], iota_sb[0:8, :]),
                 reads=[iota_b, h_b], writes=[h_b])
            p.op("v", lambda e: e.tensor_copy(h_bf[0:8, :], h_sb[0:8, :]),
                 reads=[h_b], writes=[h_bfb])
            _emit_ln0_full(ctx, consts, ln, xT, y_bf, ybufs)
            for g in range(NTG):
                _emit_ln_rows_from_sbuf(ctx, consts, y_bf, ybufs, g, rows, rowsb, 0)
            ar_ps = [(ctx.pt[4], ctx.pb[4]), (ctx.pt[5], ctx.pb[5])]
            for g in range(NTG):
                _bcast_rows(ctx, consts, rows, rowsb, 0, g,
                            (ctx.pt[2], ctx.pb[2]), (ctx.pt[3], ctx.pb[3]))
                for fc in range(NCH):
                    aps, apsb = ar_ps[fc % 2]
                    p.op("t", lambda e, fc=fc, g=g, aps=aps: e.matmul(
                        aps[:, :], wo_sb[0:8, fc * 128:(fc + 1) * 128],
                        h_bf[0:8, g * TG:(g + 1) * TG], start=True, stop=True),
                        reads=[wo_b, h_bfb], writes=[apsb])
                    xt1, xt1b = ctx.ssm_xt1[fc % 2]
                    _emit_xn1_tile(ctx, consts, fc, g, (ctx.pt[2], ctx.pb[2]),
                                   (ctx.pt[3], ctx.pb[3]), xt1[:, :], xt1b,
                                   y_bf, ybufs)
                    t0, t0b = ctx.ssm_t32[2]
                    p.op("v", lambda e, fc=fc, xt1=xt1, aps=aps, t0=t0: e.tensor_scalar(
                        t0[:, :], aps[:, :], bcols[:, fc:fc + 1], None, ALU.add),
                        reads=[apsb, bcolsb], writes=[t0b])
                    p.op("v", lambda e, fc=fc, xt1=xt1, t0=t0: e.scalar_tensor_tensor(
                        t0[:, :], xt1[:, :], 1.0, t0[:, :], ALU.bypass, ALU.add)
                        if False else e.tensor_scalar(
                        xt1[:, :], xt1[:, :], bcols[:, NCH + fc:NCH + fc + 1],
                        None, ALU.mult),
                        reads=[xt1b, bcolsb], writes=[xt1b])
                    p.op("v", lambda e, t0=t0, xt1=xt1: e.tensor_add(
                        t0[:, :], t0[:, :], xt1[:, :]),
                        reads=[t0b, xt1b], writes=[t0b])
                    p.op("v", lambda e, fc=fc, g=g, t0=t0: e.tensor_add(
                        y_bf[:, fc * T + g * TG: fc * T + g * TG + TG],
                        y_bf[:, fc * T + g * TG: fc * T + g * TG + TG], t0[:, :]),
                        reads=[t0b, ybufs[fc][g]], writes=[ybufs[fc][g]])
            for g in range(NTG):
                _emit_ln_rows_from_sbuf(ctx, consts, y_bf, ybufs, g, rows, rowsb, 2)
            for g in range(NTG):
                _bcast_rows(ctx, consts, rows, rowsb, 2, g,
                            (ctx.pt[2], ctx.pb[2]), (ctx.pt[3], ctx.pb[3]))
                hf_ps, hf_psb = ctx.pt[6], ctx.pb[6]
                for kc in range(NCH):
                    xt1, xt1b = ctx.ssm_xt1[kc % 2]
                    _emit_xn1_tile(ctx, consts, kc, g, (ctx.pt[2], ctx.pb[2]),
                                   (ctx.pt[3], ctx.pb[3]), xt1[:, :], xt1b,
                                   y_bf, ybufs)
                    p.op("t", lambda e, kc=kc, xt1=xt1: e.matmul(
                        hf_ps[0:32, :], wd_sb[:, kc * 32:(kc + 1) * 32], xt1[:, :],
                        start=(kc == 0), stop=(kc == NCH - 1)),
                        reads=[wd_b, xt1b], writes=[hf_psb])
                p.op("c", lambda e: e.activation(hf_sb[0:32, :], hf_ps[0:32, :],
                                                 AF.Relu, bias=bdn_sb[0:32, 0:1]),
                     reads=[hf_psb, bdn_b], writes=[hf_b])
                up_ps = [(ctx.pt[4], ctx.pb[4]), (ctx.pt[5], ctx.pb[5])]
                for fc in range(NCH):
                    ups, upsb = up_ps[fc % 2]
                    p.op("t", lambda e, fc=fc, ups=ups: e.matmul(
                        ups[:, :], wu_sb[0:32, fc * 128:(fc + 1) * 128], hf_sb[0:32, :],
                        start=True, stop=True),
                        reads=[wu_b, hf_b], writes=[upsb])
                    xf, xfb = xt_f[fc % 2]
                    p.dma("s", xf[:, :], xT[fc * 128:(fc + 1) * 128,
                                            g * TG:(g + 1) * TG], writes=[xfb])
                    t0, t0b = ctx.ssm_t32[2]
                    p.op("v", lambda e, fc=fc, ups=ups, t0=t0: e.tensor_scalar(
                        t0[:, :], ups[:, :], bcols[:, 2 * NCH + fc:2 * NCH + fc + 1],
                        None, ALU.add),
                        reads=[upsb, bcolsb], writes=[t0b])
                    p.op("v", lambda e, fc=fc, g=g, t0=t0: e.tensor_add(
                        t0[:, :], t0[:, :],
                        y_bf[:, fc * T + g * TG: fc * T + g * TG + TG]),
                        reads=[t0b, ybufs[fc][g]], writes=[t0b])
                    p.op("v", lambda e, xf=xf, t0=t0: e.tensor_add(
                        xf[:, :], xf[:, :], t0[:, :]),
                        reads=[t0b, xfb], writes=[xfb])
                    p.dma("s", yT[fc * 128:(fc + 1) * 128, g * TG:(g + 1) * TG],
                          xf[:, :], reads=[xfb])
            p.finish()
            p.materialize(nc, block, stack)
    return nc


def run_ssm(xTs, ins):
    w_a = {
        "winT": _np(ins["L1_attn_a_W_in"]),
        "b_in_d": _np(np.asarray(ins["L1_attn_a_b_in"]).reshape(8, 1)),
        "dl_d": _np(np.asarray(ins["L1_attn_a_decay_logit"]).reshape(8, 1)),
        "gam_d": _np(np.asarray(ins["L1_attn_a_ln_gamma"]).reshape(NCH, 128).T),
        "bet_d": _np(np.asarray(ins["L1_attn_a_ln_beta"]).reshape(NCH, 128).T),
    }
    cn = _consts_np()
    nc = _get_nc("ssm_a", build_ssm_a)
    in_maps = [{"xT": xTs[c], **w_a, **cn} for c in range(NCORES)]
    res = run_bass_kernel_spmd(nc, in_maps, core_ids=list(range(NCORES)))
    pre = [res.results[c] for c in range(NCORES)]
    hends = [np.concatenate([pre[b * 4 + j]["hend_o"] for j in range(4)], axis=1)
             for b in range(2)]
    iota = np.broadcast_to(np.arange(1, T + 1, dtype=np.float32), (8, T)).copy()
    w_b = {
        "winT": w_a["winT"], "b_in_d": w_a["b_in_d"], "dl_d": w_a["dl_d"],
        "gam_d": w_a["gam_d"], "bet_d": w_a["bet_d"],
        "wout_d": _np(ins["L1_attn_a_W_out"]),
        "bout_d": _np(np.asarray(ins["L1_attn_a_b_out"]).reshape(NCH, 128).T),
        "dskip_d": _np(np.asarray(ins["L1_attn_a_D_skip"]).reshape(NCH, 128).T),
        "wdn_d": _np(ins["L1_attn_a_W_down"]),
        "bdn_d": _np(np.asarray(ins["L1_attn_a_b_down"]).reshape(32, 1)),
        "wup_d": _np(ins["L1_attn_a_W_up"]),
        "bup_d": _np(np.asarray(ins["L1_attn_a_b_up"]).reshape(NCH, 128).T),
        "iota_d": iota,
    }
    nc2 = _get_nc("ssm_b", build_ssm_b)
    in_maps2 = []
    for c in range(NCORES):
        b, ci = divmod(c, 4)
        expv = np.full((8, 4), 1e9, np.float32)
        for j in range(ci):
            expv[:, j] = float(T * (ci - 1 - j))
        in_maps2.append({"xT": xTs[c], "u_d": pre[c]["u_o"], "hends_d": hends[b],
                         "expv_d": expv, **w_b, **cn})
    res2 = run_bass_kernel_spmd(nc2, in_maps2, core_ids=list(range(NCORES)))
    return [res2.results[c]["yT"] for c in range(NCORES)]



def build_mha_prep():
    nc = bass.Bass("TRN2", target_bir_lowering=False)
    xT = nc.dram_tensor("xT", [D, T], F32, kind="ExternalInput").ap()
    qT_o = nc.dram_tensor("qT_o", [D, T], BF16, kind="ExternalOutput").ap()
    kT_o = nc.dram_tensor("kT_o", [D, T], BF16, kind="ExternalOutput").ap()
    v_o = nc.dram_tensor("v_o", [NH * 128, NCH * 128], BF16, kind="ExternalOutput").ap()
    qn_o = nc.dram_tensor("qn_o", [1, NH], F32, kind="ExternalOutput").ap()
    kn_o = nc.dram_tensor("kn_o", [1, NH], F32, kind="ExternalOutput").ap()
    wqkvT = nc.dram_tensor("wqkvT", [32 * 128, D], F32, kind="ExternalInput").ap()
    wqkvV = nc.dram_tensor("wqkvV", [4 * 128, NCH * 512], F32, kind="ExternalInput").ap()
    qkb = nc.dram_tensor("qkb", [128, 32], F32, kind="ExternalInput").ap()
    vbr = nc.dram_tensor("vbr", [1, D], F32, kind="ExternalInput").ap()
    ones_col = nc.dram_tensor("ones_col", [128, 1], F32, kind="ExternalInput").ap()
    ones_k1 = nc.dram_tensor("ones_k1", [1, 128], F32, kind="ExternalInput").ap()
    ident = nc.dram_tensor("ident", [128, 128], F32, kind="ExternalInput").ap()
    dram = {"ones_col": ones_col, "ones_k1": ones_k1, "ident": ident}

    p = Prog()
    with contextlib.ExitStack() as stack:
        ctx = Ctx(nc, stack, p)
        consts = make_consts(ctx, p, dram)
        ln = LNBlock(ctx)
        xn_sb = ctx.sb([128, NCH * T], BF16, "xn")
        xnb = [[Buf(f"xn{c}_{g}") for g in range(NTG)] for c in range(NCH)]
        ctx.ns_sq = ctx.sbuf([128, TG], BF16, "ns_sq")
        ctx.ns_tmp = ctx.sbuf([1, 2], F32, "ns_tmp")
        qn_sb, qnb = ctx.sbuf([1, NH], F32, "qn_sb")
        kn_sb, knb = ctx.sbuf([1, NH], F32, "kn_sb")
        wt_s = [ctx.sbuf([128, NCH * 128], BF16, f"mqk{i}") for i in range(2)]
        wv_s = [ctx.sbuf([128, NCH * 512], BF16, f"mwv{i}") for i in range(2)]
        qkb_sb, qkbb = ctx.sbuf([128, 32], F32, "qkb_sb")
        vb_bc, vbcb = ctx.sbuf([128, 4 * 512], F32, "vb_bc")
        otile = [ctx.sbuf([128, TG], BF16, f"motile{i}") for i in range(2)]

        p.dma("s", qkb_sb[:, :], qkb, writes=[qkbb])

        with nc.Block() as block:
            for g in range(NTG):
                ln.emit(xT, g, xn_sb, [xnb[c][g] for c in range(NCH)], consts,
                        (ctx.pt[0], ctx.pb[0]), (ctx.pt[1], ctx.pb[1]),
                        (ctx.pt[2], ctx.pb[2]), (ctx.pt[3], ctx.pb[3]),
                        xn_off=g * TG, xn_stride=T)
            vbr_sb, vbrb = ctx.sbuf([1, D], F32, "vbr_sb")
            p.dma("s", vbr_sb[:, :], vbr, writes=[vbrb])
            bb_ps, bb_psb = ctx.pt[6], ctx.pb[6]
            for vfg in range(4):
                p.op("t", lambda e, vfg=vfg: e.matmul(
                    bb_ps[:, :], consts["ones_k1"][0][:, :],
                    vbr_sb[0:1, vfg * 512:(vfg + 1) * 512], start=True, stop=True),
                    reads=[vbrb, consts["ones_k1"][1]], writes=[bb_psb])
                p.op("v", lambda e, vfg=vfg: e.tensor_copy(
                    vb_bc[:, vfg * 512:(vfg + 1) * 512], bb_ps[:, :]),
                    reads=[bb_psb], writes=[vbcb])
            q_ps = [(ctx.pt[0], ctx.pb[0]), (ctx.pt[1], ctx.pb[1])]
            for qk in range(2):
                for h in range(NH):
                    wt, wtb = wt_s[h % 2]
                    row = (qk * NH + h) * 128
                    p.dma("g", wt[:, :], wqkvT[row:row + 128, :], writes=[wtb])
                    for g in range(NTG):
                        qp, qpb = q_ps[g % 2]
                        for kc in range(NCH):
                            p.op("t", lambda e, kc=kc, wt=wt, qp=qp, g=g: e.matmul(
                                qp[:, :], wt[:, kc * 128:(kc + 1) * 128],
                                xn_sb[:, kc * T + g * TG: kc * T + g * TG + TG],
                                start=(kc == 0), stop=(kc == NCH - 1)),
                                reads=[wtb, xnb[kc][g]], writes=[qpb])
                        ot, otb = otile[g % 2]
                        bcol = qk * NH + h
                        p.op("v", lambda e, qp=qp, ot=ot, bcol=bcol: e.tensor_scalar(
                            ot[:, :], qp[:, :], qkb_sb[:, bcol:bcol + 1], None, ALU.add),
                            reads=[qpb, qkbb], writes=[otb])
                        nrow, nrowb2 = (qn_sb, qnb) if qk == 0 else (kn_sb, knb)
                        emit_norm_stat(ctx, consts, ot[:, :], otb, nrow, nrowb2,
                                       h, g == 0)
                        dst = qT_o if qk == 0 else kT_o
                        p.dma("g", dst[h * 128:(h + 1) * 128, g * TG:(g + 1) * TG],
                              ot[:, :], reads=[otb])
            for vfg in range(4):
                wv, wvb = wv_s[vfg % 2]
                p.dma("g", wv[:, :], wqkvV[vfg * 128:(vfg + 1) * 128, :], writes=[wvb])
                for g in range(NTG):
                    for tsub in range(4):
                        vp, vpb = q_ps[tsub % 2]
                        for kc in range(NCH):
                            p.op("t", lambda e, kc=kc, wv=wv, vp=vp, g=g,
                                 tsub=tsub: e.matmul(
                                vp[:, :],
                                xn_sb[:, kc * T + g * TG + tsub * 128:
                                      kc * T + g * TG + (tsub + 1) * 128],
                                wv[:, kc * 512:(kc + 1) * 512],
                                start=(kc == 0), stop=(kc == NCH - 1)),
                                reads=[wvb, xnb[kc][g]], writes=[vpb])
                        ot, otb = otile[tsub % 2]
                        p.op("v", lambda e, vp=vp, ot=ot, vfg=vfg: e.tensor_add(
                            ot[:, :], vp[:, :], vb_bc[:, vfg * 512:(vfg + 1) * 512]),
                            reads=[vpb, vbcb], writes=[otb])
                        p.dma("g",
                              v_o.rearrange("(h p) (kc j) -> p h kc j", p=128, j=128)
                              [:, vfg * 4:(vfg + 1) * 4, g * 4 + tsub, :],
                              ot[:, :].rearrange("p (hs j) -> p hs j", hs=4),
                              reads=[otb])
            p.dma("g", qn_o, qn_sb[:, :], reads=[qnb])
            p.dma("g", kn_o, kn_sb[:, :], reads=[knb])
            p.finish()
            p.materialize(nc, block, stack)
    return nc


def run_l1b(xTs, ins):
    wqkv = np.asarray(ins["L1_attn_b_Wqkv_w"])
    bqkv = np.asarray(ins["L1_attn_b_Wqkv_b"])
    wqkvTf = _np(wqkv.T)
    w = {
        "wqkvT": _rg(wqkvTf[:, :4096], NCH, 32, 128),
        "wqkvV": _rg(wqkvTf[:, 4096:], NCH, 4, 512),
        "qkb": _np(bqkv[:4096].reshape(32, 128).T),
        "vbr": _np(bqkv[4096:].reshape(1, D)),
    }
    cn = _consts_np()
    nc = _get_nc("mha_prep", build_mha_prep)
    in_maps = [{"xT": xTs[c], **w, **cn} for c in range(NCORES)]
    res = run_bass_kernel_spmd(nc, in_maps, core_ids=list(range(NCORES)))
    pre = [res.results[c] for c in range(NCORES)]
    kTfull = [np.concatenate([pre[b * 4 + j]["kT_o"] for j in range(4)], axis=1)
              for b in range(2)]
    vfull = [np.concatenate([pre[b * 4 + j]["v_o"] for j in range(4)], axis=1)
             for b in range(2)]
    knall = [np.concatenate([pre[b * 4 + j]["kn_o"] for j in range(4)], axis=1)
             for b in range(2)]
    owT = _rg(_np(ins["L1_attn_b_proj_w"].T), NCH, 4, 512)
    pbias = _np(np.asarray(ins["L1_attn_b_proj_b"]).reshape(NCH, 128).T)
    nc2 = _get_nc("flash_bias", lambda: build_flash(True))
    in_maps2 = []
    for c in range(NCORES):
        b = c // 4
        in_maps2.append({
            "qT": pre[c]["qT_o"], "kTf": kTfull[b], "vf": vfull[b],
            "qn": pre[c]["qn_o"], "kn_all": knall[b], "owT": owT,
            "pbias": pbias, "yin": xTs[c], **cn,
        })
    res2 = run_bass_kernel_spmd(nc2, in_maps2, core_ids=list(range(NCORES)))
    return [res2.results[c]["yT"] for c in range(NCORES)]



_NC_CACHE = {}

def _get_nc(name, builder):
    if name not in _NC_CACHE:
        _NC_CACHE[name] = builder()
    return _NC_CACHE[name]


def _np(x):
    return np.ascontiguousarray(np.asarray(x), dtype=np.float32)


def _consts_np():
    return {
        "ones_col": np.ones((128, 1), np.float32),
        "ones_k1": np.ones((1, 128), np.float32),
        "ident": np.eye(128, dtype=np.float32),
    }


def run_l0a(xTs, w):
    nc = _get_nc("l0a", build_l0a)
    cn = _consts_np()
    shared = {
        "wqT": w["wqT"], "wgT": w["wgT"], "kTw": w["kTw"], "vW": w["vW"],
        "woutT": w["woutT"], **cn,
    }
    in_maps = [{"xT": xTs[c], **shared} for c in range(NCORES)]
    res = run_bass_kernel_spmd(nc, in_maps, core_ids=list(range(NCORES)))
    return [res.results[c]["yT"] for c in range(NCORES)]


def kernel(**inputs):
    ins = {k: np.asarray(v) for k, v in inputs.items()}
    x = _np(ins["L0_attn_a_x"])
    xTs = []
    for c in range(NCORES):
        b, off = divmod(c, 4)
        xTs.append(np.ascontiguousarray(x[b, off * T:(off + 1) * T, :].T))
    w = {
        "wqT": _rg(_np(ins["L0_attn_a_Wq"].T), NCH, NH, 128),
        "wgT": _rg(_np(ins["L0_attn_a_Wg"].T), NCH, NH, 128),
        "kTw": _np(ins["L0_attn_a_k_weight"].T),
        "vW": _rg(_np(ins["L0_attn_a_v_weight"]), 32, NH, 128),
        "woutT": _rg(_np(ins["L0_attn_a_Wout"].T), NCH, 4, 512),
    }
    xTs = run_l0a(xTs, w)
    f0 = prep_ffn_l0(ins)
    xTs = run_ffn("l0", xTs, f0)
    xTs = run_l0b(xTs, ins)
    xTs = run_ffn("l0", xTs, f0)
    xTs = run_ssm(xTs, ins)
    f1 = prep_ffn_l1(ins)
    xTs = run_ffn("l1", xTs, f1)
    xTs = run_l1b(xTs, ins)
    xTs = run_ffn("l1", xTs, f1)
    out = np.empty((2, S, D), np.float32)
    for c in range(NCORES):
        b, off = divmod(c, 4)
        out[b, off * T:(off + 1) * T, :] = xTs[c].T
    return out


# version 05 (final)
# speedup vs baseline: 1.6973x; 1.6973x over previous
import contextlib

import numpy as np
import ml_dtypes

import concourse.bass as bass
import concourse.mybir as mybir
from concourse.bass_utils import run_bass_kernel_spmd

F32 = mybir.dt.float32
BF16 = mybir.dt.bfloat16
AF = mybir.ActivationFunctionType
ALU = mybir.AluOpType
AX = mybir.AxisListType

NCORES = 8
D = 2048
NCH = 16
T = 2048
NTG = 4
TG = 512
S = 8192
NH = 16
DK = 128
RD2 = DK ** 0.5


class Buf:
    __slots__ = ("last_w", "readers", "name")
    def __init__(self, name=""):
        self.last_w = None
        self.readers = []
        self.name = name


class Prog:
    ENGS = ("t", "v", "c", "g", "s")
    DQS = ("s", "g", "c", "v")

    NSLOT = 6

    def __init__(self):
        self._q = {e: [] for e in self.ENGS}
        self.cnt = {e: 0 for e in self.ENGS}
        self.dcnt = {q: 0 for q in self.DQS}
        self.slotcnt = {q: [0] * self.NSLOT for q in self.DQS}
        self.watermark = {}
        self._pending = {e: [] for e in self.ENGS}

    PARANOID = False

    def _paranoid_need(self, need):
        for e in self.ENGS:
            if self.cnt[e]:
                need["P" + e] = max(need.get("P" + e, 0), self.cnt[e])
        for q in self.DQS:
            for slot in range(self.NSLOT):
                if self.slotcnt[q][slot]:
                    sname = f"D{q}{slot}"
                    need[sname] = max(need.get(sname, 0), 16 * self.slotcnt[q][slot])

    def _need(self, need, tok):
        if tok is None:
            return
        sem, val = tok
        assert val is not None, "dependency on an un-inc'd instruction"
        if need.get(sem, 0) < val:
            need[sem] = val

    def _emit_waits(self, eng, need):
        for sem, val in sorted(need.items()):
            if self.watermark.get((eng, sem), 0) >= val:
                continue
            self._q[eng].append(("wait", sem, val))
            self.watermark[(eng, sem)] = val

    def op(self, eng, fn, reads=(), writes=(), inc=True):
        need = {}
        for b in reads:
            self._need(need, b.last_w)
        for b in writes:
            self._need(need, b.last_w)
            for r in b.readers:
                self._need(need, r)
        if self.PARANOID:
            self._paranoid_need(need)
        self._emit_waits(eng, need)
        self._q[eng].append(("op", fn, inc))
        if inc:
            self.cnt[eng] += 1
            tok = ("P" + eng, self.cnt[eng])
            for b, iswrite in self._pending[eng]:
                if iswrite:
                    b.last_w = tok
                    b.readers = []
                else:
                    b.readers.append(tok)
            self._pending[eng].clear()
            for b in writes:
                b.last_w = tok
                b.readers = []
            for b in reads:
                b.readers.append(tok)
            return tok
        for b in writes:
            self._pending[eng].append((b, True))
        for b in reads:
            self._pending[eng].append((b, False))
        return None

    def dma(self, q, out_ap, in_ap, reads=(), writes=(), **kw):
        need = {}
        for b in reads:
            self._need(need, b.last_w)
        for b in writes:
            self._need(need, b.last_w)
            for r in b.readers:
                self._need(need, r)
        if self.PARANOID:
            self._paranoid_need(need)
        slot = self.dcnt[q] % self.NSLOT
        sname = f"D{q}{slot}"
        if self.slotcnt[q][slot]:
            need[sname] = max(need.get(sname, 0), 16 * self.slotcnt[q][slot])
        self._emit_waits(q, need)
        self._q[q].append(("dma", out_ap, in_ap, sname, kw))
        self.dcnt[q] += 1
        self.slotcnt[q][slot] += 1
        tok = (sname, 16 * self.slotcnt[q][slot])
        for b in writes:
            b.last_w = tok
            b.readers = []
        for b in reads:
            b.readers.append(tok)
        return tok

    def finish(self):
        allneed = {}
        for q in self.DQS:
            qneed = {}
            for slot in range(self.NSLOT):
                if self.slotcnt[q][slot]:
                    qneed[f"D{q}{slot}"] = 16 * self.slotcnt[q][slot]
            self._emit_waits(q, qneed)
            allneed.update(qneed)
        for e in self.ENGS:
            if self.cnt[e]:
                allneed["P" + e] = self.cnt[e]
        self._emit_waits("s", allneed)

    def materialize(self, nc, block, stack):
        sems = {}
        for e in self.ENGS:
            sems["P" + e] = stack.enter_context(nc.semaphore("semP" + e))
        for q in self.DQS:
            for slot in range(self.NSLOT):
                if self.slotcnt[q][slot]:
                    sems[f"D{q}{slot}"] = stack.enter_context(
                        nc.semaphore(f"semD{q}{slot}"))
        attr = {"t": "tensor", "v": "vector", "c": "scalar", "g": "gpsimd", "s": "sync"}
        for e in self.ENGS:
            items = self._q[e]
            psem = sems["P" + e]

            def body(eng, items=items, psem=psem):
                for it in items:
                    if it[0] == "wait":
                        eng.wait_ge(sems[it[1]], it[2])
                    elif it[0] == "op":
                        inst = it[1](eng)
                        if it[2]:
                            inst.then_inc(psem, 1)
                    else:
                        eng.dma_start(out=it[1], in_=it[2], **it[4]).then_inc(sems[it[3]], 16)
            getattr(block, attr[e])(body)


class Ctx:
    def __init__(self, nc, stack, p):
        self.nc = nc
        self.stack = stack
        self.p = p
        self.n_sb = 0
        self.pt = [stack.enter_context(nc.psum_tensor(f"pp{i}", [128, 512], F32))
                   for i in range(7)]
        self.pb = [Buf(f"psum{i}") for i in range(7)]
        self.ptb = stack.enter_context(nc.psum_tensor("ppb", [128, 1024], BF16))
        self.pbb = Buf("psumBF")

    def sb(self, shape, dtype, name=None):
        self.n_sb += 1
        t = self.stack.enter_context(
            self.nc.sbuf_tensor(name or f"sb{self.n_sb}", shape, dtype))
        return t

    def sbuf(self, shape, dtype, name=None):
        t = self.sb(shape, dtype, name)
        return t, Buf(name or f"b{self.n_sb}")


def make_consts(ctx, p, dram):
    c = {}
    t, b = ctx.sbuf([128, 1], F32, "c_ones_col")
    p.dma("s", t[:, :], dram["ones_col"], writes=[b])
    c["ones_col"] = (t, b)
    t, b = ctx.sbuf([128, 1], BF16, "c_ones_col_bf")
    p.dma("g", t[:, :], dram["ones_col"], writes=[b])
    c["ones_col_bf"] = (t, b)
    t, b = ctx.sbuf([1, 128], F32, "c_ones_k1")
    p.dma("s", t[:, :], dram["ones_k1"], writes=[b])
    c["ones_k1"] = (t, b)
    t, b = ctx.sbuf([1, 128], BF16, "c_ones_k1_bf")
    p.dma("g", t[:, :], dram["ones_k1"], writes=[b])
    c["ones_k1_bf"] = (t, b)
    t, b = ctx.sbuf([128, 128], BF16, "c_ident_bf")
    p.dma("g", t[:, :], dram["ident"], writes=[b])
    c["ident_bf"] = (t, b)
    t, b = ctx.sbuf([128, 128], F32, "c_ident_f32")
    p.dma("s", t[:, :], dram["ident"], writes=[b])
    c["ident_f32"] = (t, b)
    return c


class LNBlock:
    def __init__(self, ctx):
        self.ctx = ctx
        self.xt = [ctx.sbuf([128, TG], F32, f"ln_x{i}") for i in range(4)]
        self.sq = [ctx.sbuf([128, TG], BF16, f"ln_sq{i}") for i in range(4)]
        self.rows = ctx.sb([1, 8 * TG], F32, "ln_rows")
        self.rowb = [Buf(f"ln_row{i}") for i in range(8)]
        self.t1 = [ctx.sbuf([128, TG], F32, f"ln_t1{i}") for i in range(2)]

    def emit(self, x_dram, g, xn_sb, xnbufs, consts, ps_sum, ps_sq, ps_a, ps_b,
             eps=1e-5, also_keep_x=None, xn_off=0, xn_stride=TG):
        ctx, p = self.ctx, self.ctx.p
        sum_t, sum_b = ps_sum
        sq_t, sq_b = ps_sq
        a_t, a_b = ps_a
        b_t, b_b = ps_b
        ones_col, ones_col_b = consts["ones_col"]
        ones_col_bf, ones_col_bf_b = consts["ones_col_bf"]
        ones_k1, ones_k1_b = consts["ones_k1"]
        M, E2, SD, R, BR = 0, 1, 2, 3, 4

        def xsrc(c):
            return x_dram[c * 128:(c + 1) * 128, g * TG:(g + 1) * TG]

        keep_sb, keep_bufs = (None, None) if also_keep_x is None else also_keep_x
        for c in range(NCH):
            if keep_sb is not None:
                xt, xb = keep_sb, keep_bufs[c]
                xap = keep_sb[:, c * TG:(c + 1) * TG]
                p.dma("s", xap, xsrc(c), writes=[xb])
            else:
                xt, xb = self.xt[c % 4]
                xap = xt[:, :]
                p.dma("s", xap, xsrc(c), writes=[xb])
            sqt, sqb = self.sq[c % 4]
            p.op("c", lambda e, xap=xap, sqt=sqt: e.activation(sqt[:, :], xap, AF.Square),
                 reads=[xb], writes=[sqb])
            p.op("t", lambda e, xap=xap, c=c: e.matmul(
                sum_t[0:1, :], ones_col[:, :], xap, start=(c == 0), stop=(c == NCH - 1)),
                reads=[xb, ones_col_b], writes=[sum_b], inc=(c % 4 == 3))
            p.op("t", lambda e, sqt=sqt, c=c: e.matmul(
                sq_t[0:1, :], ones_col_bf[:, :], sqt[:, :], start=(c == 0), stop=(c == NCH - 1),
                skip_group_check=True),
                reads=[sqb, ones_col_bf_b], writes=[sq_b], inc=(c % 4 == 3))
        rows, rowb = self.rows, self.rowb
        p.op("v", lambda e: e.tensor_scalar(rows[0:1, M*TG:(M+1)*TG], sum_t[0:1, :], 1.0 / D, None, ALU.mult),
             reads=[sum_b], writes=[rowb[M]])
        p.op("v", lambda e: e.tensor_scalar(rows[0:1, E2*TG:(E2+1)*TG], sq_t[0:1, :], 1.0 / D, None, ALU.mult),
             reads=[sq_b], writes=[rowb[E2]])
        p.op("v", lambda e: e.scalar_tensor_tensor(
            rows[0:1, SD*TG:(SD+1)*TG], rows[0:1, M*TG:(M+1)*TG], -1.0, rows[0:1, M*TG:(M+1)*TG], ALU.mult, ALU.mult),
            reads=[rowb[M]], writes=[rowb[SD]])
        p.op("v", lambda e: e.scalar_tensor_tensor(
            rows[0:1, SD*TG:(SD+1)*TG], rows[0:1, SD*TG:(SD+1)*TG], float(eps),
            rows[0:1, E2*TG:(E2+1)*TG], ALU.add, ALU.add),
            reads=[rowb[SD], rowb[E2]], writes=[rowb[SD]])
        p.op("c", lambda e: e.activation(rows[0:1, SD*TG:(SD+1)*TG], rows[0:1, SD*TG:(SD+1)*TG], AF.Sqrt),
             reads=[rowb[SD]], writes=[rowb[SD]])
        p.op("v", lambda e: e.reciprocal(rows[0:1, R*TG:(R+1)*TG], rows[0:1, SD*TG:(SD+1)*TG]),
             reads=[rowb[SD]], writes=[rowb[R]])
        p.op("v", lambda e: e.scalar_tensor_tensor(
            rows[0:1, BR*TG:(BR+1)*TG], rows[0:1, M*TG:(M+1)*TG], -1.0, rows[0:1, R*TG:(R+1)*TG], ALU.mult, ALU.mult),
            reads=[rowb[M], rowb[R]], writes=[rowb[BR]])
        p.op("t", lambda e: e.matmul(a_t[:, :], ones_k1[:, :], rows[0:1, R*TG:(R+1)*TG],
                                     start=True, stop=True),
             reads=[rowb[R], ones_k1_b], writes=[a_b])
        p.op("t", lambda e: e.matmul(b_t[:, :], ones_k1[:, :], rows[0:1, BR*TG:(BR+1)*TG],
                                     start=True, stop=True),
             reads=[rowb[BR], ones_k1_b], writes=[b_b])
        for c in range(NCH):
            if keep_sb is not None:
                xap = keep_sb[:, c * TG:(c + 1) * TG]
                xb = keep_bufs[c]
            else:
                xt, xb = self.xt[c % 4]
                xap = xt[:, :]
                p.dma("s", xap, xsrc(c), writes=[xb])
            t1t, t1b = self.t1[c % 2]
            p.op("v", lambda e, xap=xap, t1t=t1t: e.tensor_mul(t1t[:, :], xap, a_t[:, :]),
                 reads=[xb, a_b], writes=[t1b])
            p.op("v", lambda e, t1t=t1t, c=c: e.tensor_add(
                xn_sb[:, c * xn_stride + xn_off: c * xn_stride + xn_off + TG],
                t1t[:, :], b_t[:, :]),
                reads=[t1b, b_b], writes=[xnbufs[c]])



def emit_l0a_group(ctx, consts, ln, g, x_dram, y_dram, wq_dram, wg_dram, kT_dram,
                   v_dram, woutT_dram):
    p = ctx.p
    xn_sb, xnb = ctx.l0a_xn
    xk_sb, xkb = ctx.l0a_xk
    qh, qhb = ctx.l0a_qh
    gh, ghb = ctx.l0a_gh
    wqt, wqtb = ctx.l0a_wqt
    wgt, wgtb = ctx.l0a_wgt
    kt = ctx.l0a_kt
    ktb = ctx.l0a_ktb
    vt = ctx.l0a_vt
    vtb = ctx.l0a_vtb
    es = ctx.l0a_es
    og_sb, ogb = ctx.l0a_og
    wot, wotb = ctx.l0a_wot

    ln.emit(x_dram, g, xn_sb, xnb, consts,
            (ctx.pt[0], ctx.pb[0]), (ctx.pt[1], ctx.pb[1]),
            (ctx.pt[2], ctx.pb[2]), (ctx.pt[3], ctx.pb[3]),
            also_keep_x=(xk_sb, xkb))

    e_ps = [(ctx.pt[i], ctx.pb[i]) for i in range(4)]
    q_ps = (ctx.pt[4], ctx.pb[4])
    g_ps = (ctx.pt[5], ctx.pb[5])
    o_ps = (ctx.pt[6], ctx.pb[6])

    for h in range(NH):
        p.dma("g", wqt[:, :], wq_dram[h * 128:(h + 1) * 128, :], writes=[wqtb])
        p.dma("g", wgt[:, :], wg_dram[h * 128:(h + 1) * 128, :], writes=[wgtb])
        p.dma("g", kt[:, :], kT_dram[h * 128:(h + 1) * 128, :], writes=[ktb])
        p.dma("g", vt[:, :], v_dram[h * 128:(h + 1) * 128, :], writes=[vtb])
        qt_, qb_ = q_ps
        for kc in range(NCH):
            p.op("t", lambda e, kc=kc: e.matmul(
                qt_[:, :], wqt[:, kc * 128:(kc + 1) * 128],
                xn_sb[:, kc * TG:(kc + 1) * TG],
                start=(kc == 0), stop=(kc == NCH - 1)),
                reads=[wqtb, xnb[kc]], writes=[qb_], inc=(kc == NCH - 1))
        p.op("c", lambda e: e.activation(qh[:, :], qt_[:, :], AF.Copy),
             reads=[qb_], writes=[qhb])
        gt_, gb_ = g_ps
        for kc in range(NCH):
            p.op("t", lambda e, kc=kc: e.matmul(
                gt_[:, :], wgt[:, kc * 128:(kc + 1) * 128],
                xn_sb[:, kc * TG:(kc + 1) * TG],
                start=(kc == 0), stop=(kc == NCH - 1)),
                reads=[wgtb, xnb[kc]], writes=[gb_], inc=(kc == NCH - 1))
        p.op("c", lambda e: e.activation(gh[:, :], gt_[:, :], AF.Silu),
             reads=[gb_], writes=[ghb])
        ot_, ob_ = o_ps
        for mc in range(32):
            et_, eb_ = e_ps[mc % 4]
            p.op("t", lambda e, mc=mc, et_=et_: e.matmul(
                et_[:, :], kt[:, mc * 128:(mc + 1) * 128], qh[:, :],
                start=True, stop=True),
                reads=[ktb, qhb], writes=[eb_])
            est, esb = es[mc % 4]
            p.op("c", lambda e, et_=et_, est=est: e.activation(
                est[:, :], et_[:, :], AF.Silu, scale=float(RD2)),
                reads=[eb_], writes=[esb])
            p.op("t", lambda e, mc=mc, est=est: e.matmul(
                ot_[:, :], vt[:, mc * 128:(mc + 1) * 128], est[:, :],
                start=(mc == 0), stop=(mc == 31)),
                reads=[vtb, esb], writes=[ob_], inc=(mc % 2 == 1))
        p.op("v", lambda e, h=h: e.scalar_tensor_tensor(
            og_sb[:, h * TG:(h + 1) * TG], ot_[:, :], float(RD2), gh[:, :],
            ALU.mult, ALU.mult),
            reads=[ob_, ghb], writes=[ogb[h]])

    for fcg in range(4):
        p.dma("g", wot[:, :], woutT_dram[fcg * 128:(fcg + 1) * 128, :], writes=[wotb])
        for fc in range(4):
            yt_, yb_ = (ctx.pt[fc], ctx.pb[fc])
            for kc in range(NCH):
                p.op("t", lambda e, kc=kc, fc=fc, yt_=yt_: e.matmul(
                    yt_[:, :], wot[:, kc * 512 + fc * 128: kc * 512 + (fc + 1) * 128],
                    og_sb[:, kc * TG:(kc + 1) * TG],
                    start=(kc == 0), stop=(kc == NCH - 1)),
                    reads=[wotb, ogb[kc]], writes=[yb_], inc=(kc == NCH - 1))
            c = fcg * 4 + fc
            p.op("v", lambda e, c=c, yt_=yt_: e.tensor_add(
                xk_sb[:, c * TG:(c + 1) * TG], xk_sb[:, c * TG:(c + 1) * TG], yt_[:, :]),
                reads=[xkb[c], yb_], writes=[xkb[c]])
            p.dma("s", y_dram[c * 128:(c + 1) * 128, g * TG:(g + 1) * TG],
                  xk_sb[:, c * TG:(c + 1) * TG], reads=[xkb[c]])


def build_l0a():
    nc = bass.Bass("TRN2", target_bir_lowering=False)
    xT = nc.dram_tensor("xT", [D, T], F32, kind="ExternalInput").ap()
    yT = nc.dram_tensor("yT", [D, T], F32, kind="ExternalOutput").ap()
    wqT = nc.dram_tensor("wqT", [NH * 128, D], F32, kind="ExternalInput").ap()
    wgT = nc.dram_tensor("wgT", [NH * 128, D], F32, kind="ExternalInput").ap()
    kTw = nc.dram_tensor("kTw", [D, 4096], F32, kind="ExternalInput").ap()
    vW = nc.dram_tensor("vW", [NH * 128, 32 * 128], F32, kind="ExternalInput").ap()
    woutT = nc.dram_tensor("woutT", [4 * 128, NCH * 512], F32, kind="ExternalInput").ap()
    ones_col = nc.dram_tensor("ones_col", [128, 1], F32, kind="ExternalInput").ap()
    ones_k1 = nc.dram_tensor("ones_k1", [1, 128], F32, kind="ExternalInput").ap()
    ident = nc.dram_tensor("ident", [128, 128], F32, kind="ExternalInput").ap()
    dram = {"ones_col": ones_col, "ones_k1": ones_k1, "ident": ident}

    p = Prog()
    with contextlib.ExitStack() as stack:
        ctx = Ctx(nc, stack, p)
        consts = make_consts(ctx, p, dram)
        ln = LNBlock(ctx)
        ctx.l0a_xn = (ctx.sb([128, NCH * TG], BF16, "xn"), [Buf(f"xn{c}") for c in range(NCH)])
        ctx.l0a_xk = (ctx.sb([128, NCH * TG], F32, "xk"), [Buf(f"xk{c}") for c in range(NCH)])
        ctx.l0a_qh = ctx.sbuf([128, TG], BF16, "qh")
        ctx.l0a_gh = ctx.sbuf([128, TG], BF16, "gh")
        ctx.l0a_wqt = ctx.sbuf([128, NCH * 128], BF16, "wqt")
        ctx.l0a_wgt = ctx.sbuf([128, NCH * 128], BF16, "wgt")
        kt, ktb = ctx.sbuf([128, 4096], BF16, "kt")
        ctx.l0a_kt, ctx.l0a_ktb = kt, ktb
        vt, vtb = ctx.sbuf([128, 4096], BF16, "vt")
        ctx.l0a_vt, ctx.l0a_vtb = vt, vtb
        ctx.l0a_es = [ctx.sbuf([128, TG], BF16, f"es{i}") for i in range(4)]
        ctx.l0a_og = (ctx.sb([128, NCH * TG], BF16, "og"), [Buf(f"og{c}") for c in range(NCH)])
        ctx.l0a_wot = ctx.sbuf([128, NCH * 512], BF16, "wot")

        with nc.Block() as block:
            for g in range(NTG):
                emit_l0a_group(ctx, consts, ln, g, xT, yT, wqT, wgT, kTw, vW, woutT)
            p.finish()
            p.materialize(nc, block, stack)
    return nc




def emit_gating_l0(ctx, consts, g, W, xn_sb, xnb, combT, combTb, srow, srowb,
                   gwt, gwtb, segt, segtb):
    p = ctx.p
    sc_ps, sc_psb = ctx.pt[4], ctx.pb[4]
    tr_ps, tr_psb = ctx.pt[5], ctx.pb[5]
    sg_ps, sg_psb = ctx.pt[3], ctx.pb[3]
    ident_f32, ident_f32b = consts["ident_f32"]
    sc_sb, sc_sbb = ctx.ffn_sc
    st_sb, st_sbb = ctx.ffn_st
    eq1, eq1b = ctx.ffn_eq1
    eq2, eq2b = ctx.ffn_eq2
    smk, smkb = ctx.ffn_smk
    ct, ctb = ctx.ffn_ct
    stats, statsb = ctx.ffn_stats
    for kc in range(NCH):
        p.op("t", lambda e, kc=kc: e.matmul(
            sc_ps[0:8, :], gwt[:, kc * 8:(kc + 1) * 8],
            xn_sb[:, kc * T + g * TG: kc * T + g * TG + TG],
            start=(kc == 0), stop=(kc == NCH - 1)),
            reads=[gwtb, xnb[kc][g]], writes=[sc_psb], inc=(kc == NCH - 1))
        p.op("t", lambda e, kc=kc: e.matmul(
            sg_ps[0:1, :], segt[:, kc:kc + 1],
            xn_sb[:, kc * T + g * TG: kc * T + g * TG + TG],
            start=(kc == 0), stop=(kc == NCH - 1), skip_group_check=True),
            reads=[segtb, xnb[kc][g]], writes=[sg_psb], inc=(kc == NCH - 1))
    p.op("c", lambda e: e.activation(sc_sb[:, :], sc_ps[0:8, :], AF.Copy),
         reads=[sc_psb], writes=[sc_sbb])
    p.op("c", lambda e, g=g: e.activation(srow[0:1, g * TG:(g + 1) * TG],
                                          sg_ps[0:1, :], AF.Sigmoid),
         reads=[sg_psb], writes=[srowb])
    for j in range(4):
        p.op("t", lambda e, j=j: e.transpose(
            tr_ps[:, 0:8], sc_sb[0:8, j * 128:(j + 1) * 128], ident_f32[0:8, 0:8]),
            reads=[sc_sbb, ident_f32b], writes=[tr_psb])
        p.op("v", lambda e: e.tensor_copy(st_sb[:, :], tr_ps[:, 0:8]),
             reads=[tr_psb], writes=[st_sbb])
        p.op("v", lambda e: e.reduce_max(stats[:, 0:1], st_sb[:, :], AX.X),
             reads=[st_sbb], writes=[statsb])
        p.op("v", lambda e: e.tensor_scalar(eq1[:, :], st_sb[:, :], stats[:, 0:1],
                                            None, ALU.is_equal),
             reads=[st_sbb, statsb], writes=[eq1b])
        p.op("v", lambda e: e.scalar_tensor_tensor(
            smk[:, :], eq1[:, :], -1e30, st_sb[:, :], ALU.mult, ALU.add),
            reads=[eq1b, st_sbb], writes=[smkb])
        p.op("v", lambda e: e.reduce_max(stats[:, 1:2], smk[:, :], AX.X),
             reads=[smkb, statsb], writes=[statsb])
        p.op("v", lambda e: e.tensor_sub(stats[:, 2:3], stats[:, 0:1], stats[:, 1:2]),
             reads=[statsb], writes=[statsb])
        p.op("c", lambda e: e.activation(stats[:, 3:4], stats[:, 2:3], AF.Sigmoid),
             reads=[statsb], writes=[statsb])
        p.op("c", lambda e: e.activation(stats[:, 4:5], stats[:, 2:3], AF.Sigmoid,
                                         scale=-1.0),
             reads=[statsb], writes=[statsb])
        p.op("v", lambda e: e.tensor_scalar(eq2[:, :], smk[:, :], stats[:, 1:2],
                                            None, ALU.is_equal),
             reads=[smkb, statsb], writes=[eq2b])
        p.op("v", lambda e: e.tensor_scalar(eq1[:, :], eq1[:, :], stats[:, 3:4],
                                            None, ALU.mult),
             reads=[eq1b, statsb], writes=[eq1b])
        p.op("v", lambda e: e.tensor_scalar(eq2[:, :], eq2[:, :], stats[:, 4:5],
                                            None, ALU.mult),
             reads=[eq2b, statsb], writes=[eq2b])
        p.op("v", lambda e: e.tensor_add(ct[:, :], eq1[:, :], eq2[:, :]),
             reads=[eq1b, eq2b], writes=[ctb])
        p.op("t", lambda e, j=j: e.transpose(
            tr_ps[0:8, 128:256], ct[:, :], ident_f32[:, :]),
            reads=[ctb, ident_f32b], writes=[tr_psb])
        p.op("c", lambda e, j=j, g=g: e.activation(
            combT[0:8, g * TG + j * 128: g * TG + (j + 1) * 128],
            tr_ps[0:8, 128:256], AF.Copy),
            reads=[tr_psb], writes=[combTb])


def emit_gating_l1(ctx, consts, g, W, xn_sb, xnb, combT, combTb, gwt, gwtb):
    p = ctx.p
    sc_ps, sc_psb = ctx.pt[4], ctx.pb[4]
    tr_ps, tr_psb = ctx.pt[5], ctx.pb[5]
    ident_f32, ident_f32b = consts["ident_f32"]
    sc_sb, sc_sbb = ctx.ffn_sc
    st_sb, st_sbb = ctx.ffn_st
    eq1, eq1b = ctx.ffn_eq1
    eq2, eq2b = ctx.ffn_eq2
    smk, smkb = ctx.ffn_smk
    ct, ctb = ctx.ffn_ct
    stats, statsb = ctx.ffn_stats
    for kc in range(NCH):
        p.op("t", lambda e, kc=kc: e.matmul(
            sc_ps[0:8, :], gwt[:, kc * 8:(kc + 1) * 8],
            xn_sb[:, kc * T + g * TG: kc * T + g * TG + TG],
            start=(kc == 0), stop=(kc == NCH - 1)),
            reads=[gwtb, xnb[kc][g]], writes=[sc_psb], inc=(kc == NCH - 1))
    p.op("c", lambda e: e.activation(sc_sb[:, :], sc_ps[0:8, :], AF.Copy),
         reads=[sc_psb], writes=[sc_sbb])
    for j in range(4):
        p.op("t", lambda e, j=j: e.transpose(
            tr_ps[:, 0:8], sc_sb[0:8, j * 128:(j + 1) * 128], ident_f32[0:8, 0:8]),
            reads=[sc_sbb, ident_f32b], writes=[tr_psb])
        p.op("v", lambda e: e.tensor_copy(st_sb[:, :], tr_ps[:, 0:8]),
             reads=[tr_psb], writes=[st_sbb])
        p.op("v", lambda e: e.reduce_max(stats[:, 0:1], st_sb[:, :], AX.X),
             reads=[st_sbb], writes=[statsb])
        p.op("v", lambda e: e.tensor_scalar(stats[:, 1:2], stats[:, 0:1], -1.0,
                                            None, ALU.mult),
             reads=[statsb], writes=[statsb])
        p.op("c", lambda e: e.activation(st_sb[:, :], st_sb[:, :], AF.Exp,
                                         bias=stats[:, 1:2]),
             reads=[st_sbb, statsb], writes=[st_sbb])
        p.op("v", lambda e: e.reduce_sum(stats[:, 2:3], st_sb[:, :], AX.X),
             reads=[st_sbb, statsb], writes=[statsb])
        p.op("v", lambda e: e.reciprocal(stats[:, 3:4], stats[:, 2:3]),
             reads=[statsb], writes=[statsb])
        p.op("v", lambda e: e.tensor_scalar(st_sb[:, :], st_sb[:, :], stats[:, 3:4],
                                            None, ALU.mult),
             reads=[st_sbb, statsb], writes=[st_sbb])
        p.op("v", lambda e: e.reduce_max(stats[:, 4:5], st_sb[:, :], AX.X),
             reads=[st_sbb, statsb], writes=[statsb])
        p.op("v", lambda e: e.tensor_scalar(eq1[:, :], st_sb[:, :], stats[:, 4:5],
                                            None, ALU.is_equal),
             reads=[st_sbb, statsb], writes=[eq1b])
        p.op("v", lambda e: e.scalar_tensor_tensor(
            smk[:, :], eq1[:, :], -1e30, st_sb[:, :], ALU.mult, ALU.add),
            reads=[eq1b, st_sbb], writes=[smkb])
        p.op("v", lambda e: e.reduce_max(stats[:, 5:6], smk[:, :], AX.X),
             reads=[smkb, statsb], writes=[statsb])
        p.op("v", lambda e: e.tensor_scalar(eq2[:, :], smk[:, :], stats[:, 5:6],
                                            None, ALU.is_equal),
             reads=[smkb, statsb], writes=[eq2b])
        p.op("v", lambda e: e.tensor_add(eq1[:, :], eq1[:, :], eq2[:, :]),
             reads=[eq1b, eq2b], writes=[eq1b])
        p.op("v", lambda e: e.tensor_mul(ct[:, :], st_sb[:, :], eq1[:, :]),
             reads=[st_sbb, eq1b], writes=[ctb])
        p.op("t", lambda e, j=j: e.transpose(
            tr_ps[0:8, 128:256], ct[:, :], ident_f32[:, :]),
            reads=[ctb, ident_f32b], writes=[tr_psb])
        p.op("c", lambda e, j=j, g=g: e.activation(
            combT[0:8, g * TG + j * 128: g * TG + (j + 1) * 128],
            tr_ps[0:8, 128:256], AF.Copy),
            reads=[tr_psb], writes=[combTb])


def emit_ffn(ctx, consts, ln, x_dram, y_dram, W, nh_ch, nhs_ch, kind):
    p = ctx.p
    nc = ctx.nc
    xn_sb, xnb = ctx.ffn_xn
    combT, combTb = ctx.ffn_combT
    sel, selb = ctx.ffn_sel
    gwt, gwtb = ctx.ffn_gwt
    wgt = ctx.ffn_wgt
    wut = ctx.ffn_wut
    wdt = ctx.ffn_wdt
    x1t = ctx.ffn_x1t
    m_sb, m_b = ctx.ffn_m
    cbc, cbcb = ctx.ffn_cbc
    dt_sb = ctx.ffn_dt
    ones_k1_bf, ones_k1_bfb = consts["ones_k1_bf"]

    has_sgate = (kind == "l0")
    if has_sgate:
        srow, srowb = ctx.ffn_srow
        segt, segtb = ctx.ffn_segt

    p.dma("g", gwt[:, :].rearrange("p (kc f) -> p kc f", kc=NCH),
          W["gwT"].rearrange("(kc p) f -> p kc f", p=128), writes=[gwtb])
    if has_sgate:
        p.dma("g", segt[:, :], W["segT"], writes=[segtb])
    p.dma("g", sel[:, :], W["sel"], writes=[selb])
    for g in range(NTG):
        ln.emit(x_dram, g, xn_sb, [xnb[c][g] for c in range(NCH)], consts,
                (ctx.pt[0], ctx.pb[0]), (ctx.pt[1], ctx.pb[1]),
                (ctx.pt[2], ctx.pb[2]), (ctx.pt[3], ctx.pb[3]),
                xn_off=g * TG, xn_stride=T)
        if kind == "l0":
            emit_gating_l0(ctx, consts, g, W, xn_sb, xnb, combT, combTb,
                           srow, srowb, gwt, gwtb, segt, segtb)
        else:
            emit_gating_l1(ctx, consts, g, W, xn_sb, xnb, combT, combTb, gwt, gwtb)

    ybufs = [[Buf(f"y{fc}_{ts}") for ts in range(4)] for fc in range(NCH)]
    p.dma("s", y_dram, ctx.ffn_yin, writes=[b for row in ybufs for b in row])

    x1_ps = [(ctx.pt[0], ctx.pb[0]), (ctx.pt[1], ctx.pb[1])]
    x3_ps = [(ctx.pt[2], ctx.pb[2]), (ctx.pt[3], ctx.pb[3])]
    dn_ps = [(ctx.pt[4], ctx.pb[4]), (ctx.pt[5], ctx.pb[5])]
    cb_ps = (ctx.pt[6], ctx.pb[6])

    def expert_pass(nch_h, load_g, load_u, load_d, scale_mode, e_tag):
        if scale_mode is not None:
            cbt, cbb = cb_ps
            for ts in range(4):
                if scale_mode[0] == "sel":
                    eidx = scale_mode[1]
                    p.op("t", lambda e, ts=ts, eidx=eidx: e.matmul(
                        cbt[:, :], sel[:, eidx * 128:(eidx + 1) * 128],
                        combT[0:8, ts * TG:(ts + 1) * TG], start=True, stop=True),
                        reads=[selb, combTb], writes=[cbb])
                else:
                    p.op("t", lambda e, ts=ts: e.matmul(
                        cbt[:, :], ones_k1_bf[:, :],
                        srow[0:1, ts * TG:(ts + 1) * TG], start=True, stop=True),
                        reads=[ones_k1_bfb, srowb], writes=[cbb])
                p.op("c", lambda e, ts=ts: e.activation(
                    cbc[:, ts * TG:(ts + 1) * TG], cbt[:, :], AF.Copy),
                    reads=[cbb], writes=[cbcb])
        for hc in range(nch_h):
            wgt_t, wgt_b = wgt[hc % 2]
            wut_t, wut_b = wut[hc % 2]
            p.dma("g", wgt_t[:, :], load_g(hc), writes=[wgt_b])
            p.dma("g", wut_t[:, :], load_u(hc), writes=[wut_b])
            for ts in range(4):
                x1t_t, x1t_b = x1t[ts % 2]
                xps, xpsb = x1_ps[ts % 2]
                ups, upsb = x3_ps[ts % 2]
                for kc in range(NCH):
                    p.op("t", lambda e, kc=kc, ts=ts, xps=xps, wgt_t=wgt_t: e.matmul(
                        xps[:, :], wgt_t[:, kc * 128:(kc + 1) * 128],
                        xn_sb[:, kc * T + ts * TG: kc * T + ts * TG + TG],
                        start=(kc == 0), stop=(kc == NCH - 1)),
                        reads=[wgt_b, xnb[kc][ts]], writes=[xpsb], inc=(kc == NCH - 1))
                p.op("c", lambda e, xps=xps, x1t_t=x1t_t: e.activation(
                    x1t_t[:, :], xps[:, :], AF.Silu),
                    reads=[xpsb], writes=[x1t_b])
                for kc in range(NCH):
                    p.op("t", lambda e, kc=kc, ts=ts, ups=ups, wut_t=wut_t: e.matmul(
                        ups[:, :], wut_t[:, kc * 128:(kc + 1) * 128],
                        xn_sb[:, kc * T + ts * TG: kc * T + ts * TG + TG],
                        start=(kc == 0), stop=(kc == NCH - 1), skip_group_check=True),
                        reads=[wut_b, xnb[kc][ts]], writes=[upsb], inc=(kc == NCH - 1))
                mslice = m_sb[:, hc * T + ts * TG: hc * T + ts * TG + TG]
                p.op("v", lambda e, mslice=mslice, x1t_t=x1t_t, ups=ups: e.tensor_mul(
                    mslice, x1t_t[:, :], ups[:, :]),
                    reads=[x1t_b, upsb], writes=[m_b[hc][ts]])
                if scale_mode is not None:
                    p.op("v", lambda e, mslice=mslice, ts=ts: e.tensor_mul(
                        mslice, mslice, cbc[:, ts * TG:(ts + 1) * TG]),
                        reads=[m_b[hc][ts], cbcb], writes=[m_b[hc][ts]])
        for fcg in range(4):
            wdt_t, wdt_b = wdt[fcg % 2]
            p.dma("g", wdt_t[:, 0:nch_h * 512], load_d(fcg), writes=[wdt_b])
            for fc4 in range(4):
                fc = fcg * 4 + fc4
                for ts in range(4):
                    dps, dpsb = dn_ps[(fc4 * 4 + ts) % 2]
                    for hc in range(nch_h):
                        p.op("t", lambda e, hc=hc, fc4=fc4, ts=ts, dps=dps, wdt_t=wdt_t: e.matmul(
                            dps[:, :], wdt_t[:, hc * 512 + fc4 * 128: hc * 512 + (fc4 + 1) * 128],
                            m_sb[:, hc * T + ts * TG: hc * T + ts * TG + TG],
                            start=(hc == 0), stop=(hc == nch_h - 1)),
                            reads=[wdt_b, m_b[hc][ts]], writes=[dpsb], inc=(hc == nch_h - 1))
                    dt_t, dt_b = dt_sb[(fc4 * 4 + ts) % 2]
                    p.op("v", lambda e, dps=dps, dt_t=dt_t: e.tensor_copy(dt_t[:, :], dps[:, :]),
                         reads=[dpsb], writes=[dt_b])
                    p.dma("g", y_dram[fc * 128:(fc + 1) * 128, ts * TG:(ts + 1) * TG],
                          dt_t[:, :], reads=[dt_b, ybufs[fc][ts]],
                          writes=[ybufs[fc][ts]], accum_op=ALU.add)

    for e_i in range(8):
        expert_pass(
            nh_ch,
            lambda hc, e_i=e_i: W["wg"][(e_i * nh_ch + hc) * 128:
                                        (e_i * nh_ch + hc + 1) * 128, :],
            lambda hc, e_i=e_i: W["wu"][(e_i * nh_ch + hc) * 128:
                                        (e_i * nh_ch + hc + 1) * 128, :],
            lambda fcg, e_i=e_i: W["wd"][(e_i * 4 + fcg) * 128:
                                         (e_i * 4 + fcg + 1) * 128, :],
            ("sel", e_i), f"e{e_i}")
    half = nhs_ch // 2
    for sh_i in range(2):
        expert_pass(
            half,
            lambda hc, sh_i=sh_i: W["shg"][(sh_i * half + hc) * 128:
                                           (sh_i * half + hc + 1) * 128, :],
            lambda hc, sh_i=sh_i: W["shu"][(sh_i * half + hc) * 128:
                                           (sh_i * half + hc + 1) * 128, :],
            lambda fcg, sh_i=sh_i: W["shd"][fcg * 128:(fcg + 1) * 128,
                                            sh_i * half * 512:(sh_i + 1) * half * 512],
            ("sgate", None) if has_sgate else None, f"sh{sh_i}")


def build_ffn(kind):
    nh_ch = 8 if kind == "l0" else 11
    nhs_ch = 16 if kind == "l0" else 22
    hdim = nh_ch * 128
    hs = nhs_ch * 128
    nc = bass.Bass("TRN2", target_bir_lowering=False)
    xT = nc.dram_tensor("xT", [D, T], F32, kind="ExternalInput").ap()
    yT = nc.dram_tensor("yT", [D, T], F32, kind="ExternalOutput").ap()
    yin = nc.dram_tensor("yin", [D, T], F32, kind="ExternalInput").ap()
    W = {
        "gwT": nc.dram_tensor("gwT", [D, 8], F32, kind="ExternalInput").ap(),
        "wg": nc.dram_tensor("wg", [8 * nh_ch * 128, D], F32, kind="ExternalInput").ap(),
        "wu": nc.dram_tensor("wu", [8 * nh_ch * 128, D], F32, kind="ExternalInput").ap(),
        "wd": nc.dram_tensor("wd", [8 * 4 * 128, nh_ch * 512], F32, kind="ExternalInput").ap(),
        "shg": nc.dram_tensor("shg", [nhs_ch * 128, D], F32, kind="ExternalInput").ap(),
        "shu": nc.dram_tensor("shu", [nhs_ch * 128, D], F32, kind="ExternalInput").ap(),
        "shd": nc.dram_tensor("shd", [4 * 128, nhs_ch * 512], F32, kind="ExternalInput").ap(),
        "sel": nc.dram_tensor("sel", [8, 1024], F32, kind="ExternalInput").ap(),
    }
    if kind == "l0":
        W["segT"] = nc.dram_tensor("segT", [128, NCH], F32, kind="ExternalInput").ap()
    ones_col = nc.dram_tensor("ones_col", [128, 1], F32, kind="ExternalInput").ap()
    ones_k1 = nc.dram_tensor("ones_k1", [1, 128], F32, kind="ExternalInput").ap()
    ident = nc.dram_tensor("ident", [128, 128], F32, kind="ExternalInput").ap()
    dram = {"ones_col": ones_col, "ones_k1": ones_k1, "ident": ident}

    p = Prog()
    with contextlib.ExitStack() as stack:
        ctx = Ctx(nc, stack, p)
        consts = make_consts(ctx, p, dram)
        ln = LNBlock(ctx)
        ctx.ffn_xn = (ctx.sb([128, NCH * T], BF16, "xn"),
                      [[Buf(f"xn{c}_{g}") for g in range(NTG)] for c in range(NCH)])
        ctx.ffn_combT = ctx.sbuf([8, T], BF16, "combT")
        ctx.ffn_sel = ctx.sbuf([8, 1024], BF16, "self_sel")
        ctx.ffn_gwt = ctx.sbuf([128, NCH * 8], BF16, "gwt")
        ctx.ffn_wgt = [ctx.sbuf([128, NCH * 128], BF16, f"wgt{i}") for i in range(2)]
        ctx.ffn_wut = [ctx.sbuf([128, NCH * 128], BF16, f"wut{i}") for i in range(2)]
        ctx.ffn_wdt = [ctx.sbuf([128, 11 * 512], BF16, f"wdt{i}") for i in range(2)]
        ctx.ffn_x1t = [ctx.sbuf([128, TG], BF16, f"x1t{i}") for i in range(2)]
        nh_max = max(nh_ch, nhs_ch // 2)
        ctx.ffn_m = (ctx.sb([128, nh_max * T], BF16, "m_sb"),
                     [[Buf(f"m{h}_{t}") for t in range(4)] for h in range(nh_max)])
        ctx.ffn_cbc = ctx.sbuf([128, NTG * TG], BF16, "cbc")
        ctx.ffn_dt = [ctx.sbuf([128, TG], F32, f"dtb{i}") for i in range(2)]
        ctx.ffn_sc = ctx.sbuf([8, TG], F32, "ffn_sc")
        ctx.ffn_st = ctx.sbuf([128, 8], F32, "ffn_st")
        ctx.ffn_eq1 = ctx.sbuf([128, 8], F32, "ffn_eq1")
        ctx.ffn_eq2 = ctx.sbuf([128, 8], F32, "ffn_eq2")
        ctx.ffn_smk = ctx.sbuf([128, 8], F32, "ffn_smk")
        ctx.ffn_ct = ctx.sbuf([128, 8], F32, "ffn_ct")
        ctx.ffn_stats = ctx.sbuf([128, 8], F32, "ffn_stats")
        if kind == "l0":
            ctx.ffn_srow = ctx.sbuf([1, T], BF16, "ffn_srow")
            ctx.ffn_segt = ctx.sbuf([128, NCH], BF16, "ffn_segt")

        with nc.Block() as block:
            ctx.ffn_yin = yin
            emit_ffn(ctx, consts, ln, xT, yT, W, nh_ch, nhs_ch, kind)
            p.finish()
            p.materialize(nc, block, stack)
    return nc


def run_ffn(kind, xTs, w):
    nc = _get_nc("ffn_" + kind, lambda: build_ffn(kind))
    cn = _consts_np()
    shared = dict(w)
    shared.update(cn)
    in_maps = [{"xT": xTs[c], "yin": xTs[c], **shared} for c in range(NCORES)]
    res = run_bass_kernel_spmd(nc, in_maps, core_ids=list(range(NCORES)))
    return [res.results[c]["yT"] for c in range(NCORES)]


def _sel_np():
    s = np.zeros((8, 1024), np.float32)
    for e in range(8):
        s[e, e * 128:(e + 1) * 128] = 1.0
    return s


def _rg(w, kc_n, out_n, ow):
    a = np.asarray(w, np.float32).reshape(kc_n, 128, out_n, ow)
    return np.ascontiguousarray(a.transpose(2, 1, 0, 3).reshape(out_n * 128, kc_n * ow))


def _restage_up(w_de, nh):
    a = np.asarray(w_de, np.float32).reshape(NCH, 128, nh, 128)
    return np.ascontiguousarray(a.transpose(2, 1, 0, 3).reshape(nh * 128, D))


def _restage_down(w_hd, nh):
    a = np.asarray(w_hd, np.float32).reshape(nh, 128, 4, 512)
    return np.ascontiguousarray(a.transpose(2, 1, 0, 3).reshape(4 * 128, nh * 512))


def prep_ffn_l0(ins):
    wg = np.asarray(ins["L0_ffn_w_gate"], np.float32)
    wu = np.asarray(ins["L0_ffn_w_up"], np.float32)
    wd = np.asarray(ins["L0_ffn_w_down"], np.float32)
    return {
        "gwT": _np(ins["L0_ffn_gate_w"].T),
        "wg": np.concatenate([_restage_up(wg[e].T, 8) for e in range(8)], axis=0),
        "wu": np.concatenate([_restage_up(wu[e].T, 8) for e in range(8)], axis=0),
        "wd": np.concatenate([_restage_down(wd[e].T, 8) for e in range(8)], axis=0),
        "shg": _restage_up(np.asarray(ins["L0_ffn_sh_gate"], np.float32).T, 16),
        "shu": _restage_up(np.asarray(ins["L0_ffn_sh_up"], np.float32).T, 16),
        "shd": _restage_down(np.asarray(ins["L0_ffn_sh_down"], np.float32).T, 16),
        "segT": _np(np.asarray(ins["L0_ffn_se_gate_w"]).reshape(NCH, 128).T),
        "sel": _sel_np(),
    }


def prep_ffn_l1(ins):
    wg = np.asarray(ins["L1_ffn_w_gate"], np.float32)
    wu = np.asarray(ins["L1_ffn_w_up"], np.float32)
    wd = np.asarray(ins["L1_ffn_w_down"], np.float32)
    return {
        "gwT": _np(ins["L1_ffn_gate_w"]),
        "wg": np.concatenate([_restage_up(wg[e], 11) for e in range(8)], axis=0),
        "wu": np.concatenate([_restage_up(wu[e], 11) for e in range(8)], axis=0),
        "wd": np.concatenate([_restage_down(wd[e], 11) for e in range(8)], axis=0),
        "shg": _restage_up(np.asarray(ins["L1_ffn_sw_gate"], np.float32), 22),
        "shu": _restage_up(np.asarray(ins["L1_ffn_sw_up"], np.float32), 22),
        "shd": _restage_down(np.asarray(ins["L1_ffn_sw_down"], np.float32), 22),
        "sel": _sel_np(),
    }




def emit_rope_evac(ctx, src_ps, src_psb, dst_ap, dst_buf, cosx, cosxb, sinx, sinxb,
                   gcol, rot_t, rot_tb, rq_ps, rq_psb):
    p = ctx.p
    ra, rab = ctx.rope_ra
    qsb, qsbb = ctx.rope_qs
    p.op("c", lambda e: e.activation(qsb[:, :], src_ps[:, :], AF.Copy),
         reads=[src_psb], writes=[qsbb])
    p.op("t", lambda e: e.matmul(rq_ps[:, :], rot_t[:, :], qsb[:, :],
                                 start=True, stop=True),
         reads=[qsbb, rot_tb], writes=[rq_psb])
    p.op("v", lambda e: e.tensor_mul(ra[:, :], qsb[:, :],
                                     cosx[:, gcol:gcol + TG]),
         reads=[qsbb, cosxb], writes=[rab])
    p.op("v", lambda e: e.tensor_mul(dst_ap, rq_ps[:, :],
                                     sinx[:, gcol:gcol + TG]),
         reads=[rq_psb, sinxb], writes=[dst_buf])
    p.op("v", lambda e: e.tensor_add(dst_ap, dst_ap, ra[:, :]),
         reads=[rab, dst_buf], writes=[dst_buf])


def emit_norm_stat(ctx, consts, src_ap, src_buf, nrow, nrowb, h, first):
    p = ctx.p
    ones_col_bf, ones_col_bfb = consts["ones_col_bf"]
    sq, sqb = ctx.ns_sq
    ns_ps, ns_psb = ctx.pt[6], ctx.pb[6]
    tmp, tmpb = ctx.ns_tmp
    p.op("c", lambda e: e.activation(sq[:, :], src_ap, AF.Square),
         reads=[src_buf], writes=[sqb])
    p.op("t", lambda e: e.matmul(ns_ps[0:1, :], ones_col_bf[:, :], sq[:, :],
                                 start=True, stop=True),
         reads=[sqb, ones_col_bfb], writes=[ns_psb])
    p.op("v", lambda e: e.reduce_max(tmp[0:1, 0:1], ns_ps[0:1, :], AX.X),
         reads=[ns_psb], writes=[tmpb])
    if first:
        p.op("v", lambda e, h=h: e.tensor_copy(nrow[0:1, h:h + 1], tmp[0:1, 0:1]),
             reads=[tmpb], writes=[nrowb])
    else:
        p.op("v", lambda e, h=h: e.tensor_max(nrow[0:1, h:h + 1], nrow[0:1, h:h + 1],
                                              tmp[0:1, 0:1]),
             reads=[tmpb, nrowb], writes=[nrowb])


def build_mla_prep():
    nc = bass.Bass("TRN2", target_bir_lowering=False)
    xT = nc.dram_tensor("xT", [D, T], F32, kind="ExternalInput").ap()
    qT_o = nc.dram_tensor("qT_o", [D, T], BF16, kind="ExternalOutput").ap()
    kT_o = nc.dram_tensor("kT_o", [D, T], BF16, kind="ExternalOutput").ap()
    v_o = nc.dram_tensor("v_o", [NH * 128, NCH * 128], BF16, kind="ExternalOutput").ap()
    qn_o = nc.dram_tensor("qn_o", [1, NH], F32, kind="ExternalOutput").ap()
    kn_o = nc.dram_tensor("kn_o", [1, NH], F32, kind="ExternalOutput").ap()
    qwT = nc.dram_tensor("qwT", [NH * 128, D], F32, kind="ExternalInput").ap()
    kvlT = nc.dram_tensor("kvlT", [4 * 128, D], F32, kind="ExternalInput").ap()
    kvbkT = nc.dram_tensor("kvbkT", [NH * 128, 512], F32, kind="ExternalInput").ap()
    kvbvT = nc.dram_tensor("kvbvT", [4 * 128, 4 * 512], F32, kind="ExternalInput").ap()
    kvnw = nc.dram_tensor("kvnw", [128, 4], F32, kind="ExternalInput").ap()
    cos_d = nc.dram_tensor("cos_d", [128, T], F32, kind="ExternalInput").ap()
    sin_d = nc.dram_tensor("sin_d", [128, T], F32, kind="ExternalInput").ap()
    rot_d = nc.dram_tensor("rot_d", [128, 128], F32, kind="ExternalInput").ap()
    ones_col = nc.dram_tensor("ones_col", [128, 1], F32, kind="ExternalInput").ap()
    ones_k1 = nc.dram_tensor("ones_k1", [1, 128], F32, kind="ExternalInput").ap()
    ident = nc.dram_tensor("ident", [128, 128], F32, kind="ExternalInput").ap()
    dram = {"ones_col": ones_col, "ones_k1": ones_k1, "ident": ident}

    p = Prog()
    with contextlib.ExitStack() as stack:
        ctx = Ctx(nc, stack, p)
        consts = make_consts(ctx, p, dram)
        ln = LNBlock(ctx)
        xn_sb = ctx.sb([128, NCH * T], BF16, "xn")
        xnb = [[Buf(f"xn{c}_{g}") for g in range(NTG)] for c in range(NCH)]
        cos_sb, cosb = ctx.sbuf([128, T], F32, "cos_sb")
        sin_sb, sinb = ctx.sbuf([128, T], F32, "sin_sb")
        ctx.rope_ra = ctx.sbuf([128, TG], F32, "rope_ra")
        ctx.rope_qs = ctx.sbuf([128, TG], BF16, "rope_qs")
        rot_sb, rot_sbb = ctx.sbuf([128, 128], BF16, "rot_sb")
        ctx.ns_sq = ctx.sbuf([128, TG], BF16, "ns_sq")
        ctx.ns_tmp = ctx.sbuf([1, 2], F32, "ns_tmp")
        qn_sb, qnb = ctx.sbuf([1, NH], F32, "qn_sb")
        kn_sb, knb = ctx.sbuf([1, NH], F32, "kn_sb")
        wqt = [ctx.sbuf([128, NCH * 128], BF16, f"wqt{i}") for i in range(2)]
        kvlt = [ctx.sbuf([128, NCH * 128], BF16, f"kvlt{i}") for i in range(2)]
        kbt = [ctx.sbuf([128, 4 * 128], BF16, f"kbt{i}") for i in range(2)]
        vbt = [ctx.sbuf([128, 4 * 512], BF16, f"vbt{i}") for i in range(2)]
        lraw = ctx.sbuf([128, 4 * TG], F32, "lraw")
        latn_sb = ctx.sb([128, 4 * T], BF16, "latn")
        latnb = [[Buf(f"lat{lc}_{g}") for g in range(NTG)] for lc in range(4)]
        kvnw_sb, kvnwb = ctx.sbuf([128, 4], F32, "kvnw_sb")
        otile = [ctx.sbuf([128, TG], BF16, f"otile{i}") for i in range(2)]
        rows2 = ctx.sb([1, 4 * TG], F32, "rms_rows")
        rows2b = [Buf(f"rms_row{i}") for i in range(4)]

        p.dma("s", cos_sb[:, :], cos_d, writes=[cosb])
        p.dma("s", sin_sb[:, :], sin_d, writes=[sinb])
        p.dma("s", kvnw_sb[:, :], kvnw, writes=[kvnwb])
        p.dma("g", rot_sb[:, :], rot_d, writes=[rot_sbb])

        with nc.Block() as block:
            for g in range(NTG):
                ln.emit(xT, g, xn_sb, [xnb[c][g] for c in range(NCH)], consts,
                        (ctx.pt[0], ctx.pb[0]), (ctx.pt[1], ctx.pb[1]),
                        (ctx.pt[2], ctx.pb[2]), (ctx.pt[3], ctx.pb[3]),
                        xn_off=g * TG, xn_stride=T)
                sq_ps, sq_psb = ctx.pt[4], ctx.pb[4]
                for lc in range(4):
                    lt, ltb = kvlt[lc % 2]
                    p.dma("g", lt[:, :], kvlT[lc * 128:(lc + 1) * 128, :], writes=[ltb])
                    lps, lpsb = ctx.pt[lc % 2], ctx.pb[lc % 2]
                    for kc in range(NCH):
                        p.op("t", lambda e, kc=kc, lt=lt, lps=lps, g=g: e.matmul(
                            lps[:, :], lt[:, kc * 128:(kc + 1) * 128],
                            xn_sb[:, kc * T + g * TG: kc * T + g * TG + TG],
                            start=(kc == 0), stop=(kc == NCH - 1)),
                            reads=[ltb, xnb[kc][g]], writes=[lpsb], inc=(kc == NCH - 1))
                    p.op("v", lambda e, lc=lc, lps=lps: e.tensor_copy(
                        lraw[0][:, lc * TG:(lc + 1) * TG], lps[:, :]),
                        reads=[lpsb], writes=[lraw[1]])
                    sqt, sqtb = ctx.ns_sq
                    p.op("c", lambda e, lc=lc: e.activation(
                        sqt[:, :], lraw[0][:, lc * TG:(lc + 1) * TG], AF.Square),
                        reads=[lraw[1]], writes=[sqtb])
                    p.op("t", lambda e, lc=lc, sqt=sqt: e.matmul(
                        sq_ps[0:1, :], consts["ones_col_bf"][0][:, :], sqt[:, :],
                        start=(lc == 0), stop=(lc == 3), skip_group_check=True),
                        reads=[sqtb, consts["ones_col_bf"][1]], writes=[sq_psb])
                MS, RS = 0, 1
                p.op("v", lambda e: e.tensor_scalar(
                    rows2[0:1, MS * TG:(MS + 1) * TG], sq_ps[0:1, :], 1.0 / 512.0,
                    1e-6, ALU.mult, ALU.add),
                    reads=[sq_psb], writes=[rows2b[MS]])
                p.op("c", lambda e: e.activation(
                    rows2[0:1, MS * TG:(MS + 1) * TG], rows2[0:1, MS * TG:(MS + 1) * TG],
                    AF.Sqrt), reads=[rows2b[MS]], writes=[rows2b[MS]])
                p.op("v", lambda e: e.reciprocal(
                    rows2[0:1, RS * TG:(RS + 1) * TG], rows2[0:1, MS * TG:(MS + 1) * TG]),
                    reads=[rows2b[MS]], writes=[rows2b[RS]])
                rbc_ps, rbc_psb = ctx.pt[5], ctx.pb[5]
                p.op("t", lambda e: e.matmul(
                    rbc_ps[:, :], consts["ones_k1"][0][:, :],
                    rows2[0:1, RS * TG:(RS + 1) * TG], start=True, stop=True),
                    reads=[rows2b[RS], consts["ones_k1"][1]], writes=[rbc_psb])
                for lc in range(4):
                    p.op("v", lambda e, lc=lc: e.tensor_mul(
                        lraw[0][:, lc * TG:(lc + 1) * TG],
                        lraw[0][:, lc * TG:(lc + 1) * TG], rbc_ps[:, :]),
                        reads=[lraw[1], rbc_psb], writes=[lraw[1]])
                    p.op("v", lambda e, lc=lc, g=g: e.tensor_scalar(
                        latn_sb[:, lc * T + g * TG: lc * T + g * TG + TG],
                        lraw[0][:, lc * TG:(lc + 1) * TG], kvnw_sb[:, lc:lc + 1],
                        None, ALU.mult),
                        reads=[lraw[1], kvnwb], writes=[latnb[lc][g]])
            q_ps = [(ctx.pt[0], ctx.pb[0]), (ctx.pt[1], ctx.pb[1])]
            for h in range(NH):
                wt, wtb = wqt[h % 2]
                p.dma("g", wt[:, :], qwT[h * 128:(h + 1) * 128, :], writes=[wtb])
                for g in range(NTG):
                    qp, qpb = q_ps[g % 2]
                    for kc in range(NCH):
                        p.op("t", lambda e, kc=kc, wt=wt, qp=qp, g=g: e.matmul(
                            qp[:, :], wt[:, kc * 128:(kc + 1) * 128],
                            xn_sb[:, kc * T + g * TG: kc * T + g * TG + TG],
                            start=(kc == 0), stop=(kc == NCH - 1)),
                            reads=[wtb, xnb[kc][g]], writes=[qpb], inc=(kc == NCH - 1))
                    ot, otb = otile[g % 2]
                    emit_rope_evac(ctx, qp, qpb, ot[:, :], otb, cos_sb, cosb,
                                   sin_sb, sinb, g * TG, rot_sb, rot_sbb,
                                   ctx.pt[5], ctx.pb[5])
                    emit_norm_stat(ctx, consts, ot[:, :], otb, qn_sb, qnb, h, g == 0)
                    p.dma("g", qT_o[h * 128:(h + 1) * 128, g * TG:(g + 1) * TG],
                          ot[:, :], reads=[otb])
            for h in range(NH):
                kt, ktb2 = kbt[h % 2]
                p.dma("g", kt[:, :], kvbkT[h * 128:(h + 1) * 128, :], writes=[ktb2])
                for g in range(NTG):
                    qp, qpb = q_ps[g % 2]
                    for lc in range(4):
                        p.op("t", lambda e, lc=lc, kt=kt, qp=qp, g=g: e.matmul(
                            qp[:, :], kt[:, lc * 128:(lc + 1) * 128],
                            latn_sb[:, lc * T + g * TG: lc * T + g * TG + TG],
                            start=(lc == 0), stop=(lc == 3)),
                            reads=[ktb2, latnb[lc][g]], writes=[qpb], inc=(lc == 3))
                    ot, otb = otile[g % 2]
                    emit_rope_evac(ctx, qp, qpb, ot[:, :], otb, cos_sb, cosb,
                                   sin_sb, sinb, g * TG, rot_sb, rot_sbb,
                                   ctx.pt[5], ctx.pb[5])
                    emit_norm_stat(ctx, consts, ot[:, :], otb, kn_sb, knb, h, g == 0)
                    p.dma("g", kT_o[h * 128:(h + 1) * 128, g * TG:(g + 1) * TG],
                          ot[:, :], reads=[otb])
            for vfg in range(4):
                vt, vtb = vbt[vfg % 2]
                p.dma("g", vt[:, :], kvbvT[vfg * 128:(vfg + 1) * 128, :], writes=[vtb])
                for g in range(NTG):
                    for tsub in range(4):
                        vp, vpb = q_ps[tsub % 2]
                        for lc in range(4):
                            p.op("t", lambda e, lc=lc, vt=vt, vp=vp, g=g, tsub=tsub: e.matmul(
                                vp[:, :],
                                latn_sb[:, lc * T + g * TG + tsub * 128:
                                        lc * T + g * TG + (tsub + 1) * 128],
                                vt[:, lc * 512:(lc + 1) * 512],
                                start=(lc == 0), stop=(lc == 3)),
                                reads=[vtb, latnb[lc][g]], writes=[vpb], inc=(lc == 3))
                        ot, otb = otile[tsub % 2]
                        p.op("c", lambda e, vp=vp, ot=ot: e.activation(
                            ot[:, :], vp[:, :], AF.Copy), reads=[vpb], writes=[otb])
                        p.dma("g",
                              v_o.rearrange("(h p) (kc j) -> p h kc j", p=128, j=128)
                              [:, vfg * 4:(vfg + 1) * 4, g * 4 + tsub, :],
                              ot[:, :].rearrange("p (hs j) -> p hs j", hs=4),
                              reads=[otb])
            p.dma("g", qn_o, qn_sb[:, :], reads=[qnb])
            p.dma("g", kn_o, kn_sb[:, :], reads=[knb])
            p.finish()
            p.materialize(nc, block, stack)
    return nc


def build_flash(with_bias):
    nc = bass.Bass("TRN2", target_bir_lowering=False)
    qT = nc.dram_tensor("qT", [D, T], BF16, kind="ExternalInput").ap()
    kTf = nc.dram_tensor("kTf", [D, S], BF16, kind="ExternalInput").ap()
    vf = nc.dram_tensor("vf", [NH * 128, 64 * 128], BF16, kind="ExternalInput").ap()
    qn = nc.dram_tensor("qn", [1, NH], F32, kind="ExternalInput").ap()
    kn_all = nc.dram_tensor("kn_all", [1, 4 * NH], F32, kind="ExternalInput").ap()
    owT = nc.dram_tensor("owT", [4 * 128, NCH * 512], F32, kind="ExternalInput").ap()
    yin = nc.dram_tensor("yin", [D, T], F32, kind="ExternalInput").ap()
    yT = nc.dram_tensor("yT", [D, T], F32, kind="ExternalOutput").ap()
    if with_bias:
        pbias = nc.dram_tensor("pbias", [128, NCH], F32, kind="ExternalInput").ap()
    ones_col = nc.dram_tensor("ones_col", [128, 1], F32, kind="ExternalInput").ap()
    ones_k1 = nc.dram_tensor("ones_k1", [1, 128], F32, kind="ExternalInput").ap()
    ident = nc.dram_tensor("ident", [128, 128], F32, kind="ExternalInput").ap()
    dram = {"ones_col": ones_col, "ones_k1": ones_k1, "ident": ident}
    SCALE = float(DK) ** -0.5

    p = Prog()
    with contextlib.ExitStack() as stack:
        ctx = Ctx(nc, stack, p)
        consts = make_consts(ctx, p, dram)
        kt_sb = [ctx.sbuf([128, S], BF16, f"kt_sb{i}") for i in range(2)]
        v_sb = [ctx.sbuf([128, 64 * 129], BF16, f"v_sb{i}") for i in range(2)]
        q_sb = [ctx.sbuf([128, TG], BF16, f"q_sb{i}") for i in range(2)]
        at_sb = [ctx.sbuf([128, TG], BF16, f"at_sb{i}") for i in range(4)]
        outT = ctx.sb([128, NCH * T], BF16, "outT")
        outTb = [[Buf(f"oT{c}_{g}") for g in range(NTG)] for c in range(NCH)]
        nrow, nrowb = ctx.sbuf([1, 8 * NH], F32, "nrow")
        bias_sb, bias_sbb = ctx.sbuf([128, NH], F32, "bias_sb")
        rl_sb, rlb = ctx.sbuf([128, 1], F32, "rl_sb")
        osb, osbb = ctx.sbuf([128, 128], BF16, "osb")
        wot = [ctx.sbuf([128, NCH * 512], BF16, f"wotf{i}") for i in range(2)]
        dt_sb = [ctx.sbuf([128, TG], F32, f"dtf{i}") for i in range(2)]
        if with_bias:
            pb_sb, pb_sbb = ctx.sbuf([128, NCH], F32, "pb_sb")
            p.dma("s", pb_sb[:, :], pbias, writes=[pb_sbb])

        vob = [Buf(f"vones{i}") for i in range(2)]
        for i in range(2):
            p.op("v", lambda e, i=i: e.memset(
                v_sb[i][0][:, :].rearrange("p (c f) -> p c f", c=64)[:, :, 128:129], 1.0),
                writes=[v_sb[i][1], vob[i]])

        KN, QN, BR = 0, 1, 2
        p.dma("s", nrow[0:1, 4 * NH:5 * NH], qn, writes=[nrowb])
        p.dma("s", nrow[0:1, 0:4 * NH], kn_all, writes=[nrowb])
        p.op("v", lambda e: e.tensor_max(nrow[0:1, 0:NH], nrow[0:1, 0:NH],
                                         nrow[0:1, NH:2 * NH]),
             reads=[nrowb], writes=[nrowb])
        p.op("v", lambda e: e.tensor_max(nrow[0:1, 2 * NH:3 * NH], nrow[0:1, 2 * NH:3 * NH],
                                         nrow[0:1, 3 * NH:4 * NH]),
             reads=[nrowb], writes=[nrowb])
        p.op("v", lambda e: e.tensor_max(nrow[0:1, 0:NH], nrow[0:1, 0:NH],
                                         nrow[0:1, 2 * NH:3 * NH]),
             reads=[nrowb], writes=[nrowb])
        p.op("c", lambda e: e.activation(nrow[0:1, 0:NH], nrow[0:1, 0:NH], AF.Sqrt),
             reads=[nrowb], writes=[nrowb])
        p.op("c", lambda e: e.activation(nrow[0:1, 4 * NH:5 * NH],
                                         nrow[0:1, 4 * NH:5 * NH], AF.Sqrt),
             reads=[nrowb], writes=[nrowb])
        p.op("v", lambda e: e.tensor_mul(nrow[0:1, 5 * NH:6 * NH], nrow[0:1, 0:NH],
                                         nrow[0:1, 4 * NH:5 * NH]),
             reads=[nrowb], writes=[nrowb])
        p.op("v", lambda e: e.tensor_scalar(nrow[0:1, 5 * NH:6 * NH],
                                            nrow[0:1, 5 * NH:6 * NH], -SCALE,
                                            None, ALU.mult),
             reads=[nrowb], writes=[nrowb])
        bb_ps, bb_psb = ctx.pt[6], ctx.pb[6]
        p.op("t", lambda e: e.matmul(bb_ps[:, 0:NH], consts["ones_k1"][0][:, :],
                                     nrow[0:1, 5 * NH:6 * NH], start=True, stop=True),
             reads=[nrowb, consts["ones_k1"][1]], writes=[bb_psb])
        p.op("v", lambda e: e.tensor_copy(bias_sb[:, :], bb_ps[:, 0:NH]),
             reads=[bb_psb], writes=[bias_sbb])

        ov_ps = [(ctx.pt[i], ctx.pb[i]) for i in range(4)]
        st_ps = [(ctx.pt[4], ctx.pb[4]), (ctx.pt[5], ctx.pb[5]),
                 (ctx.pt[6], ctx.pb[6])]

        with nc.Block() as block:
            for h in range(NH):
                ktt, kttb = kt_sb[h % 2]
                p.dma("g", ktt[:, :], kTf[h * 128:(h + 1) * 128, :], writes=[kttb])
                vtt, vttb = v_sb[h % 2]
                p.dma("g", vtt[:, :].rearrange("p (c f) -> p c f", c=64)[:, :, 0:128],
                      vf[h * 128:(h + 1) * 128, :].rearrange("p (c j) -> p c j", j=128),
                      reads=[vob[h % 2]], writes=[vttb])
                for qg in range(NTG):
                    qt, qtb = q_sb[qg % 2]
                    p.dma("g", qt[:, :], qT[h * 128:(h + 1) * 128, qg * TG:(qg + 1) * TG],
                          writes=[qtb])
                    first = True
                    for kc in range(64):
                        sp, spb = st_ps[kc % 3]
                        p.op("t", lambda e, kc=kc, ktt=ktt, qt=qt, sp=sp: e.matmul(
                            sp[:, :], ktt[:, kc * 128:(kc + 1) * 128], qt[:, :],
                            start=True, stop=True),
                            reads=[kttb, qtb], writes=[spb])
                        att, attb = at_sb[kc % 4]
                        p.op("c", lambda e, sp=sp, att=att, h=h: e.activation(
                            att[:, :], sp[:, :], AF.Exp, bias=bias_sb[:, h:h + 1],
                            scale=SCALE),
                            reads=[spb, bias_sbb], writes=[attb])
                        for qs in range(4):
                            op_, opb_ = ov_ps[qs]
                            p.op("t", lambda e, kc=kc, qs=qs, att=att, vtt=vtt,
                                 op_=op_, first=first: e.matmul(
                                op_[:, 0:129], att[:, qs * 128:(qs + 1) * 128],
                                vtt[:, kc * 129:(kc + 1) * 129],
                                start=first, stop=(kc == 63), skip_group_check=True),
                                reads=[attb, vttb], writes=[opb_], inc=(qs == 3))
                        first = False
                    for qs in range(4):
                        op_, opb_ = ov_ps[qs]
                        p.op("v", lambda e, op_=op_: e.reciprocal(rl_sb[:, :],
                                                                  op_[:, 128:129]),
                             reads=[opb_], writes=[rlb])
                        p.op("v", lambda e, op_=op_: e.tensor_scalar(
                            osb[:, :], op_[:, 0:128], rl_sb[:, 0:1], None, ALU.mult),
                            reads=[opb_, rlb], writes=[osbb])
                        p.op("t", lambda e: e.transpose(ctx.ptb[:, 0:128], osb[:, :],
                                                        consts["ident_bf"][0][:, :]),
                             reads=[osbb, consts["ident_bf"][1]], writes=[ctx.pbb])
                        p.op("c", lambda e, h=h, qg=qg, qs=qs: e.activation(
                            outT[:, h * T + qg * TG + qs * 128:
                                 h * T + qg * TG + (qs + 1) * 128],
                            ctx.ptb[:, 0:128], AF.Copy),
                            reads=[ctx.pbb], writes=[outTb[h][qg]])
            ybufs = [[Buf(f"y{fc}_{ts}") for ts in range(4)] for fc in range(NCH)]
            p.dma("s", yT, yin, writes=[b for row in ybufs for b in row])
            dn_ps = [(ctx.pt[4], ctx.pb[4]), (ctx.pt[5], ctx.pb[5])]
            for fcg in range(4):
                wt, wtb = wot[fcg % 2]
                p.dma("g", wt[:, :], owT[fcg * 128:(fcg + 1) * 128, :], writes=[wtb])
                for fc4 in range(4):
                    fc = fcg * 4 + fc4
                    for ts in range(4):
                        dps, dpsb = dn_ps[(fc4 * 4 + ts) % 2]
                        for kc in range(NCH):
                            p.op("t", lambda e, kc=kc, fc4=fc4, ts=ts, dps=dps,
                                 wt=wt: e.matmul(
                                dps[:, :], wt[:, kc * 512 + fc4 * 128:
                                              kc * 512 + (fc4 + 1) * 128],
                                outT[:, kc * T + ts * TG:(kc) * T + ts * TG + TG],
                                start=(kc == 0), stop=(kc == NCH - 1)),
                                reads=[wtb, outTb[kc][ts]], writes=[dpsb], inc=(kc == NCH - 1))
                        dt, dtb = dt_sb[(fc4 * 4 + ts) % 2]
                        if with_bias:
                            p.op("v", lambda e, dps=dps, dt=dt, fc=fc: e.tensor_scalar(
                                dt[:, :], dps[:, :], pb_sb[:, fc:fc + 1], None, ALU.add),
                                reads=[dpsb, pb_sbb], writes=[dtb])
                        else:
                            p.op("v", lambda e, dps=dps, dt=dt: e.tensor_copy(
                                dt[:, :], dps[:, :]),
                                reads=[dpsb], writes=[dtb])
                        p.dma("g", yT[fc * 128:(fc + 1) * 128, ts * TG:(ts + 1) * TG],
                              dt[:, :], reads=[dtb, ybufs[fc][ts]],
                              writes=[ybufs[fc][ts]], accum_op=ALU.add)
            p.finish()
            p.materialize(nc, block, stack)
    return nc


def _yarn_cos_sin():
    import math
    ROPE_DIM, MAX_POS, ORIG_POS, BASE = 64, 131072, 8192, 500000.0
    BETA_FAST, BETA_SLOW = 32.0, 1.0
    pos_freqs = BASE ** (np.arange(0, ROPE_DIM, 2, dtype=np.float64) / ROPE_DIM)
    extrap = 1.0 / pos_freqs
    interp = 1.0 / ((MAX_POS / ORIG_POS) * pos_freqs)
    low = max(math.floor(ROPE_DIM * math.log(ORIG_POS / (BETA_FAST * 2 * math.pi))
                         / (2 * math.log(BASE))), 0)
    high = min(math.ceil(ROPE_DIM * math.log(ORIG_POS / (BETA_SLOW * 2 * math.pi))
                         / (2 * math.log(BASE))), ROPE_DIM - 1)
    i = np.arange(ROPE_DIM // 2, dtype=np.float64)
    smooth = np.clip((i - low) / max(high - low, 1), 0.0, 1.0)
    inv_freq = ((1.0 - smooth) * interp + smooth * extrap).astype(np.float32)
    posv = np.arange(S, dtype=np.float32)
    freqs = posv[:, None] * inv_freq[None, :]
    emb = np.concatenate([freqs, freqs], axis=-1)
    mscale = 0.1 * math.log(MAX_POS / ORIG_POS) + 1.0
    cos = (np.cos(emb) * mscale).astype(np.float32)
    sin = (np.sin(emb) * mscale).astype(np.float32)
    cosx = np.concatenate([cos.T, np.ones((64, S), np.float32)], axis=0)
    sinx = np.concatenate([sin.T, np.zeros((64, S), np.float32)], axis=0)
    return np.ascontiguousarray(cosx), np.ascontiguousarray(sinx)


def _rot_np():
    P = np.zeros((128, 128), np.float32)
    for r in range(32):
        P[r, r + 32] = -1.0
    for r in range(32, 64):
        P[r, r - 32] = 1.0
    return np.ascontiguousarray(P.T)


def run_l0b(xTs, ins):
    cosT, sinT = _yarn_cos_sin()
    w = {
        "qwT": _rg(_np(ins["L0_attn_b_q_w"].T), NCH, NH, 128),
        "kvlT": _rg(_np(np.asarray(ins["L0_attn_b_kv_a_w"])[64:, :].T), NCH, 4, 128),
        "kvbkT": _rg(_np(np.asarray(ins["L0_attn_b_kv_b_w"]).reshape(NH, 2, 128, 512)[:, 0]
                     .transpose(2, 0, 1).reshape(512, D)), 4, NH, 128),
        "kvbvT": _rg(_np(np.asarray(ins["L0_attn_b_kv_b_w"]).reshape(NH, 2, 128, 512)[:, 1]
                     .transpose(2, 0, 1).reshape(512, D)), 4, 4, 512),
        "kvnw": _np(np.asarray(ins["L0_attn_b_kv_norm_w"]).reshape(4, 128).T),
        "rot_d": _rot_np(),
    }
    cn = _consts_np()
    nc = _get_nc("mla_prep", build_mla_prep)
    in_maps = []
    for c in range(NCORES):
        b, off = divmod(c, 4)
        in_maps.append({
            "xT": xTs[c],
            "cos_d": np.ascontiguousarray(cosT[:, off * T:(off + 1) * T]),
            "sin_d": np.ascontiguousarray(sinT[:, off * T:(off + 1) * T]),
            **w, **cn,
        })
    res = run_bass_kernel_spmd(nc, in_maps, core_ids=list(range(NCORES)))
    pre = [res.results[c] for c in range(NCORES)]
    kTfull = [np.concatenate([pre[b * 4 + j]["kT_o"] for j in range(4)], axis=1)
              for b in range(2)]
    vfull = [np.concatenate([pre[b * 4 + j]["v_o"] for j in range(4)], axis=1)
             for b in range(2)]
    knall = [np.concatenate([pre[b * 4 + j]["kn_o"] for j in range(4)], axis=1)
             for b in range(2)]
    owT = _rg(_np(ins["L0_attn_b_o_w"].T), NCH, 4, 512)
    nc2 = _get_nc("flash_nobias", lambda: build_flash(False))
    in_maps2 = []
    for c in range(NCORES):
        b = c // 4
        in_maps2.append({
            "qT": pre[c]["qT_o"], "kTf": kTfull[b], "vf": vfull[b],
            "qn": pre[c]["qn_o"], "kn_all": knall[b], "owT": owT,
            "yin": xTs[c], **cn,
        })
    res2 = run_bass_kernel_spmd(nc2, in_maps2, core_ids=list(range(NCORES)))
    return [res2.results[c]["yT"] for c in range(NCORES)]




def _emit_ln0_full(ctx, consts, ln, x_dram, y_bf, ybufs):
    for g in range(NTG):
        ln.emit(x_dram, g, y_bf, [ybufs[c][g] for c in range(NCH)], ctx.consts_,
                (ctx.pt[0], ctx.pb[0]), (ctx.pt[1], ctx.pb[1]),
                (ctx.pt[2], ctx.pb[2]), (ctx.pt[3], ctx.pb[3]),
                xn_off=g * TG, xn_stride=T)


def _emit_ln_rows_from_sbuf(ctx, consts, src_sb, src_bufs, g, rows, rowsb, roff,
                            eps=1e-5):
    p = ctx.p
    ones_col_bf, ones_col_bfb = consts["ones_col_bf"]
    ones_k1, ones_k1b = consts["ones_k1"]
    sum_t, sum_b = ctx.pt[0], ctx.pb[0]
    sq_t, sq_b = ctx.pt[1], ctx.pb[1]
    sq, sqb = ctx.ssm_sq
    T4 = NTG * TG

    def col(r):
        return r * T4 + g * TG

    for c in range(NCH):
        sl = src_sb[:, c * T + g * TG: c * T + g * TG + TG]
        p.op("c", lambda e, sl=sl: e.activation(sq[:, :], sl, AF.Square),
             reads=[src_bufs[c][g]], writes=[sqb])
        p.op("t", lambda e, sl=sl, c=c: e.matmul(
            sum_t[0:1, :], ones_col_bf[:, :], sl, start=(c == 0), stop=(c == NCH - 1)),
            reads=[src_bufs[c][g], ones_col_bfb], writes=[sum_b], inc=(c == NCH - 1))
        p.op("t", lambda e, c=c: e.matmul(
            sq_t[0:1, :], ones_col_bf[:, :], sq[:, :], start=(c == 0),
            stop=(c == NCH - 1), skip_group_check=True),
            reads=[sqb, ones_col_bfb], writes=[sq_b])
    A, B = roff, roff + 1
    p.op("v", lambda e: e.tensor_scalar(rows[0:1, col(A):col(A) + TG], sum_t[0:1, :],
                                        1.0 / D, None, ALU.mult),
         reads=[sum_b], writes=[rowsb])
    p.op("v", lambda e: e.tensor_scalar(rows[0:1, col(B):col(B) + TG], sq_t[0:1, :],
                                        1.0 / D, None, ALU.mult),
         reads=[sq_b], writes=[rowsb])
    t4 = ctx.ssm_trow
    p.op("v", lambda e: e.scalar_tensor_tensor(
        t4[0][0:1, 0:TG], rows[0:1, col(A):col(A) + TG], -1.0,
        rows[0:1, col(A):col(A) + TG], ALU.mult, ALU.mult),
        reads=[rowsb], writes=[t4[1]])
    p.op("v", lambda e: e.scalar_tensor_tensor(
        t4[0][0:1, 0:TG], rows[0:1, col(B):col(B) + TG], float(eps),
        t4[0][0:1, 0:TG], ALU.add, ALU.add),
        reads=[rowsb, t4[1]], writes=[t4[1]])
    p.op("c", lambda e: e.activation(t4[0][0:1, 0:TG], t4[0][0:1, 0:TG], AF.Sqrt),
         reads=[t4[1]], writes=[t4[1]])
    p.op("v", lambda e: e.reciprocal(t4[0][0:1, TG:2 * TG], t4[0][0:1, 0:TG]),
         reads=[t4[1]], writes=[t4[1]])
    p.op("v", lambda e: e.scalar_tensor_tensor(
        rows[0:1, col(B):col(B) + TG], rows[0:1, col(A):col(A) + TG], -1.0,
        t4[0][0:1, TG:2 * TG], ALU.mult, ALU.mult),
        reads=[rowsb, t4[1]], writes=[rowsb])
    p.op("v", lambda e: e.tensor_copy(rows[0:1, col(A):col(A) + TG],
                                      t4[0][0:1, TG:2 * TG]),
         reads=[t4[1]], writes=[rowsb])


def _bcast_rows(ctx, consts, rows, rowsb, roff, g, ps_a, ps_b):
    p = ctx.p
    ones_k1, ones_k1b = consts["ones_k1"]
    T4 = NTG * TG
    a_t, a_b = ps_a
    b_t, b_b = ps_b
    p.op("t", lambda e: e.matmul(a_t[:, :], ones_k1[:, :],
                                 rows[0:1, roff * T4 + g * TG: roff * T4 + g * TG + TG],
                                 start=True, stop=True),
         reads=[rowsb, ones_k1b], writes=[a_b])
    p.op("t", lambda e: e.matmul(b_t[:, :], ones_k1[:, :],
                                 rows[0:1, (roff + 1) * T4 + g * TG:
                                      (roff + 1) * T4 + g * TG + TG],
                                 start=True, stop=True),
         reads=[rowsb, ones_k1b], writes=[b_b])


def _ssm_common_tensors(ctx, nc):
    ctx.ssm_sq = ctx.sbuf([128, TG], BF16, "ssm_sq")
    ctx.ssm_trow = ctx.sbuf([1, 2 * TG], F32, "ssm_trow")
    ctx.ssm_y = (ctx.sb([128, NCH * T], BF16, "ssm_y"),
                 [[Buf(f"y{c}_{g}") for g in range(NTG)] for c in range(NCH)])
    ctx.ssm_rows = ctx.sbuf([1, 4 * NTG * TG], F32, "ssm_rows")
    ctx.ssm_xt1 = [ctx.sbuf([128, TG], BF16, f"ssm_xt1_{i}") for i in range(2)]
    ctx.ssm_t32 = [ctx.sbuf([128, TG], F32, f"ssm_t32_{i}") for i in range(3)]
    ctx.ssm_g = ctx.sbuf([128, NCH], F32, "ssm_g")
    ctx.ssm_bt = ctx.sbuf([128, NCH], F32, "ssm_bt")
    ctx.ssm_u = ctx.sbuf([8, T], F32, "ssm_u")
    ctx.ssm_db = ctx.sbuf([8, T], F32, "ssm_dbb")
    ctx.ssm_h = ctx.sbuf([8, T], F32, "ssm_h")
    ctx.ssm_small = ctx.sbuf([8, 16], F32, "ssm_small")
    ctx.ssm_winT = ctx.sbuf([128, NCH * 8], BF16, "ssm_winT")


def _emit_xn1_tile(ctx, consts, c, g, a_ps, b_ps, dst, dstb, src_sb, src_bufs):
    p = ctx.p
    t32, t32b = ctx.ssm_t32[0]
    gcol, gcolb = ctx.ssm_gb
    bcol, bcolb = ctx.ssm_btb
    sl = src_sb[:, c * T + g * TG: c * T + g * TG + TG]
    p.op("v", lambda e, sl=sl: e.tensor_mul(t32[:, :], sl, a_ps[0][:, :]),
         reads=[src_bufs[c][g], a_ps[1]], writes=[t32b])
    p.op("v", lambda e: e.tensor_add(t32[:, :], t32[:, :], b_ps[0][:, :]),
         reads=[t32b, b_ps[1]], writes=[t32b])
    p.op("v", lambda e, c=c: e.tensor_scalar(
        dst, t32[:, :], ctx.ssm_g[0][:, c:c + 1], ctx.ssm_bt[0][:, c:c + 1],
        ALU.mult, ALU.add),
        reads=[t32b, ctx.ssm_gb2, ctx.ssm_btb2],
        writes=[dstb])


def build_ssm_a():
    nc = bass.Bass("TRN2", target_bir_lowering=False)
    xT = nc.dram_tensor("xT", [D, T], F32, kind="ExternalInput").ap()
    u_o = nc.dram_tensor("u_o", [8, T], F32, kind="ExternalOutput").ap()
    hend_o = nc.dram_tensor("hend_o", [8, 1], F32, kind="ExternalOutput").ap()
    winT = nc.dram_tensor("winT", [D, 8], F32, kind="ExternalInput").ap()
    b_in_d = nc.dram_tensor("b_in_d", [8, 1], F32, kind="ExternalInput").ap()
    dl_d = nc.dram_tensor("dl_d", [8, 1], F32, kind="ExternalInput").ap()
    gam_d = nc.dram_tensor("gam_d", [128, NCH], F32, kind="ExternalInput").ap()
    bet_d = nc.dram_tensor("bet_d", [128, NCH], F32, kind="ExternalInput").ap()
    ones_col = nc.dram_tensor("ones_col", [128, 1], F32, kind="ExternalInput").ap()
    ones_k1 = nc.dram_tensor("ones_k1", [1, 128], F32, kind="ExternalInput").ap()
    ident = nc.dram_tensor("ident", [128, 128], F32, kind="ExternalInput").ap()
    dram = {"ones_col": ones_col, "ones_k1": ones_k1, "ident": ident}

    p = Prog()
    with contextlib.ExitStack() as stack:
        ctx = Ctx(nc, stack, p)
        consts = make_consts(ctx, p, dram)
        ctx.consts_ = consts
        ln = LNBlock(ctx)
        _ssm_common_tensors(ctx, nc)
        ctx.ssm_gb2 = Buf("gam")
        ctx.ssm_btb2 = Buf("bet")
        ctx.ssm_gb = (ctx.ssm_g[0], ctx.ssm_gb2)
        ctx.ssm_btb = (ctx.ssm_bt[0], ctx.ssm_btb2)
        y_bf, ybufs = ctx.ssm_y
        rows, rowsb_ = ctx.ssm_rows
        rowsb = rowsb_
        u_sb, u_b = ctx.ssm_u
        sm, smb = ctx.ssm_small
        winT_sb, winTb = ctx.ssm_winT

        p.dma("s", ctx.ssm_g[0][:, :], gam_d, writes=[ctx.ssm_gb2])
        p.dma("s", ctx.ssm_bt[0][:, :], bet_d, writes=[ctx.ssm_btb2])
        p.dma("s", sm[0:8, 0:1], b_in_d, writes=[smb])
        p.dma("s", sm[0:8, 1:2], dl_d, writes=[smb])
        p.dma("g", winT_sb[:, :].rearrange("p (kc f) -> p kc f", kc=NCH),
              winT.rearrange("(kc p) f -> p kc f", p=128), writes=[winTb])

        with nc.Block() as block:
            _emit_ln0_full(ctx, consts, ln, xT, y_bf, ybufs)
            for g in range(NTG):
                _emit_ln_rows_from_sbuf(ctx, consts, y_bf, ybufs, g, rows, rowsb, 0)
            u_ps, u_psb = ctx.pt[6], ctx.pb[6]
            for g in range(NTG):
                _bcast_rows(ctx, consts, rows, rowsb, 0, g,
                            (ctx.pt[2], ctx.pb[2]), (ctx.pt[3], ctx.pb[3]))
                for kc in range(NCH):
                    xt1, xt1b = ctx.ssm_xt1[kc % 2]
                    _emit_xn1_tile(ctx, consts, kc, g, (ctx.pt[2], ctx.pb[2]),
                                   (ctx.pt[3], ctx.pb[3]), xt1[:, :], xt1b,
                                   y_bf, ybufs)
                    p.op("t", lambda e, kc=kc, xt1=xt1, g=g: e.matmul(
                        u_ps[0:8, :], winT_sb[:, kc * 8:(kc + 1) * 8], xt1[:, :],
                        start=(kc == 0), stop=(kc == NCH - 1)),
                        reads=[winTb, xt1b], writes=[u_psb], inc=(kc % 2 == 1))
                p.op("v", lambda e, g=g: e.tensor_scalar(
                    u_sb[0:8, g * TG:(g + 1) * TG], u_ps[0:8, :], sm[0:8, 0:1],
                    None, ALU.add),
                    reads=[u_psb, smb], writes=[u_b])
            p.op("c", lambda e: e.activation(sm[0:8, 2:3], sm[0:8, 1:2], AF.Sigmoid),
                 reads=[smb], writes=[smb])
            db_sb, db_b = ctx.ssm_db
            p.op("v", lambda e: e.memset(db_sb[0:8, :], 1.0), writes=[db_b])
            p.op("v", lambda e: e.tensor_scalar(db_sb[0:8, :], db_sb[0:8, :],
                                                sm[0:8, 2:3], None, ALU.mult),
                 reads=[smb, db_b], writes=[db_b])
            h_sb, h_b = ctx.ssm_h
            p.op("v", lambda e: e.tensor_tensor_scan(
                h_sb[0:8, :], db_sb[0:8, :], u_sb[0:8, :], 0.0, ALU.mult, ALU.add),
                reads=[db_b, u_b], writes=[h_b])
            p.dma("g", u_o, u_sb[0:8, :], reads=[u_b])
            p.dma("g", hend_o, h_sb[0:8, T - 1:T], reads=[h_b])
            p.finish()
            p.materialize(nc, block, stack)
    return nc


def build_ssm_b():
    nc = bass.Bass("TRN2", target_bir_lowering=False)
    xT = nc.dram_tensor("xT", [D, T], F32, kind="ExternalInput").ap()
    yT = nc.dram_tensor("yT", [D, T], F32, kind="ExternalOutput").ap()
    u_d = nc.dram_tensor("u_d", [8, T], F32, kind="ExternalInput").ap()
    hends_d = nc.dram_tensor("hends_d", [8, 4], F32, kind="ExternalInput").ap()
    expv_d = nc.dram_tensor("expv_d", [8, 4], F32, kind="ExternalInput").ap()
    iota_d = nc.dram_tensor("iota_d", [8, T], F32, kind="ExternalInput").ap()
    winT = nc.dram_tensor("winT", [D, 8], F32, kind="ExternalInput").ap()
    b_in_d = nc.dram_tensor("b_in_d", [8, 1], F32, kind="ExternalInput").ap()
    dl_d = nc.dram_tensor("dl_d", [8, 1], F32, kind="ExternalInput").ap()
    wout_d = nc.dram_tensor("wout_d", [8, D], F32, kind="ExternalInput").ap()
    bout_d = nc.dram_tensor("bout_d", [128, NCH], F32, kind="ExternalInput").ap()
    dskip_d = nc.dram_tensor("dskip_d", [128, NCH], F32, kind="ExternalInput").ap()
    wdn_d = nc.dram_tensor("wdn_d", [D, 32], F32, kind="ExternalInput").ap()
    bdn_d = nc.dram_tensor("bdn_d", [32, 1], F32, kind="ExternalInput").ap()
    wup_d = nc.dram_tensor("wup_d", [32, D], F32, kind="ExternalInput").ap()
    bup_d = nc.dram_tensor("bup_d", [128, NCH], F32, kind="ExternalInput").ap()
    gam_d = nc.dram_tensor("gam_d", [128, NCH], F32, kind="ExternalInput").ap()
    bet_d = nc.dram_tensor("bet_d", [128, NCH], F32, kind="ExternalInput").ap()
    ones_col = nc.dram_tensor("ones_col", [128, 1], F32, kind="ExternalInput").ap()
    ones_k1 = nc.dram_tensor("ones_k1", [1, 128], F32, kind="ExternalInput").ap()
    ident = nc.dram_tensor("ident", [128, 128], F32, kind="ExternalInput").ap()
    dram = {"ones_col": ones_col, "ones_k1": ones_k1, "ident": ident}

    p = Prog()
    with contextlib.ExitStack() as stack:
        ctx = Ctx(nc, stack, p)
        consts = make_consts(ctx, p, dram)
        ctx.consts_ = consts
        ln = LNBlock(ctx)
        _ssm_common_tensors(ctx, nc)
        ctx.ssm_gb2 = Buf("gam")
        ctx.ssm_btb2 = Buf("bet")
        ctx.ssm_gb = (ctx.ssm_g[0], ctx.ssm_gb2)
        ctx.ssm_btb = (ctx.ssm_bt[0], ctx.ssm_btb2)
        y_bf, ybufs = ctx.ssm_y
        rows, rowsb = ctx.ssm_rows
        u_sb, u_b = ctx.ssm_u
        sm, smb = ctx.ssm_small
        winT_sb, winTb = ctx.ssm_winT
        bcols, bcolsb = ctx.sbuf([128, 4 * NCH], F32, "ssm_bcols")
        wo_sb, wo_b = ctx.sbuf([8, D], BF16, "ssm_wo")
        wd_sb, wd_b = ctx.sbuf([128, NCH * 32], BF16, "ssm_wd")
        wu_sb, wu_b = ctx.sbuf([32, D], BF16, "ssm_wu")
        bdn_sb, bdn_b = ctx.sbuf([32, 1], F32, "ssm_bdn")
        hf_sb, hf_b = ctx.sbuf([32, TG], BF16, "ssm_hf")
        he_sb, he_b = ctx.sbuf([8, 8], F32, "ssm_he")
        h_bf, h_bfb = ctx.sbuf([8, T], BF16, "ssm_hbf")
        xt_f = [ctx.sbuf([128, TG], F32, f"ssm_xf{i}") for i in range(2)]

        p.dma("s", ctx.ssm_g[0][:, :], gam_d, writes=[ctx.ssm_gb2])
        p.dma("s", ctx.ssm_bt[0][:, :], bet_d, writes=[ctx.ssm_btb2])
        p.dma("s", sm[0:8, 0:1], b_in_d, writes=[smb])
        p.dma("s", sm[0:8, 1:2], dl_d, writes=[smb])
        p.dma("s", u_sb[0:8, :], u_d, writes=[u_b])
        p.dma("s", he_sb[0:8, 0:4], hends_d, writes=[he_b])
        p.dma("s", he_sb[0:8, 4:8], expv_d, writes=[he_b])
        iota_sb, iota_b = ctx.sbuf([8, T], F32, "ssm_iota")
        p.dma("s", iota_sb[0:8, :], iota_d, writes=[iota_b])
        p.dma("g", wo_sb[0:8, :], wout_d, writes=[wo_b])
        p.dma("g", wd_sb[:, :].rearrange("p (kc f) -> p kc f", kc=NCH),
              wdn_d.rearrange("(kc p) f -> p kc f", p=128), writes=[wd_b])
        p.dma("g", wu_sb[0:32, :], wup_d, writes=[wu_b])
        p.dma("s", bdn_sb[0:32, :], bdn_d, writes=[bdn_b])
        p.dma("s", bcols[:, 0:NCH], bout_d, writes=[bcolsb])
        p.dma("s", bcols[:, NCH:2 * NCH], dskip_d, writes=[bcolsb])
        p.dma("s", bcols[:, 2 * NCH:3 * NCH], bup_d, writes=[bcolsb])

        with nc.Block() as block:
            p.op("c", lambda e: e.activation(sm[0:8, 2:3], sm[0:8, 1:2], AF.Sigmoid),
                 reads=[smb], writes=[smb])
            p.op("c", lambda e: e.activation(sm[0:8, 3:4], sm[0:8, 2:3], AF.Ln),
                 reads=[smb], writes=[smb])
            db_sb, db_b = ctx.ssm_db
            p.op("v", lambda e: e.memset(db_sb[0:8, :], 1.0), writes=[db_b])
            p.op("v", lambda e: e.tensor_scalar(db_sb[0:8, :], db_sb[0:8, :],
                                                sm[0:8, 2:3], None, ALU.mult),
                 reads=[smb, db_b], writes=[db_b])
            h_sb, h_b = ctx.ssm_h
            p.op("v", lambda e: e.tensor_tensor_scan(
                h_sb[0:8, :], db_sb[0:8, :], u_sb[0:8, :], 0.0, ALU.mult, ALU.add),
                reads=[db_b, u_b], writes=[h_b])
            p.op("v", lambda e: e.tensor_scalar(he_sb[0:8, 4:8], he_sb[0:8, 4:8],
                                                sm[0:8, 3:4], None, ALU.mult),
                 reads=[smb, he_b], writes=[he_b])
            p.op("c", lambda e: e.activation(he_sb[0:8, 4:8], he_sb[0:8, 4:8], AF.Exp),
                 reads=[he_b], writes=[he_b])
            p.op("v", lambda e: e.tensor_mul(he_sb[0:8, 4:8], he_sb[0:8, 4:8],
                                             he_sb[0:8, 0:4]),
                 reads=[he_b], writes=[he_b])
            p.op("v", lambda e: e.reduce_sum(sm[0:8, 4:5], he_sb[0:8, 4:8], AX.X),
                 reads=[he_b], writes=[smb])
            p.op("v", lambda e: e.tensor_scalar(iota_sb[0:8, :], iota_sb[0:8, :],
                                                sm[0:8, 3:4], None, ALU.mult),
                 reads=[smb, iota_b], writes=[iota_b])
            p.op("c", lambda e: e.activation(iota_sb[0:8, :], iota_sb[0:8, :], AF.Exp),
                 reads=[iota_b], writes=[iota_b])
            p.op("v", lambda e: e.tensor_scalar(iota_sb[0:8, :], iota_sb[0:8, :],
                                                sm[0:8, 4:5], None, ALU.mult),
                 reads=[smb, iota_b], writes=[iota_b])
            p.op("v", lambda e: e.tensor_add(h_sb[0:8, :], h_sb[0:8, :], iota_sb[0:8, :]),
                 reads=[iota_b, h_b], writes=[h_b])
            p.op("v", lambda e: e.tensor_copy(h_bf[0:8, :], h_sb[0:8, :]),
                 reads=[h_b], writes=[h_bfb])
            _emit_ln0_full(ctx, consts, ln, xT, y_bf, ybufs)
            for g in range(NTG):
                _emit_ln_rows_from_sbuf(ctx, consts, y_bf, ybufs, g, rows, rowsb, 0)
            ar_ps = [(ctx.pt[4], ctx.pb[4]), (ctx.pt[5], ctx.pb[5])]
            for g in range(NTG):
                _bcast_rows(ctx, consts, rows, rowsb, 0, g,
                            (ctx.pt[2], ctx.pb[2]), (ctx.pt[3], ctx.pb[3]))
                for fc in range(NCH):
                    aps, apsb = ar_ps[fc % 2]
                    p.op("t", lambda e, fc=fc, g=g, aps=aps: e.matmul(
                        aps[:, :], wo_sb[0:8, fc * 128:(fc + 1) * 128],
                        h_bf[0:8, g * TG:(g + 1) * TG], start=True, stop=True),
                        reads=[wo_b, h_bfb], writes=[apsb])
                    xt1, xt1b = ctx.ssm_xt1[fc % 2]
                    _emit_xn1_tile(ctx, consts, fc, g, (ctx.pt[2], ctx.pb[2]),
                                   (ctx.pt[3], ctx.pb[3]), xt1[:, :], xt1b,
                                   y_bf, ybufs)
                    t0, t0b = ctx.ssm_t32[2]
                    p.op("v", lambda e, fc=fc, xt1=xt1, aps=aps, t0=t0: e.tensor_scalar(
                        t0[:, :], aps[:, :], bcols[:, fc:fc + 1], None, ALU.add),
                        reads=[apsb, bcolsb], writes=[t0b])
                    p.op("v", lambda e, fc=fc, xt1=xt1, t0=t0: e.scalar_tensor_tensor(
                        t0[:, :], xt1[:, :], 1.0, t0[:, :], ALU.bypass, ALU.add)
                        if False else e.tensor_scalar(
                        xt1[:, :], xt1[:, :], bcols[:, NCH + fc:NCH + fc + 1],
                        None, ALU.mult),
                        reads=[xt1b, bcolsb], writes=[xt1b])
                    p.op("v", lambda e, t0=t0, xt1=xt1: e.tensor_add(
                        t0[:, :], t0[:, :], xt1[:, :]),
                        reads=[t0b, xt1b], writes=[t0b])
                    p.op("v", lambda e, fc=fc, g=g, t0=t0: e.tensor_add(
                        y_bf[:, fc * T + g * TG: fc * T + g * TG + TG],
                        y_bf[:, fc * T + g * TG: fc * T + g * TG + TG], t0[:, :]),
                        reads=[t0b, ybufs[fc][g]], writes=[ybufs[fc][g]])
            for g in range(NTG):
                _emit_ln_rows_from_sbuf(ctx, consts, y_bf, ybufs, g, rows, rowsb, 2)
            for g in range(NTG):
                _bcast_rows(ctx, consts, rows, rowsb, 2, g,
                            (ctx.pt[2], ctx.pb[2]), (ctx.pt[3], ctx.pb[3]))
                hf_ps, hf_psb = ctx.pt[6], ctx.pb[6]
                for kc in range(NCH):
                    xt1, xt1b = ctx.ssm_xt1[kc % 2]
                    _emit_xn1_tile(ctx, consts, kc, g, (ctx.pt[2], ctx.pb[2]),
                                   (ctx.pt[3], ctx.pb[3]), xt1[:, :], xt1b,
                                   y_bf, ybufs)
                    p.op("t", lambda e, kc=kc, xt1=xt1: e.matmul(
                        hf_ps[0:32, :], wd_sb[:, kc * 32:(kc + 1) * 32], xt1[:, :],
                        start=(kc == 0), stop=(kc == NCH - 1)),
                        reads=[wd_b, xt1b], writes=[hf_psb], inc=(kc % 2 == 1))
                p.op("c", lambda e: e.activation(hf_sb[0:32, :], hf_ps[0:32, :],
                                                 AF.Relu, bias=bdn_sb[0:32, 0:1]),
                     reads=[hf_psb, bdn_b], writes=[hf_b])
                up_ps = [(ctx.pt[4], ctx.pb[4]), (ctx.pt[5], ctx.pb[5])]
                for fc in range(NCH):
                    ups, upsb = up_ps[fc % 2]
                    p.op("t", lambda e, fc=fc, ups=ups: e.matmul(
                        ups[:, :], wu_sb[0:32, fc * 128:(fc + 1) * 128], hf_sb[0:32, :],
                        start=True, stop=True),
                        reads=[wu_b, hf_b], writes=[upsb])
                    xf, xfb = xt_f[fc % 2]
                    p.dma("s", xf[:, :], xT[fc * 128:(fc + 1) * 128,
                                            g * TG:(g + 1) * TG], writes=[xfb])
                    t0, t0b = ctx.ssm_t32[2]
                    p.op("v", lambda e, fc=fc, ups=ups, t0=t0: e.tensor_scalar(
                        t0[:, :], ups[:, :], bcols[:, 2 * NCH + fc:2 * NCH + fc + 1],
                        None, ALU.add),
                        reads=[upsb, bcolsb], writes=[t0b])
                    p.op("v", lambda e, fc=fc, g=g, t0=t0: e.tensor_add(
                        t0[:, :], t0[:, :],
                        y_bf[:, fc * T + g * TG: fc * T + g * TG + TG]),
                        reads=[t0b, ybufs[fc][g]], writes=[t0b])
                    p.op("v", lambda e, xf=xf, t0=t0: e.tensor_add(
                        xf[:, :], xf[:, :], t0[:, :]),
                        reads=[t0b, xfb], writes=[xfb])
                    p.dma("s", yT[fc * 128:(fc + 1) * 128, g * TG:(g + 1) * TG],
                          xf[:, :], reads=[xfb])
            p.finish()
            p.materialize(nc, block, stack)
    return nc


def run_ssm(xTs, ins):
    w_a = {
        "winT": _np(ins["L1_attn_a_W_in"]),
        "b_in_d": _np(np.asarray(ins["L1_attn_a_b_in"]).reshape(8, 1)),
        "dl_d": _np(np.asarray(ins["L1_attn_a_decay_logit"]).reshape(8, 1)),
        "gam_d": _np(np.asarray(ins["L1_attn_a_ln_gamma"]).reshape(NCH, 128).T),
        "bet_d": _np(np.asarray(ins["L1_attn_a_ln_beta"]).reshape(NCH, 128).T),
    }
    cn = _consts_np()
    nc = _get_nc("ssm_a", build_ssm_a)
    in_maps = [{"xT": xTs[c], **w_a, **cn} for c in range(NCORES)]
    res = run_bass_kernel_spmd(nc, in_maps, core_ids=list(range(NCORES)))
    pre = [res.results[c] for c in range(NCORES)]
    hends = [np.concatenate([pre[b * 4 + j]["hend_o"] for j in range(4)], axis=1)
             for b in range(2)]
    iota = np.broadcast_to(np.arange(1, T + 1, dtype=np.float32), (8, T)).copy()
    w_b = {
        "winT": w_a["winT"], "b_in_d": w_a["b_in_d"], "dl_d": w_a["dl_d"],
        "gam_d": w_a["gam_d"], "bet_d": w_a["bet_d"],
        "wout_d": _np(ins["L1_attn_a_W_out"]),
        "bout_d": _np(np.asarray(ins["L1_attn_a_b_out"]).reshape(NCH, 128).T),
        "dskip_d": _np(np.asarray(ins["L1_attn_a_D_skip"]).reshape(NCH, 128).T),
        "wdn_d": _np(ins["L1_attn_a_W_down"]),
        "bdn_d": _np(np.asarray(ins["L1_attn_a_b_down"]).reshape(32, 1)),
        "wup_d": _np(ins["L1_attn_a_W_up"]),
        "bup_d": _np(np.asarray(ins["L1_attn_a_b_up"]).reshape(NCH, 128).T),
        "iota_d": iota,
    }
    nc2 = _get_nc("ssm_b", build_ssm_b)
    in_maps2 = []
    for c in range(NCORES):
        b, ci = divmod(c, 4)
        expv = np.full((8, 4), 1e9, np.float32)
        for j in range(ci):
            expv[:, j] = float(T * (ci - 1 - j))
        in_maps2.append({"xT": xTs[c], "u_d": pre[c]["u_o"], "hends_d": hends[b],
                         "expv_d": expv, **w_b, **cn})
    res2 = run_bass_kernel_spmd(nc2, in_maps2, core_ids=list(range(NCORES)))
    return [res2.results[c]["yT"] for c in range(NCORES)]



def build_mha_prep():
    nc = bass.Bass("TRN2", target_bir_lowering=False)
    xT = nc.dram_tensor("xT", [D, T], F32, kind="ExternalInput").ap()
    qT_o = nc.dram_tensor("qT_o", [D, T], BF16, kind="ExternalOutput").ap()
    kT_o = nc.dram_tensor("kT_o", [D, T], BF16, kind="ExternalOutput").ap()
    v_o = nc.dram_tensor("v_o", [NH * 128, NCH * 128], BF16, kind="ExternalOutput").ap()
    qn_o = nc.dram_tensor("qn_o", [1, NH], F32, kind="ExternalOutput").ap()
    kn_o = nc.dram_tensor("kn_o", [1, NH], F32, kind="ExternalOutput").ap()
    wqkvT = nc.dram_tensor("wqkvT", [32 * 128, D], F32, kind="ExternalInput").ap()
    wqkvV = nc.dram_tensor("wqkvV", [4 * 128, NCH * 512], F32, kind="ExternalInput").ap()
    qkb = nc.dram_tensor("qkb", [128, 32], F32, kind="ExternalInput").ap()
    vbr = nc.dram_tensor("vbr", [1, D], F32, kind="ExternalInput").ap()
    ones_col = nc.dram_tensor("ones_col", [128, 1], F32, kind="ExternalInput").ap()
    ones_k1 = nc.dram_tensor("ones_k1", [1, 128], F32, kind="ExternalInput").ap()
    ident = nc.dram_tensor("ident", [128, 128], F32, kind="ExternalInput").ap()
    dram = {"ones_col": ones_col, "ones_k1": ones_k1, "ident": ident}

    p = Prog()
    with contextlib.ExitStack() as stack:
        ctx = Ctx(nc, stack, p)
        consts = make_consts(ctx, p, dram)
        ln = LNBlock(ctx)
        xn_sb = ctx.sb([128, NCH * T], BF16, "xn")
        xnb = [[Buf(f"xn{c}_{g}") for g in range(NTG)] for c in range(NCH)]
        ctx.ns_sq = ctx.sbuf([128, TG], BF16, "ns_sq")
        ctx.ns_tmp = ctx.sbuf([1, 2], F32, "ns_tmp")
        qn_sb, qnb = ctx.sbuf([1, NH], F32, "qn_sb")
        kn_sb, knb = ctx.sbuf([1, NH], F32, "kn_sb")
        wt_s = [ctx.sbuf([128, NCH * 128], BF16, f"mqk{i}") for i in range(2)]
        wv_s = [ctx.sbuf([128, NCH * 512], BF16, f"mwv{i}") for i in range(2)]
        qkb_sb, qkbb = ctx.sbuf([128, 32], F32, "qkb_sb")
        vb_bc, vbcb = ctx.sbuf([128, 4 * 512], F32, "vb_bc")
        otile = [ctx.sbuf([128, TG], BF16, f"motile{i}") for i in range(2)]

        p.dma("s", qkb_sb[:, :], qkb, writes=[qkbb])

        with nc.Block() as block:
            for g in range(NTG):
                ln.emit(xT, g, xn_sb, [xnb[c][g] for c in range(NCH)], consts,
                        (ctx.pt[0], ctx.pb[0]), (ctx.pt[1], ctx.pb[1]),
                        (ctx.pt[2], ctx.pb[2]), (ctx.pt[3], ctx.pb[3]),
                        xn_off=g * TG, xn_stride=T)
            vbr_sb, vbrb = ctx.sbuf([1, D], F32, "vbr_sb")
            p.dma("s", vbr_sb[:, :], vbr, writes=[vbrb])
            bb_ps, bb_psb = ctx.pt[6], ctx.pb[6]
            for vfg in range(4):
                p.op("t", lambda e, vfg=vfg: e.matmul(
                    bb_ps[:, :], consts["ones_k1"][0][:, :],
                    vbr_sb[0:1, vfg * 512:(vfg + 1) * 512], start=True, stop=True),
                    reads=[vbrb, consts["ones_k1"][1]], writes=[bb_psb])
                p.op("v", lambda e, vfg=vfg: e.tensor_copy(
                    vb_bc[:, vfg * 512:(vfg + 1) * 512], bb_ps[:, :]),
                    reads=[bb_psb], writes=[vbcb])
            q_ps = [(ctx.pt[0], ctx.pb[0]), (ctx.pt[1], ctx.pb[1])]
            for qk in range(2):
                for h in range(NH):
                    wt, wtb = wt_s[h % 2]
                    row = (qk * NH + h) * 128
                    p.dma("g", wt[:, :], wqkvT[row:row + 128, :], writes=[wtb])
                    for g in range(NTG):
                        qp, qpb = q_ps[g % 2]
                        for kc in range(NCH):
                            p.op("t", lambda e, kc=kc, wt=wt, qp=qp, g=g: e.matmul(
                                qp[:, :], wt[:, kc * 128:(kc + 1) * 128],
                                xn_sb[:, kc * T + g * TG: kc * T + g * TG + TG],
                                start=(kc == 0), stop=(kc == NCH - 1)),
                                reads=[wtb, xnb[kc][g]], writes=[qpb], inc=(kc == NCH - 1))
                        ot, otb = otile[g % 2]
                        bcol = qk * NH + h
                        p.op("v", lambda e, qp=qp, ot=ot, bcol=bcol: e.tensor_scalar(
                            ot[:, :], qp[:, :], qkb_sb[:, bcol:bcol + 1], None, ALU.add),
                            reads=[qpb, qkbb], writes=[otb])
                        nrow, nrowb2 = (qn_sb, qnb) if qk == 0 else (kn_sb, knb)
                        emit_norm_stat(ctx, consts, ot[:, :], otb, nrow, nrowb2,
                                       h, g == 0)
                        dst = qT_o if qk == 0 else kT_o
                        p.dma("g", dst[h * 128:(h + 1) * 128, g * TG:(g + 1) * TG],
                              ot[:, :], reads=[otb])
            for vfg in range(4):
                wv, wvb = wv_s[vfg % 2]
                p.dma("g", wv[:, :], wqkvV[vfg * 128:(vfg + 1) * 128, :], writes=[wvb])
                for g in range(NTG):
                    for tsub in range(4):
                        vp, vpb = q_ps[tsub % 2]
                        for kc in range(NCH):
                            p.op("t", lambda e, kc=kc, wv=wv, vp=vp, g=g,
                                 tsub=tsub: e.matmul(
                                vp[:, :],
                                xn_sb[:, kc * T + g * TG + tsub * 128:
                                      kc * T + g * TG + (tsub + 1) * 128],
                                wv[:, kc * 512:(kc + 1) * 512],
                                start=(kc == 0), stop=(kc == NCH - 1)),
                                reads=[wvb, xnb[kc][g]], writes=[vpb], inc=(kc == NCH - 1))
                        ot, otb = otile[tsub % 2]
                        p.op("v", lambda e, vp=vp, ot=ot, vfg=vfg: e.tensor_add(
                            ot[:, :], vp[:, :], vb_bc[:, vfg * 512:(vfg + 1) * 512]),
                            reads=[vpb, vbcb], writes=[otb])
                        p.dma("g",
                              v_o.rearrange("(h p) (kc j) -> p h kc j", p=128, j=128)
                              [:, vfg * 4:(vfg + 1) * 4, g * 4 + tsub, :],
                              ot[:, :].rearrange("p (hs j) -> p hs j", hs=4),
                              reads=[otb])
            p.dma("g", qn_o, qn_sb[:, :], reads=[qnb])
            p.dma("g", kn_o, kn_sb[:, :], reads=[knb])
            p.finish()
            p.materialize(nc, block, stack)
    return nc


def run_l1b(xTs, ins):
    wqkv = np.asarray(ins["L1_attn_b_Wqkv_w"])
    bqkv = np.asarray(ins["L1_attn_b_Wqkv_b"])
    wqkvTf = _np(wqkv.T)
    w = {
        "wqkvT": _rg(wqkvTf[:, :4096], NCH, 32, 128),
        "wqkvV": _rg(wqkvTf[:, 4096:], NCH, 4, 512),
        "qkb": _np(bqkv[:4096].reshape(32, 128).T),
        "vbr": _np(bqkv[4096:].reshape(1, D)),
    }
    cn = _consts_np()
    nc = _get_nc("mha_prep", build_mha_prep)
    in_maps = [{"xT": xTs[c], **w, **cn} for c in range(NCORES)]
    res = run_bass_kernel_spmd(nc, in_maps, core_ids=list(range(NCORES)))
    pre = [res.results[c] for c in range(NCORES)]
    kTfull = [np.concatenate([pre[b * 4 + j]["kT_o"] for j in range(4)], axis=1)
              for b in range(2)]
    vfull = [np.concatenate([pre[b * 4 + j]["v_o"] for j in range(4)], axis=1)
             for b in range(2)]
    knall = [np.concatenate([pre[b * 4 + j]["kn_o"] for j in range(4)], axis=1)
             for b in range(2)]
    owT = _rg(_np(ins["L1_attn_b_proj_w"].T), NCH, 4, 512)
    pbias = _np(np.asarray(ins["L1_attn_b_proj_b"]).reshape(NCH, 128).T)
    nc2 = _get_nc("flash_bias", lambda: build_flash(True))
    in_maps2 = []
    for c in range(NCORES):
        b = c // 4
        in_maps2.append({
            "qT": pre[c]["qT_o"], "kTf": kTfull[b], "vf": vfull[b],
            "qn": pre[c]["qn_o"], "kn_all": knall[b], "owT": owT,
            "pbias": pbias, "yin": xTs[c], **cn,
        })
    res2 = run_bass_kernel_spmd(nc2, in_maps2, core_ids=list(range(NCORES)))
    return [res2.results[c]["yT"] for c in range(NCORES)]



_NC_CACHE = {}

def _get_nc(name, builder):
    if name not in _NC_CACHE:
        _NC_CACHE[name] = builder()
    return _NC_CACHE[name]


def _np(x):
    return np.ascontiguousarray(np.asarray(x), dtype=np.float32)


def _consts_np():
    return {
        "ones_col": np.ones((128, 1), np.float32),
        "ones_k1": np.ones((1, 128), np.float32),
        "ident": np.eye(128, dtype=np.float32),
    }


def run_l0a(xTs, w):
    nc = _get_nc("l0a", build_l0a)
    cn = _consts_np()
    shared = {
        "wqT": w["wqT"], "wgT": w["wgT"], "kTw": w["kTw"], "vW": w["vW"],
        "woutT": w["woutT"], **cn,
    }
    in_maps = [{"xT": xTs[c], **shared} for c in range(NCORES)]
    res = run_bass_kernel_spmd(nc, in_maps, core_ids=list(range(NCORES)))
    return [res.results[c]["yT"] for c in range(NCORES)]


def kernel(**inputs):
    ins = {k: np.asarray(v) for k, v in inputs.items()}
    x = _np(ins["L0_attn_a_x"])
    xTs = []
    for c in range(NCORES):
        b, off = divmod(c, 4)
        xTs.append(np.ascontiguousarray(x[b, off * T:(off + 1) * T, :].T))
    w = {
        "wqT": _rg(_np(ins["L0_attn_a_Wq"].T), NCH, NH, 128),
        "wgT": _rg(_np(ins["L0_attn_a_Wg"].T), NCH, NH, 128),
        "kTw": _np(ins["L0_attn_a_k_weight"].T),
        "vW": _rg(_np(ins["L0_attn_a_v_weight"]), 32, NH, 128),
        "woutT": _rg(_np(ins["L0_attn_a_Wout"].T), NCH, 4, 512),
    }
    xTs = run_l0a(xTs, w)
    f0 = prep_ffn_l0(ins)
    xTs = run_ffn("l0", xTs, f0)
    xTs = run_l0b(xTs, ins)
    xTs = run_ffn("l0", xTs, f0)
    xTs = run_ssm(xTs, ins)
    f1 = prep_ffn_l1(ins)
    xTs = run_ffn("l1", xTs, f1)
    xTs = run_l1b(xTs, ins)
    xTs = run_ffn("l1", xTs, f1)
    out = np.empty((2, S, D), np.float32)
    for c in range(NCORES):
        b, off = divmod(c, 4)
        out[b, off * T:(off + 1) * T, :] = xTs[c].T
    return out
